# version 21
# baseline (speedup 1.0000x reference)
import math
import numpy as np

import concourse.bass as bass
import concourse.mybir as mybir
from concourse.bass_utils import run_bass_kernel_spmd
from concourse.bass_core_rust import IndirectOffsetOnAxis

F32 = mybir.dt.float32
I32 = mybir.dt.int32
BF16 = mybir.dt.bfloat16
AF = mybir.ActivationFunctionType
ALU = mybir.AluOpType
AX = mybir.AxisListType
SWAPMASK = [i ^ 1 for i in range(32)]


class CFG:
    def __init__(self, D=2048, S=4096, H=32, HV=8, JH=5632, IH=1408, E=8, CAP=384):
        self.D, self.S, self.H, self.HV = D, S, H, HV
        self.JH, self.IH, self.E, self.CAP = JH, IH, E, CAP
        self.T = S // 4
        self.NM = self.T // 128
        self.NC = D // 128
        self.NKT = S // 128
        self.NTC = S // 512
        self.NG = self.NM // 4
        self.NF = HV * 64 // 128
        self.NCAP = CAP // 128
        self.NJC = JH // 128
        self.NIC = IH // 128
        self.HGRP = 4
        self.NDB = D // 512
        assert H == 4 * HV and H * 64 == D
        assert D % 512 == 0 and S % 512 == 0 and self.T % 512 == 0
        assert self.NJC % self.HGRP == 0 and IH % 128 == 0
        assert CAP % 128 == 0 and self.NM % 4 == 0 and HV * 64 % 128 == 0


class Chan:
    __slots__ = ("sem", "step", "n")

    def __init__(self, sem, step):
        self.sem, self.step, self.n = sem, step, 0


class Prog:
    ENGS = ("sync", "tensor", "vector", "scalar", "gpsimd")

    def __init__(self, nc):
        self.nc = nc
        self.q = {e: [] for e in self.ENGS}
        self.chans = {}
        self.seen = {e: {} for e in self.ENGS}
        self.nojoin = set()

    def chan(self, name, step=1):
        if name not in self.chans:
            self.chans[name] = Chan(self.nc.alloc_semaphore(name), step)
        return self.chans[name]

    def wait(self, eng, chan, val=None):
        v = chan.n if val is None else val
        if v <= 0:
            return
        seen = self.seen[eng]
        if seen.get(chan.sem.num, 0) >= v:
            return
        seen[chan.sem.num] = v
        sem = chan.sem
        self.q[eng].append(lambda E: E.wait_ge(sem, v))

    def op(self, eng, fn, inc=None, step=None):
        ss = getattr(self, "selfserial", None)
        if ss and eng in ss and inc is ss[eng]:
            self.wait(eng, inc, inc.n)
        if inc is not None:
            add = step if step is not None else inc.step
            inc.n += add
            sem = inc.sem
            self.q[eng].append(lambda E: fn(E).then_inc(sem, add))
            return inc.n
        self.q[eng].append(fn)
        return None

    def join(self):
        snap = [(ch.sem, ch.n) for ch in self.chans.values()
                if ch.n > 0 and ch not in self.nojoin]
        for e in self.ENGS:
            seen = self.seen[e]
            todo = [(s, v) for s, v in snap if seen.get(s.num, 0) < v]
            for s, v in todo:
                seen[s.num] = v
            if todo:
                def f(E, todo=todo):
                    for s, v in todo:
                        E.wait_ge(s, v)
                self.q[e].append(f)

    def finalize(self):
        with self.nc.Block() as block:
            for ename, dec in (("sync", block.sync), ("tensor", block.tensor),
                               ("vector", block.vector), ("scalar", block.scalar),
                               ("gpsimd", block.gpsimd)):
                fns = self.q[ename]
                def body(E, fns=fns):
                    for f in fns:
                        f(E)
                dec(body)


def build_program(c: CFG, upto="E", dbg_pairs=False):
    nc = bass.Bass("TRN2", target_bir_lowering=False, debug=False)
    din = lambda n, sh, d=F32: nc.dram_tensor(n, sh, d, kind="ExternalInput").ap()
    dint = lambda n, sh, d=F32: nc.dram_tensor(n, sh, d, kind="Internal").ap()

    NHG = c.NJC // c.HGRP
    xbR = din("xbR", [c.NTC, 128, c.NC * 512])
    xoR = din("xoR", [c.T // 512, 128, c.NC * 512])
    wqR = din("wqR", [c.NC, 128, c.NC * 128])
    wkR = din("wkR", [128, c.NC * c.NF * 128])
    wvR = din("wvR", [128, c.NC * c.NF * 128])
    woR = din("woR", [c.NC, 128, c.NC * 128])
    routerR = din("routerR", [128, c.NC * c.E])
    sugR = din("sugR", [2 * c.NJC, 128, c.NC * 128])
    swdR = din("swdR", [NHG, c.NDB, 128, c.HGRP * 512])
    eugR = din("eugR", [c.E, 2 * c.NIC, 128, c.NC * 128])
    ewdR = din("ewdR", [c.E, c.NDB, 128, c.NIC * 512])
    kcos = din("kcos", [64, c.S])
    ksin = din("ksin", [64, c.S])
    qcost = din("qcost", [64, c.T])
    qsint = din("qsint", [64, c.T])
    masks16 = din("masks16", [16, 128, 512], BF16)
    identin = din("identin", [128, 128])
    triuin = din("triuin", [128, 128])
    iota8in = din("iota8in", [128, c.E])
    itokin = din("itokin", [128, c.NM])
    conesin = din("conesin", [128, 1])
    onesrin = din("onesrin", [128, 128])
    ind64in = din("ind64in", [128, 2])
    identbfin = din("identbfin", [128, 128], BF16)

    out = nc.dram_tensor("out", [c.T + 1, c.D], F32, kind="ExternalOutput").ap()

    kspill = dint("kspill", [c.HV * 64, c.S])
    vspill = dint("vspill", [c.HV * 64, c.S])
    h2tok = dint("h2tok", [c.T + 1, c.D])
    NRANK = c.E * c.CAP + 128
    ranktok = dint("ranktok", [NRANK, 1])
    rankcw = dint("rankcw", [NRANK, 1])
    dpr = ((lambda n, sh, d: nc.dram_tensor(n, sh, d,
                                            kind="ExternalOutput").ap())
           if dbg_pairs else dint)
    sug_h = dpr("sug_h", [2 * c.NJC, 128, c.NC * 128], BF16)
    sug_l = dpr("sug_l", [2 * c.NJC, 128, c.NC * 128], BF16)

    P = Prog(nc)
    mm = P.chan("mm")
    act = P.chan("act")
    dve = P.chan("dve")
    gpc = P.chan("gpc")
    ldc = P.chan("ldc", 16)
    ldw = P.chan("ldw", 16)
    ldx = [P.chan("ldx0", 16), P.chan("ldx1", 16)]
    rp = [P.chan("rp0", 16), P.chan("rp1", 16)]
    rpq = P.chan("rpq", 16)
    wst = [P.chan("wst0", 16), P.chan("wst1", 16)]
    wsd = P.chan("wsd", 16)
    lrt = P.chan("lrt", 16)
    kvl = P.chan("kvl", 16)
    kvv = P.chan("kvv", 16)
    lxr = [P.chan("lxr0", 16), P.chan("lxr1", 16)]
    spl = P.chan("spl", 16)
    sst = P.chan("sst", 16)
    ler = P.chan("ler", 16)
    lcw = P.chan("lcw", 16)
    swr = P.chan("swr", 16)
    swa = [P.chan("swa0", 16), P.chan("swa1", 16)]
    swe = P.chan("swe", 16)
    ewl = [P.chan("ewl0", 16), P.chan("ewl1", 16)]
    etk = [P.chan("etk0", 16), P.chan("etk1", 16)]
    tstc = [P.chan("tst0", 16), P.chan("tst1", 16)]
    plf = P.chan("plf", 16)
    pst = P.chan("pst", 16)
    psp = P.chan("psp")
    P.nojoin = {plf, pst, psp}
    P.selfserial = {"vector": dve, "scalar": act, "gpsimd": gpc}

    AR_COLS = 43648
    AR = nc.alloc_sbuf_tensor("AR", [128, AR_COLS], F32)
    R_A, R_B, R_W = 0, 16384, 32768
    RWE = AR_COLS - R_W

    def ar(off, cols, rows=128):
        assert off + cols <= AR_COLS, (off, cols)
        return AR[0:rows, off:off + cols]

    ARB = AR[:, :].bitcast(BF16)

    def arb(off_eq, cols_bf, rows=128):
        assert 2 * off_eq + cols_bf <= 2 * AR_COLS, (off_eq, cols_bf)
        return ARB[0:rows, 2 * off_eq:2 * off_eq + cols_bf]

    Al = lambda n, sh, d=F32: nc.alloc_sbuf_tensor(n, sh, d)
    sident = Al("sident", [128, 128])
    sidbf = Al("sidbf", [128, 128], BF16)
    striu = Al("striu", [128, 128])
    siota8 = Al("siota8", [128, c.E])
    sitok = Al("sitok", [128, c.NM])
    scones = Al("scones", [128, 1])
    onesr = Al("onesr", [128, 128])
    sind64 = Al("sind64", [128, 2])

    smask = Al("smask", [128, 16 * 512], BF16)
    kmrun = Al("kmrun", [1, c.HV])
    qmrun = Al("qmrun", [1, c.H])
    negrow = Al("negrow", [1, c.H])
    negBb = Al("negBb", [128, c.H])
    srowA = Al("srowA", [1, 512])
    srowB = Al("srowB", [1, 512])
    srowC = Al("srowC", [1, 512])
    s11 = Al("s11", [1, 1])
    sBC1 = Al("sBC1", [128, 512])
    sBC2 = Al("sBC2", [128, 512])
    sq = Al("sq", [128, 512])
    t1s = Al("t1s", [128, 512])
    slog = Al("slog", [128, c.E])
    sr_is1 = Al("sr_is1", [128, c.E])
    sr_is2 = Al("sr_is2", [128, c.E])
    sr_msk = Al("sr_msk", [128, c.E])
    sr_pos = Al("sr_pos", [128, c.E])
    sr_obc = Al("sr_obc", [128, c.E])
    sr_m1 = Al("sr_m1", [128, 1])
    sr_m2 = Al("sr_m2", [128, 1])
    sr_d = Al("sr_d", [128, 1])
    sr_w1 = Al("sr_w1", [128, 1])
    sr_w2 = Al("sr_w2", [128, 1])
    sr_i1 = Al("sr_i1", [128, 1])
    sr_i2 = Al("sr_i2", [128, 1])
    sr_p1 = Al("sr_p1", [128, 1])
    sr_p2 = Al("sr_p2", [128, 1])
    sr_t = Al("sr_t", [128, c.E])
    sr_s = Al("sr_s", [128, 1])
    srow8 = Al("srow8", [1, (c.NM + 1) * c.E])
    sd1i = Al("sd1i", [128, 2], I32)
    eidx = Al("eidx", [128, 2 * c.NCAP], I32)

    ZT = (NRANK + 127) // 128
    zrf = Al("zrf", [128, ZT])
    ztrf = Al("ztrf", [128, ZT])
    plsc = Al("plsc", [128, 512])
    plscb = plsc[:, :].bitcast(BF16)
    pl_f32 = plsc[:, 0:256]
    pl_hi = plscb[:, 512:768]
    pl_lo = plscb[:, 768:1024]
    erow = srowA[0:1, 0:128]
    cwrow = srowB[0:1, 0:c.CAP] if c.CAP <= 512 else None

    PB = [nc.alloc_psum_tensor("pb%d" % i, [128, 512], F32) for i in range(8)]

    def dma(eng, chan, o, i, **kw):
        return P.op(eng, lambda E: E.dma_start(out=o, in_=i, **kw), inc=chan)

    def ld_cm(eng, chan, dst2d, src, fsel=None):
        s3 = src.rearrange("(n p) f -> p n f", p=128)
        if fsel is not None:
            s3 = s3[:, :, fsel[0]:fsel[1]]
        f = s3.shape[2]
        d3 = dst2d.rearrange("p (n f) -> p n f", f=f)
        return dma(eng, chan, d3, s3)

    def matmul(o, l, r, start, stop, inc=None):
        return P.op("tensor",
                    lambda E: E.matmul(o, l, r, start=start, stop=stop), inc=inc)

    def vop(name, *aps, inc=dve):
        return P.op("vector", lambda E: getattr(E, name)(*aps), inc=inc)

    for o, i in ((sident[:, :], identin), (striu[:, :], triuin),
                 (siota8[:, :], iota8in), (sitok[:, :], itokin),
                 (scones[:, :], conesin), (sind64[:, :], ind64in),
                 (onesr[:, :], onesrin), (sidbf[:, :], identbfin)):
        dma("sync", ldc, o, i[:, :])
    dma("sync", ldc, smask[:, :].rearrange("p (r q) -> p r q", q=512),
        masks16.rearrange("r p q -> p r q"))
    ld_consts = ldc.n

    P.op("gpsimd", lambda E: E.memset(zrf[:, :], 0.0), inc=gpc)
    P.op("gpsimd", lambda E: E.memset(ztrf[:, :], float(c.T)), inc=gpc)
    P.op("gpsimd", lambda E: E.memset(kmrun[:, :], 0.0), inc=gpc)
    P.op("gpsimd", lambda E: E.memset(qmrun[:, :], 0.0), inc=gpc)
    init_zeros = gpc.n
    P.wait("gpsimd", gpc, init_zeros)
    dma("gpsimd", swr, ranktok.rearrange("(p z) one -> p (z one)", p=128),
        ztrf[:, :])
    dma("gpsimd", swr, rankcw.rearrange("(p z) one -> p (z one)", p=128),
        zrf[:, :])
    ranks_zeroed = swr.n
    P.op("gpsimd",
         lambda E: E.memset(bass.AP(AR, R_W + 64,
                                    [[AR_COLS, 128], [65, c.NKT]]), 1.0),
         inc=gpc)
    v65_ones = gpc.n

    PSLAB = 256
    assert (c.NC * 128) % PSLAB == 0
    nsub = (c.NC * 128) // PSLAB
    pr_state = [0, 0]

    def emit_pr_slab(src2, dsth2, dstl2):
        for sb in range(nsub):
            cs = slice(sb * PSLAB, (sb + 1) * PSLAB)
            pld = dma("gpsimd", plf, pl_f32, src2[:, cs])
            P.wait("gpsimd", plf, pld)
            P.wait("gpsimd", pst, pr_state[1])
            P.op("gpsimd", lambda E: E.tensor_copy(pl_hi, pl_f32), inc=psp)
            P.wait("gpsimd", psp)
            P.op("gpsimd", lambda E: E.tensor_sub(pl_lo, pl_f32, pl_hi),
                 inc=psp)
            pr_state[0] = psp.n
            P.wait("gpsimd", psp, pr_state[0])
            dma("gpsimd", pst, dsth2[:, cs], pl_hi)
            pr_state[1] = dma("gpsimd", pst, dstl2[:, cs], pl_lo)

    for blk in range(2 * c.NJC):
        emit_pr_slab(sugR[blk, :, :], sug_h[blk, :, :], sug_l[blk, :, :])
    pr_sug = pst.n

    O_V65 = R_W
    O_S1 = R_W + c.NKT * 65
    NKV = c.NF * 512

    w_k = ar(R_B, c.NC * c.HV * 64)
    w_v = ar(R_B + c.NC * c.HV * 64, c.NC * c.HV * 64)
    dma("sync", ldw, w_k, wkR[:, :])
    dma("sync", ldw, w_v, wvR[:, :])
    ld_wkv = ldw.n

    O_KST = O_S1
    O_VST = O_KST + NKV
    O_RC = O_VST + NKV
    O_RS = O_RC + 512
    O_RSH = O_RS + 512
    assert O_RSH + 512 <= AR_COLS
    kst = ar(O_KST, NKV)
    vst = ar(O_VST, NKV)
    rcs = ar(O_RC, 512)
    rsn = ar(O_RS, 512)

    def emit_ln(tile_of_n, pair=None):
        P.wait("tensor", act)
        P.wait("tensor", dve)
        for n in range(c.NC):
            matmul(PB[0][0:1, 0:512], scones[:, :], tile_of_n(n),
                   start=(n == 0), stop=(n == c.NC - 1),
                   inc=mm if n == c.NC - 1 else None)
        sq_mm = [0, 0]
        for n in range(c.NC):
            buf = (sq, t1s)[n % 2]
            P.wait("vector", mm, sq_mm[n % 2])
            vop("tensor_mul", buf[:, :], tile_of_n(n), tile_of_n(n))
            P.wait("tensor", dve)
            sq_mm[n % 2] = matmul(PB[1][0:1, 0:512], scones[:, :], buf[:, :],
                                  start=(n == 0), stop=(n == c.NC - 1), inc=mm)
        P.wait("vector", mm)
        vop("tensor_single_scalar", srowA[:, :], PB[0][0:1, :], 1.0 / c.D,
            ALU.mult)
        vop("tensor_single_scalar", srowB[:, :], PB[1][0:1, :], 1.0 / c.D,
            ALU.mult)
        vop("tensor_mul", srowC[:, :], srowA[:, :], srowA[:, :])
        vop("tensor_sub", srowB[:, :], srowB[:, :], srowC[:, :])
        vop("tensor_scalar_add", srowB[:, :], srowB[:, :], 1e-5)
        P.wait("scalar", dve)
        P.op("scalar", lambda E: E.activation(srowB[:, :], srowB[:, :],
                                              AF.Sqrt, bias=0.0, scale=1.0),
             inc=act)
        P.wait("vector", act)
        vop("reciprocal", srowB[:, :], srowB[:, :])
        vop("tensor_mul", srowC[:, :], srowA[:, :], srowB[:, :])
        vop("tensor_scalar_mul", srowC[:, :], srowC[:, :], -1.0)
        P.wait("tensor", dve)
        matmul(PB[6][:, :], onesr[0:1, :], srowB[:, :], True, True, inc=mm)
        matmul(PB[7][:, :], onesr[0:1, :], srowC[:, :], True, True, inc=mm)
        P.wait("vector", mm)
        for n in range(c.NC):
            vop("tensor_mul", t1s[:, :], tile_of_n(n), PB[6][:, :])
            if pair is None:
                vop("tensor_add", tile_of_n(n), t1s[:, :], PB[7][:, :])
            else:
                hi_n, lo_n = pair(n)
                vop("tensor_add", sq[:, :], t1s[:, :], PB[7][:, :])
                vop("tensor_copy", hi_n, sq[:, :])
                vop("tensor_sub", lo_n, sq[:, :], hi_n)

    def emit_rope(buf, col0, tcs=None, tsn=None, tcol=0):
        rsh = ar(O_RSH, 512)
        tcs = rcs if tcs is None else tcs
        tsn = rsn if tsn is None else tsn
        for half in range(2):
            hh = slice(half * 64, half * 64 + 64)
            src = buf[hh, col0:col0 + 512]
            vop("stream_shuffle", rsh[hh, :], src, SWAPMASK)
            vop("tensor_mul", rsh[hh, :], rsh[hh, :], tsn[hh, tcol:tcol + 512])
            vop("tensor_mul", src, src, tcs[hh, tcol:tcol + 512])
            vop("tensor_add", src, src, rsh[hh, :])

    norm_st = {"dve": 0, "mm": 0}

    def emit_norm_accum(buf, col0, run, col2):
        P.wait("vector", mm, norm_st["mm"])
        vop("tensor_mul", sq[:, :], buf[:, col0:col0 + 512],
            buf[:, col0:col0 + 512])
        P.wait("tensor", dve)
        P.wait("tensor", dve, norm_st["dve"])
        for half in range(2):
            matmul(PB[4 + half][0:1, :], sind64[:, half:half + 1], sq[:, :],
                   True, True, inc=mm)
        norm_st["mm"] = mm.n
        P.wait("vector", mm, norm_st["mm"])
        for half in range(2):
            vop("reduce_max", s11[:, :], PB[4 + half][0:1, :], AX.X)
            cc = col2 + half
            vop("tensor_max", run[0:1, cc:cc + 1], run[0:1, cc:cc + 1],
                s11[:, :])
        norm_st["dve"] = dve.n

    CNC = c.NC * 512
    O_KT2 = O_RSH + 512
    krcs = [rcs, ar(O_KT2, 512)]
    krsn = [rsn, ar(O_KT2 + 512, 512)]
    xfree = [(0, 0), (0, 0)]
    tbl_rd = [0, 0]
    prev_spill = 0
    ln_h = [0] * c.NTC
    ln_rld = [0] * c.NTC

    def emit_chunk_load_ln(tc):
        par = tc % 2
        xbuf = ar(R_A + par * CNC, CNC)
        P.wait("sync", dve, xfree[par][0])
        P.wait("sync", mm, xfree[par][1])
        xld = dma("sync", ldx[par], xbuf, xbR[tc, :, :])
        P.wait("sync", dve, tbl_rd[par])
        rld = dma("sync", rp[par], krcs[par][0:64, :],
                  kcos[:, tc * 512:(tc + 1) * 512])
        rld = dma("sync", rp[par], krcs[par][64:128, :],
                  kcos[:, tc * 512:(tc + 1) * 512])
        rld = dma("sync", rp[par], krsn[par][0:64, :],
                  ksin[:, tc * 512:(tc + 1) * 512])
        rld = dma("sync", rp[par], krsn[par][64:128, :],
                  ksin[:, tc * 512:(tc + 1) * 512])
        ln_rld[tc] = rld
        if tc == 0:
            P.wait("tensor", ldc, ld_consts)
            P.wait("vector", gpc, init_zeros)
        P.wait("tensor", ldx[par], xld)
        P.wait("vector", ldx[par], xld)
        emit_ln(lambda n, xbuf=xbuf: xbuf[:, n * 512:(n + 1) * 512])
        ln_h[tc] = dve.n

    emit_chunk_load_ln(0)
    for tc in range(c.NTC):
        par = tc % 2
        xbuf = ar(R_A + par * CNC, CNC)
        h_done = ln_h[tc]
        rld = ln_rld[tc]
        P.wait("tensor", ldw, ld_wkv)
        P.wait("tensor", dve, h_done)
        kvgroups = ([(w_k, kst, fo) for fo in range(c.NF)]
                    + [(w_v, vst, fo) for fo in range(c.NF)])
        for gi, (wres, stg, fo) in enumerate(kvgroups):
            bank = PB[2 + (gi % 2)]
            P.wait("tensor", dve, h_done)
            for n in range(c.NC):
                lt = wres[:, (n * c.NF + fo) * 128:(n * c.NF + fo) * 128 + 128]
                matmul(bank[:, :], lt, xbuf[:, n * 512:(n + 1) * 512],
                       start=(n == 0), stop=(n == c.NC - 1),
                       inc=mm if n == c.NC - 1 else None)
            P.wait("vector", mm)
            P.wait("vector", spl, prev_spill)
            vop("tensor_copy", stg[:, fo * 512:(fo + 1) * 512], bank[:, :])
            P.wait("tensor", dve)
            if 1 <= gi <= c.NF:
                kfo = gi - 1
                if gi == 1:
                    P.wait("vector", rp[par], rld)
                emit_rope(kst, kfo * 512, krcs[par], krsn[par], 0)
                emit_norm_accum(kst, kfo * 512, kmrun, 2 * kfo)
            if gi == 1 and tc + 1 < c.NTC:
                emit_chunk_load_ln(tc + 1)
        kv_mm = mm.n
        P.wait("scalar", dve)
        dma("scalar", spl,
            kspill.rearrange("(f p) s -> p f s", p=128)
            [:, :, tc * 512:(tc + 1) * 512],
            kst.rearrange("p (f s) -> p f s", s=512))
        prev_spill = dma(
            "scalar", spl,
            vspill.rearrange("(f p) s -> p f s", p=128)
            [:, :, tc * 512:(tc + 1) * 512],
            vst.rearrange("p (f s) -> p f s", s=512))
        xfree[par] = (h_done, kv_mm)
        tbl_rd[par] = dve.n
    a_spills = spl.n
    P.join()

    Qr = ar(R_A, c.NC * c.T)
    for t2 in range(c.T // 512):
        xbuf = ar(R_B + t2 * CNC, CNC)
        xld = dma("sync", ldx[t2 % 2], xbuf, xoR[t2, :, :])
        P.wait("tensor", ldx[t2 % 2], xld)
        P.wait("vector", ldx[t2 % 2], xld)
        emit_ln(lambda n, t2=t2: ar(R_B + t2 * CNC, CNC)
                [:, n * 512:(n + 1) * 512])
    h_own = dve.n

    O_WQ = O_S1
    WQC = c.NC * 128
    O_QRC = O_RSH + 512 + 1024
    qrc = ar(O_QRC, c.T)
    qrs = ar(O_QRC + c.T, c.T)
    rq1 = dma("sync", rpq, qrc[0:64, :], qcost[:, :])
    rq1 = dma("sync", rpq, qrc[64:128, :], qcost[:, :])
    rq1 = dma("sync", rpq, qrs[0:64, :], qsint[:, :])
    rq1 = dma("sync", rpq, qrs[64:128, :], qsint[:, :])

    def emit_qtail(fo):
        for t2 in range(c.T // 512):
            col0 = fo * c.T + t2 * 512
            emit_rope(Qr, col0, qrc, qrs, t2 * 512)
            emit_norm_accum(Qr, col0, qmrun, 2 * fo)

    wqfree = [0, 0]
    for fo in range(c.NC):
        par = fo % 2
        wstg = ar(O_WQ + par * WQC, WQC)
        P.wait("sync", mm, wqfree[par])
        wld = dma("sync", wst[par], wstg, wqR[fo, :, :])
        P.wait("tensor", wst[par], wld)
        P.wait("tensor", dve, h_own)
        for t2 in range(c.T // 512):
            xbuf = ar(R_B + t2 * CNC, CNC)
            bank = PB[2 + (t2 % 2)]
            for n in range(c.NC):
                matmul(bank[:, :], wstg[:, n * 128:(n + 1) * 128],
                       xbuf[:, n * 512:(n + 1) * 512],
                       start=(n == 0), stop=(n == c.NC - 1),
                       inc=mm if n == c.NC - 1 else None)
            P.wait("vector", mm)
            if fo == 0 and t2 == 0:
                P.wait("vector", rpq, rq1)
            vop("tensor_copy",
                Qr[:, fo * c.T + t2 * 512:fo * c.T + t2 * 512 + 512],
                bank[:, :])
            P.wait("tensor", dve)
        wqfree[par] = mm.n
        if fo > 0:
            emit_qtail(fo - 1)
    emit_qtail(c.NC - 1)
    P.wait("scalar", dve)
    P.op("scalar", lambda E: E.activation(qmrun[:, :], qmrun[:, :], AF.Sqrt,
                                          bias=0.0, scale=1.0), inc=act)
    P.op("scalar", lambda E: E.activation(kmrun[:, :], kmrun[:, :], AF.Sqrt,
                                          bias=0.0, scale=1.0), inc=act)
    P.wait("vector", act)
    for k in range(4):
        vop("tensor_copy", bass.AP(negrow, k, [[c.H, 1], [4, c.HV]]),
            kmrun[0:1, 0:c.HV])
    vop("tensor_mul", negrow[:, :], negrow[:, :], qmrun[:, :])
    vop("tensor_scalar_mul", negrow[:, :], negrow[:, :], -0.125)
    P.wait("tensor", dve)
    matmul(PB[5][0:128, 0:c.H], onesr[0:1, :], negrow[:, :], True, True,
           inc=mm)
    P.wait("vector", mm)
    vop("tensor_copy", negBb[:, :], PB[5][0:128, 0:c.H])
    P.join()

    if upto == "A2":
        dbg = nc.dram_tensor("dbg", [128, c.NC * c.T], F32,
                             kind="ExternalOutput").ap()
        dma("sync", sst, dbg[:, :], Qr)
        P.wait("sync", sst)
        P.finalize()
        return nc

    oT = ar(R_B, c.NC * c.T)
    v65c = ar(O_V65, c.NKT * 65)
    O_KB = O_S1
    O_VB = O_KB + c.S
    O_PT = O_VB + c.S
    assert O_PT + 512 <= AR_COLS
    kb = ar(O_KB, c.S)
    vb = ar(O_VB, c.S, rows=64)
    pt = [ar(O_PT, 512), sBC2[:, :]]
    lb = sBC1[0:64, 0:512]
    ostp = [P.chan("ostp0", 16), P.chan("ostp1", 16)]
    ost_n = [0]
    kvfree = (0, 0)
    for hv in range(c.HV):
        P.wait("sync", mm, kvfree[0])
        P.wait("sync", dve, kvfree[1])
        if hv == 0:
            P.wait("sync", spl, a_spills)
            P.wait("scalar", spl, a_spills)
        l0 = dma("sync", kvl, kb[0:64, :], kspill[hv * 64:hv * 64 + 64, :])
        l0 = dma("sync", kvl, kb[64:128, :], kspill[hv * 64:hv * 64 + 64, :])
        P.wait("scalar", mm, kvfree[0])
        l1 = dma("scalar", kvv, vb[0:64, :], vspill[hv * 64:hv * 64 + 64, :])
        if hv == 0:
            P.wait("tensor", gpc, v65_ones)
        P.wait("tensor", kvv, l1)
        for kt in range(c.NKT):
            bank = PB[3 + (kt % 2)]
            P.op("tensor", lambda E, kt=kt, bank=bank: E.transpose(
                bank[:, 0:64], vb[0:64, kt * 128:(kt + 1) * 128],
                sident[0:64, 0:64]), inc=mm)
            P.wait("vector", mm)
            vop("tensor_copy", v65c[:, kt * 65:kt * 65 + 64], bank[:, 0:64])
            P.wait("tensor", dve)
        P.wait("tensor", kvl, l0)
        SBa = (PB[0], PB[1])
        SBb = (PB[5], PB[6])
        PTa = (pt[0], pt[1])
        PTb = (ar(O_VB, 512), ar(O_VB + 512, 512))
        DPTH = 2
        for hp in range(2):
            a = hv * 4 + 2 * hp
            fo = a // 2
            qc0 = fo * c.T
            for g in range(c.NG):
                nkt = 16 * g + 16
                qs = qc0 + g * 512
                exp_t = [[0, 0], [0, 0]]
                msk_t = [[0, 0], [0, 0]]
                use_t = [[0, 0], [0, 0]]

                def emit_S(kt):
                    sp = kt % 2
                    P.wait("tensor", act, exp_t[0][sp])
                    P.wait("tensor", act, exp_t[1][sp])
                    ta = matmul(SBa[sp][:, :],
                                kb[0:64, kt * 128:(kt + 1) * 128],
                                Qr[0:64, qs:qs + 512], True, True, inc=mm)
                    tb = matmul(SBb[sp][:, :],
                                kb[64:128, kt * 128:(kt + 1) * 128],
                                Qr[64:128, qs:qs + 512], True, True, inc=mm)
                    return tb

                def emit_exp(kt, s_t):
                    sp = kt % 2
                    r = kt - 16 * g
                    P.wait("scalar", mm, s_t)
                    for hx, (SB, PT) in enumerate(((SBa, PTa), (SBb, PTb))):
                        P.wait("scalar", mm, use_t[hx][sp])
                        P.wait("scalar", dve, msk_t[hx][sp])
                        exp_t[hx][sp] = P.op(
                            "scalar",
                            lambda E, hx=hx, sp=sp, a=a: E.activation(
                                (PTa, PTb)[hx][sp], (SBa, SBb)[hx][sp][:, :],
                                AF.Exp, bias=negBb[:, a + hx:a + hx + 1],
                                scale=0.125), inc=act)
                        if r >= 0:
                            P.wait("vector", act, exp_t[hx][sp])
                            msk_t[hx][sp] = vop(
                                "tensor_mul", (PTa, PTb)[hx][sp],
                                (PTa, PTb)[hx][sp],
                                smask[:, r * 512:(r + 1) * 512])

                def emit_AV(kt):
                    sp = kt % 2
                    r = kt - 16 * g
                    for hx in range(2):
                        if r >= 0:
                            P.wait("tensor", dve, msk_t[hx][sp])
                        else:
                            P.wait("tensor", act, exp_t[hx][sp])
                        use_t[hx][sp] = matmul(
                            (PB[2], PB[7])[hx][0:65, :],
                            v65c[:, kt * 65:kt * 65 + 65],
                            (PTa, PTb)[hx][sp],
                            start=(kt == 0), stop=(kt == nkt - 1), inc=mm)

                for kt in range(nkt + DPTH):
                    if kt >= DPTH:
                        emit_AV(kt - DPTH)
                    if kt < nkt:
                        s_t = emit_S(kt)
                        emit_exp(kt, s_t)
                P.wait("vector", mm, use_t[1][(nkt - 1) % 2])
                for hx, AVB in enumerate((PB[2], PB[7])):
                    vop("reciprocal", sq[64:65, :], AVB[64:65, :])
                    P.wait("tensor", dve)
                    matmul(PB[3][0:64, :], onesr[64:65, 0:64], sq[64:65, :],
                           True, True, inc=mm)
                    P.wait("vector", mm)
                    vop("tensor_copy", lb, PB[3][0:64, :])
                    ocol = qs
                    if hx == 0:
                        vop("tensor_mul", oT[0:64, ocol:ocol + 512],
                            AVB[0:64, :], lb)
                    else:
                        par_o = ost_n[0] % 2
                        stg_o = (t1s[0:64, :], sq[0:64, :])[par_o]
                        P.wait("vector", ostp[par_o])
                        vop("tensor_mul", stg_o, AVB[0:64, :], lb)
                        P.wait("scalar", dve)
                        dma("scalar", ostp[par_o],
                            oT[64:128, ocol:ocol + 512], stg_o)
                        ost_n[0] += 1
                    P.wait("tensor", dve)
        kvfree = (mm.n, dve.n)
    P.join()

    if upto == "B":
        dbg = nc.dram_tensor("dbg", [128, c.NC * c.T], F32,
                             kind="ExternalOutput").ap()
        dma("sync", sst, dbg[:, :], oT)
        P.wait("sync", sst)
        P.finalize()
        return nc

    x2 = ar(R_A, c.NC * c.T)
    O_WO = O_S1
    O_XR = O_WO + 2 * WQC
    O_TS = O_XR + 1024
    assert O_TS + WQC <= AR_COLS
    wofree = [0, 0]
    xrfree = [0, 0]
    for fo in range(c.NC):
        par = fo % 2
        wstg = ar(O_WO + par * WQC, WQC)
        P.wait("sync", mm, wofree[par])
        wld = dma("sync", wst[par], wstg, woR[fo, :, :])
        for t2 in range(c.T // 512):
            xpar = t2 % 2
            xr = ar(O_XR + xpar * 512, 512)
            P.wait("scalar", dve, xrfree[xpar])
            xrld = dma("scalar", lxr[xpar], xr,
                       xoR[t2, :, fo * 512:(fo + 1) * 512])
            bank = PB[0 + (t2 % 2)]
            P.wait("tensor", wst[par], wld)
            for n in range(c.NC):
                matmul(bank[:, :], wstg[:, n * 128:(n + 1) * 128],
                       oT[:, n * c.T + t2 * 512:n * c.T + t2 * 512 + 512],
                       start=(n == 0), stop=(n == c.NC - 1),
                       inc=mm if n == c.NC - 1 else None)
            P.wait("vector", mm)
            P.wait("vector", lxr[xpar], xrld)
            xrfree[xpar] = vop(
                "tensor_add",
                x2[:, fo * c.T + t2 * 512:fo * c.T + t2 * 512 + 512],
                bank[:, :], xr)
            P.wait("tensor", dve)
        wofree[par] = mm.n
    P.join()

    if upto == "C1":
        dbg = nc.dram_tensor("dbg", [128, c.NC * c.T], F32,
                             kind="ExternalOutput").ap()
        dma("sync", sst, dbg[:, :], x2)
        P.wait("sync", sst)
        P.finalize()
        return nc

    h2hi = arb(R_B, c.NC * c.T)
    h2lo = arb(R_B + 8192, c.NC * c.T)
    for t2 in range(c.T // 512):
        emit_ln(lambda n, t2=t2:
                x2[:, n * c.T + t2 * 512:n * c.T + t2 * 512 + 512],
                pair=lambda n, t2=t2:
                (h2hi[:, n * c.T + t2 * 512:n * c.T + t2 * 512 + 512],
                 h2lo[:, n * c.T + t2 * 512:n * c.T + t2 * 512 + 512]))

    if upto == "C2":
        dbg = nc.dram_tensor("dbg", [128, 2 * c.NC * c.T], BF16,
                             kind="ExternalOutput").ap()
        P.join()
        dma("sync", sst, dbg[:, 0:c.NC * c.T], h2hi)
        dma("sync", sst, dbg[:, c.NC * c.T:], h2lo)
        P.wait("sync", sst)
        P.finalize()
        return nc

    tsb = [ar(O_XR, WQC), ar(O_XR + WQC, WQC)]
    tw = [0, 0]
    ntt = 0
    for which in range(2):
        dstap = (out, h2tok)[which]
        for tt in range(c.NM):
            tp_ = ntt % 2
            ntt += 1
            tstg = tsb[tp_]
            P.wait("vector", tstc[tp_], tw[tp_])
            for n in range(c.NC):
                bank = PB[4 + (n % 2)]
                cs0 = n * c.T + tt * 128
                if which == 0:
                    P.op("tensor", lambda E, cs0=cs0, bank=bank: E.transpose(
                         bank[:, 0:128], x2[:, cs0:cs0 + 128],
                         sident[:, :]), inc=mm)
                else:
                    matmul(bank[:, 0:128], h2hi[:, cs0:cs0 + 128],
                           sidbf[:, :], True, False)
                    matmul(bank[:, 0:128], h2lo[:, cs0:cs0 + 128],
                           sidbf[:, :], False, True, inc=mm)
                P.wait("vector", mm)
                vop("tensor_copy", tstg[:, n * 128:(n + 1) * 128],
                    bank[:, 0:128])
                P.wait("tensor", dve)
            P.wait("scalar", dve)
            tw[tp_] = dma("scalar", tstc[tp_], dstap[tt * 128:(tt + 1) * 128, :],
                          tstg)
    P.wait("vector", tstc[0])
    P.wait("vector", tstc[1])
    vop("memset", tsb[0][0:1, 0:c.D], 0.0)
    P.wait("scalar", dve)
    dma("scalar", sst, h2tok[c.T:c.T + 1, :], tsb[0][0:1, 0:c.D])
    out_base = sst.n
    tst_all = (tstc[0].n, tstc[1].n)

    srout = ar(O_WO, c.NC * c.E)
    srhi = arb(O_WO + 256, c.NC * c.E)
    srlo = arb(O_WO + 256 + c.NC * c.E // 2, c.NC * c.E)
    rout_ld = dma("sync", lrt, srout, routerR[:, :])
    P.wait("vector", lrt, rout_ld)
    vop("tensor_copy", srhi, srout)
    vop("tensor_sub", srlo, srout, srhi)
    P.wait("tensor", dve)
    for tt in range(c.NM):
        for ps in range(3):
            hh = (h2hi, h2lo, h2hi)[ps]
            rr = (srhi, srhi, srlo)[ps]
            for n in range(c.NC):
                matmul(PB[6][0:128, 0:c.E],
                       hh[:, n * c.T + tt * 128:n * c.T + tt * 128 + 128],
                       rr[:, n * c.E:(n + 1) * c.E],
                       start=(ps == 0 and n == 0),
                       stop=(ps == 2 and n == c.NC - 1),
                       inc=mm if (ps == 2 and n == c.NC - 1) else None)
        P.wait("vector", mm)
        P.wait("vector", swr)
        vop("tensor_copy", slog[:, :], PB[6][0:128, 0:c.E])
        P.wait("tensor", dve)
        vop("reduce_max", sr_m1[:, :], slog[:, :], AX.X)
        vop("tensor_single_scalar", sr_is1[:, :], slog[:, :], sr_m1[:, :],
            ALU.is_ge)
        vop("tensor_scalar", sr_msk[:, :], sr_is1[:, :], -1e30, None, ALU.mult)
        vop("tensor_add", sr_msk[:, :], sr_msk[:, :], slog[:, :])
        vop("reduce_max", sr_m2[:, :], sr_msk[:, :], AX.X)
        vop("tensor_single_scalar", sr_is2[:, :], sr_msk[:, :], sr_m2[:, :],
            ALU.is_ge)
        vop("tensor_sub", sr_d[:, :], sr_m2[:, :], sr_m1[:, :])
        P.wait("scalar", dve)
        P.op("scalar", lambda E: E.activation(sr_d[:, :], sr_d[:, :], AF.Exp,
                                              bias=0.0, scale=1.0), inc=act)
        P.wait("vector", act)
        vop("tensor_scalar_add", sr_s[:, :], sr_d[:, :], 1.0)
        vop("reciprocal", sr_w1[:, :], sr_s[:, :])
        vop("tensor_mul", sr_w2[:, :], sr_d[:, :], sr_w1[:, :])
        vop("tensor_mul", sr_t[:, :], sr_is1[:, :], siota8[:, :])
        vop("reduce_sum", sr_i1[:, :], sr_t[:, :], AX.X)
        vop("tensor_mul", sr_t[:, :], sr_is2[:, :], siota8[:, :])
        vop("reduce_sum", sr_i2[:, :], sr_t[:, :], AX.X)
        vop("tensor_add", sr_t[:, :], sr_is1[:, :], sr_is2[:, :])
        P.wait("tensor", dve)
        P.wait("tensor", act)
        matmul(PB[7][0:128, 0:c.E], striu[:, :], sr_t[:, :], True, True,
               inc=mm)
        matmul(PB[6][0:1, 0:c.E], scones[:, :], sr_t[:, :], True, True,
               inc=mm)
        P.wait("vector", mm)
        vop("tensor_copy", srow8[:, tt * c.E:(tt + 1) * c.E], PB[6][0:1, 0:c.E])
        if tt == 0:
            vop("memset", srow8[:, c.NM * c.E:(c.NM + 1) * c.E], 0.0)
        else:
            vop("tensor_add", srow8[:, c.NM * c.E:(c.NM + 1) * c.E],
                srow8[:, c.NM * c.E:(c.NM + 1) * c.E],
                srow8[:, (tt - 1) * c.E:tt * c.E])
        P.wait("tensor", dve)
        matmul(PB[5][0:128, 0:c.E], onesr[0:1, :],
               srow8[0:1, c.NM * c.E:(c.NM + 1) * c.E], True, True, inc=mm)
        P.wait("vector", mm)
        vop("tensor_copy", sr_obc[:, :], PB[5][0:128, 0:c.E])
        vop("tensor_add", sr_pos[:, :], PB[7][0:128, 0:c.E], sr_obc[:, :])
        P.wait("tensor", dve)
        for isx, ix, px, wx, k in ((sr_is1, sr_i1, sr_p1, sr_w1, 0),
                                   (sr_is2, sr_i2, sr_p2, sr_w2, 1)):
            vop("tensor_mul", sr_t[:, :], isx[:, :], sr_pos[:, :])
            vop("reduce_sum", px[:, :], sr_t[:, :], AX.X)
            vop("tensor_scalar", sr_s[:, :], ix[:, :], float(c.CAP), None,
                ALU.mult)
            vop("tensor_add", sr_s[:, :], sr_s[:, :], px[:, :])
            vop("tensor_scalar_min", sr_s[:, :], sr_s[:, :],
                float(c.E * c.CAP))
            vop("tensor_copy", sd1i[:, k:k + 1], sr_s[:, :])
            P.wait("gpsimd", dve)
            if tt == 0 and k == 0:
                P.wait("gpsimd", swr, ranks_zeroed)
            P.op("gpsimd", lambda E, k=k, tt=tt: E.indirect_dma_start(
                out=ranktok[:, :],
                out_offset=IndirectOffsetOnAxis(sd1i[:, k:k + 1], 0),
                in_=sitok[:, tt:tt + 1], in_offset=None), inc=swr)
            P.op("gpsimd", lambda E, k=k, wx=wx: E.indirect_dma_start(
                out=rankcw[:, :],
                out_offset=IndirectOffsetOnAxis(sd1i[:, k:k + 1], 0),
                in_=wx[:, :], in_offset=None), inc=swr)
    routing_done = swr.n
    P.join()

    if upto == "C":
        dbg = nc.dram_tensor("dbg", [128, 2 * c.NC * c.T], BF16,
                             kind="ExternalOutput").ap()
        dma("sync", sst, dbg[:, 0:c.NC * c.T], h2hi)
        dma("sync", sst, dbg[:, c.NC * c.T:], h2lo)
        P.wait("sync", sst)
        P.finalize()
        return nc

    O_HS = R_W
    O_SUG = R_W + c.HGRP * c.T
    SWDC = c.HGRP * 512
    O_SWDF = R_A
    O_SWDP = R_A + 2 * SWDC
    O_ACCS = R_A + 4 * SWDC
    assert O_ACCS + 2 * c.NM * 512 <= R_A + 16384
    HST = c.HGRP * c.T
    hshi = arb(O_HS, HST)
    hslo = arb(O_HS + HST // 2, HST)
    ldslot = [0, 0]
    nld = 0
    down_done = 0
    swd_split_a = [0, 0]
    swd_split_d = [0, 0]
    swd_rd = [0, 0]
    accS_rd = [0, 0]
    scr_act = [0, 0]
    d_h2 = dve.n
    d_act = [0, 0]
    d_dve = [0, 0]
    for hg in range(NHG):
        for gl in range(c.HGRP):
            gc = hg * c.HGRP + gl
            stg = [None, None]
            lds_t = [0, 0]
            for half, colblk in ((0, gc), (1, c.NJC + gc)):
                par = nld % 2
                nld += 1
                sgh = arb(O_SUG + par * WQC, WQC)
                sgl = arb(O_SUG + par * WQC + WQC // 2, WQC)
                stg[half] = (sgh, sgl)
                P.wait("sync", mm, ldslot[par])
                if nld <= 2:
                    P.wait("sync", pst, pr_sug)
                dma("sync", wst[par], sgh, sug_h[colblk, :, :])
                lds_t[half] = dma("sync", wst[par], sgl, sug_l[colblk, :, :])
            last_gate = 0
            for t2 in range(c.T // 512):
                bG = PB[0 + 2 * (t2 % 2)]
                bU = PB[1 + 2 * (t2 % 2)]
                P.wait("tensor", wst[(nld - 2) % 2], lds_t[0])
                P.wait("tensor", dve, d_h2)
                P.wait("tensor", act, d_act[t2 % 2])
                P.wait("tensor", dve, d_dve[t2 % 2])
                for bank, half in ((bG, 0), (bU, 1)):
                    if half == 1:
                        P.wait("tensor", wst[(nld - 1) % 2], lds_t[1])
                    for ps in range(3):
                        wv_ = stg[half][(0, 1, 0)[ps]]
                        xv_ = (h2hi, h2hi, h2lo)[ps]
                        for n in range(c.NC):
                            matmul(bank[:, :], wv_[:, n * 128:(n + 1) * 128],
                                   xv_[:, n * c.T + t2 * 512:
                                       n * c.T + t2 * 512 + 512],
                                   start=(ps == 0 and n == 0),
                                   stop=(ps == 2 and n == c.NC - 1),
                                   inc=mm if (ps == 2 and n == c.NC - 1)
                                   else None)
                    if half == 0:
                        last_gate = mm.n
                gum = mm.n
                P.wait("scalar", mm, gum)
                P.wait("scalar", dve)
                P.op("scalar", lambda E, bG=bG: E.activation(
                    t1s[:, :], bG[:, :], AF.Silu, bias=0.0, scale=1.0),
                    inc=act)
                d_act[t2 % 2] = act.n
                scr = (sBC1, sBC2)[t2 % 2]
                hcol = gl * c.T + t2 * 512
                P.wait("vector", act, scr_act[t2 % 2])
                P.wait("vector", act, d_act[t2 % 2])
                vop("tensor_mul", scr[:, :], t1s[:, :], bU[:, :])
                d_dve[t2 % 2] = dve.n
                P.wait("scalar", dve, d_dve[t2 % 2])
                P.wait("scalar", mm, down_done)
                scr_act[t2 % 2] = P.op(
                    "scalar", lambda E, scr=scr, hcol=hcol: E.activation(
                        hshi[:, hcol:hcol + 512], scr[:, :], AF.Copy,
                        bias=0.0, scale=1.0), inc=act)
                P.wait("vector", act, scr_act[t2 % 2])
                P.wait("vector", mm, down_done)
                vop("tensor_sub", hslo[:, hcol:hcol + 512], scr[:, :],
                    hshi[:, hcol:hcol + 512])
            ldslot[(nld - 2) % 2] = last_gate
            ldslot[(nld - 1) % 2] = mm.n
        hs_done = dve.n
        hs_act = act.n
        if upto == "Dhs" and hg == 0:
            dbg = nc.dram_tensor("dbg", [128, 2 * HST], BF16,
                                 kind="ExternalOutput").ap()
            P.join()
            dma("sync", sst, dbg[:, 0:HST], hshi)
            dma("sync", sst, dbg[:, HST:], hslo)
            P.wait("sync", sst)
            P.finalize()
            return nc
        for fm in range(c.NDB):
            kf = hg * c.NDB + fm
            p2s = kf % 2
            wf = ar(O_SWDF + p2s * SWDC, SWDC)
            wph = arb(O_SWDP + p2s * SWDC, SWDC)
            wpl = arb(O_SWDP + p2s * SWDC + SWDC // 2, SWDC)
            P.wait("sync", act, swd_split_a[p2s])
            P.wait("sync", dve, swd_split_d[p2s])
            swl = dma("sync", wsd, wf, swdR[hg, fm, :, :])
            P.wait("scalar", wsd, swl)
            P.wait("scalar", mm, swd_rd[p2s])
            swd_split_a[p2s] = P.op(
                "scalar", lambda E, wph=wph, wf=wf: E.activation(
                    wph, wf, AF.Copy, bias=0.0, scale=1.0), inc=act)
            P.wait("vector", wsd, swl)
            P.wait("vector", mm, swd_rd[p2s])
            P.wait("vector", act, swd_split_a[p2s])
            swd_split_d[p2s] = vop("tensor_sub", wpl, wf, wph)
            accS = ar(O_ACCS + p2s * (c.NM * 512), c.NM * 512)
            P.wait("tensor", act, swd_split_a[p2s])
            P.wait("tensor", dve, swd_split_d[p2s])
            P.wait("tensor", dve, hs_done)
            P.wait("tensor", act, hs_act)
            for tt in range(c.NM):
                bank = PB[4 + (tt % 2)]
                for ps in range(3):
                    hv_ = (hshi, hslo, hshi)[ps]
                    wv2 = (wph, wph, wpl)[ps]
                    for gl in range(c.HGRP):
                        matmul(bank[:, :],
                               hv_[:, gl * c.T + tt * 128:
                                   gl * c.T + tt * 128 + 128],
                               wv2[:, gl * 512:(gl + 1) * 512],
                               start=(ps == 0 and gl == 0),
                               stop=(ps == 2 and gl == c.HGRP - 1),
                               inc=mm if (ps == 2 and gl == c.HGRP - 1)
                               else None)
                P.wait("vector", mm)
                P.wait("vector", swa[p2s], accS_rd[p2s])
                vop("tensor_copy", accS[:, tt * 512:(tt + 1) * 512],
                    bank[:, :])
                P.wait("tensor", dve)
            swd_rd[p2s] = mm.n
            P.wait("gpsimd", dve)
            P.wait("gpsimd", sst, out_base)
            P.wait("gpsimd", tstc[0], tst_all[0])
            P.wait("gpsimd", tstc[1], tst_all[1])
            accS_rd[p2s] = dma(
                "gpsimd", swa[p2s],
                out[0:c.T, :].rearrange("(t p) d -> p t d", p=128)
                [:, :, fm * 512:(fm + 1) * 512],
                accS.rearrange("p (t d) -> p t d", d=512),
                accum_op=ALU.add)
        down_done = mm.n
    P.join()
    shared_adds = (swa[0].n, swa[1].n)
    if upto == "Dpair":
        dbg = nc.dram_tensor("dbg", [128, 2 * c.NC * c.T], BF16,
                             kind="ExternalOutput").ap()
        dma("sync", sst, dbg[:, 0:c.NC * c.T], h2hi)
        dma("sync", sst, dbg[:, c.NC * c.T:], h2lo)
        P.wait("sync", sst)
        P.finalize()
        return nc

    if upto == "D":
        P.wait("sync", swa[0], shared_adds[0])
        P.wait("sync", swa[1], shared_adds[1])
        P.finalize()
        return nc

    O_GAT = R_B
    O_EF = R_B + 2 * c.NC * c.CAP
    O_HE = R_W
    O_GT = R_W + c.NIC * c.CAP
    O_EWD = R_A
    O_EUG = R_A + 2 * c.NIC * 512
    O_ETK = O_EUG + 2 * WQC
    assert O_ETK + 1024 <= R_A + 16384
    assert O_EF + 2 * WQC <= R_B + 16384
    assert 2 * c.NC * c.CAP <= 16384
    assert O_GT + c.NCAP * c.D <= AR_COLS
    nldE = 0
    eugfree = [0, 0]
    ewd_free = 0
    he_free = 0
    gat_rd = [0, 0]
    tr_done = [0, 0]
    gather_done = [0] * (c.E + 1)
    eidx_done = [0] * (c.E + 1)
    scat_prev = 0
    for eng in ("sync", "vector", "tensor", "scalar", "gpsimd"):
        P.wait(eng, swa[0], shared_adds[0])
        P.wait(eng, swa[1], shared_adds[1])
    P.wait("scalar", swr, routing_done)

    def emit_prep(e):
        par = e % 2
        P.wait("vector", etk[0])
        P.wait("vector", etk[1])
        for r in range(c.NCAP):
            P.wait("scalar", dve)
            lr = dma("scalar", ler, erow,
                     ranktok.rearrange("(a b) one -> a (b one)", a=1)
                     [:, e * c.CAP + r * 128:e * c.CAP + r * 128 + 128])
            P.wait("tensor", ler, lr)
            P.wait("tensor", dve)
            P.op("tensor", lambda E: E.transpose(
                PB[6][0:128, 0:1], erow, sident[0:1, 0:1]), inc=mm)
            P.wait("vector", mm)
            vop("tensor_copy", eidx[:, par * c.NCAP + r:par * c.NCAP + r + 1],
                PB[6][0:128, 0:1])
        cwr = (srowB, srowC)[par][0:1, 0:c.CAP]
        P.wait("scalar", mm)
        lcww = dma("scalar", lcw, cwr,
                   rankcw.rearrange("(a b) one -> a (b one)", a=1)
                   [:, e * c.CAP:(e + 1) * c.CAP])
        P.wait("tensor", lcw, lcww)
        P.wait("tensor", act)
        P.wait("tensor", dve)
        matmul(PB[6][0:128, 0:c.CAP], onesr[0:1, :], cwr, True, True, inc=mm)
        P.wait("vector", mm)
        vop("tensor_copy", (sBC1, sBC2)[par][:, 0:c.CAP],
            PB[6][0:128, 0:c.CAP])
        eidx_done[e] = dve.n

    def emit_gather(e):
        par = e % 2
        P.wait("gpsimd", sst, out_base)
        P.wait("gpsimd", tstc[0], tst_all[0])
        P.wait("gpsimd", tstc[1], tst_all[1])
        P.wait("gpsimd", dve, eidx_done[e])
        P.wait("gpsimd", mm, tr_done[1 - par] if e > 0 else 0)
        for r in range(c.NCAP):
            P.op("gpsimd", lambda E, r=r, par=par: E.indirect_dma_start(
                out=ar(O_GT + r * c.D, c.D), out_offset=None,
                in_=h2tok[:, :],
                in_offset=IndirectOffsetOnAxis(
                    eidx[:, par * c.NCAP + r:par * c.NCAP + r + 1], 0)),
                inc=swe)
        gather_done[e] = swe.n

    def gview(par):
        base = O_GAT + par * c.NC * c.CAP
        return (arb(base, c.NC * c.CAP),
                arb(base + c.NC * c.CAP // 2, c.NC * c.CAP))

    def emit_transposes(e):
        par = e % 2
        gh, gl2 = gview(par)
        P.wait("tensor", swe, gather_done[e])
        P.wait("tensor", mm, gat_rd[par])
        bk3 = (PB[4], PB[5], PB[7])
        last_a = [0, 0, 0]
        last_d = [0, 0, 0]
        k3 = 0
        for r in range(c.NCAP):
            for n in range(c.NC):
                bi = k3 % 3
                k3 += 1
                bank = bk3[bi]
                P.wait("tensor", act, last_a[bi])
                P.wait("tensor", dve, last_d[bi])
                P.op("tensor", lambda E, r=r, n=n, bank=bank: E.transpose(
                    bank[:, 0:128],
                    ar(O_GT + r * c.D, c.D)[:, n * 128:(n + 1) * 128],
                    sident[:, :]), inc=mm)
                tmm = mm.n
                cs2 = slice(n * c.CAP + r * 128, n * c.CAP + r * 128 + 128)
                P.wait("scalar", mm, tmm)
                last_a[bi] = P.op(
                    "scalar", lambda E, gh=gh, cs2=cs2, bank=bank:
                    E.activation(gh[:, cs2], bank[:, 0:128], AF.Copy,
                                 bias=0.0, scale=1.0), inc=act)
                P.wait("vector", mm, tmm)
                P.wait("vector", act, last_a[bi])
                last_d[bi] = vop("tensor_sub", gl2[:, cs2], bank[:, 0:128],
                                 gh[:, cs2])
        tr_done[par] = mm.n

    def emit_compute(e, gat_dve):
        par = e % 2
        gath, gatl = gview(par)
        he = ar(O_HE, c.NIC * c.CAP)
        for gc in range(c.NIC):
            stg = [None, None]
            lt = [0, 0]
            pars = [0, 0]
            spl_t = [0, 0]
            for half, colblk in ((0, gc), (1, c.NIC + gc)):
                p2 = E_state["nldE"] % 2
                E_state["nldE"] += 1
                pars[half] = p2
                ef = ar(O_EF + p2 * WQC, WQC)
                egh = arb(O_EUG + p2 * WQC, WQC)
                egl = arb(O_EUG + p2 * WQC + WQC // 2, WQC)
                stg[half] = (egh, egl)
                P.wait("sync", act, E_state["eugsplit_a"][p2])
                P.wait("sync", dve, E_state["eugsplit_d"][p2])
                lt[half] = dma("sync", wst[p2], ef, eugR[e, colblk, :, :])
                P.wait("scalar", wst[p2], lt[half])
                P.wait("scalar", mm, eugfree[p2])
                sa = P.op("scalar", lambda E, egh=egh, ef=ef: E.activation(
                    egh, ef, AF.Copy, bias=0.0, scale=1.0), inc=act)
                P.wait("vector", wst[p2], lt[half])
                P.wait("vector", mm, eugfree[p2])
                P.wait("vector", act, sa)
                sd = vop("tensor_sub", egl, ef, egh)
                spl_t[half] = (sa, sd)
                E_state["eugsplit_a"][p2] = sa
                E_state["eugsplit_d"][p2] = sd
            bG, bU = PB[0], PB[1]
            P.wait("tensor", act, spl_t[0][0])
            P.wait("tensor", dve, spl_t[0][1])
            P.wait("tensor", dve, gat_dve)
            grp_end = [0, 0]
            for bank, half in ((bG, 0), (bU, 1)):
                if half == 1:
                    P.wait("tensor", act, spl_t[1][0])
                    P.wait("tensor", dve, spl_t[1][1])
                for ps in range(3):
                    wv_ = stg[half][(0, 1, 0)[ps]]
                    gv_ = (gath, gath, gatl)[ps]
                    for n in range(c.NC):
                        matmul(bank[0:128, 0:c.CAP],
                               wv_[:, n * 128:(n + 1) * 128],
                               gv_[:, n * c.CAP:(n + 1) * c.CAP],
                               start=(ps == 0 and n == 0),
                               stop=(ps == 2 and n == c.NC - 1),
                               inc=mm if (ps == 2 and n == c.NC - 1)
                               else None)
                grp_end[half] = mm.n
            eugfree[pars[0]] = grp_end[0]
            eugfree[pars[1]] = grp_end[1]
            P.wait("scalar", mm)
            P.wait("scalar", dve)
            P.op("scalar", lambda E, bG=bG: E.activation(
                t1s[:, 0:c.CAP], bG[0:128, 0:c.CAP],
                AF.Silu, bias=0.0, scale=1.0), inc=act)
            P.wait("vector", act)
            vop("tensor_mul", sq[:, 0:c.CAP], t1s[:, 0:c.CAP],
                bU[0:128, 0:c.CAP])
            P.wait("vector", mm, E_state["he_free"])
            vop("tensor_mul", he[:, gc * c.CAP:(gc + 1) * c.CAP],
                sq[:, 0:c.CAP], (sBC1, sBC2)[par][:, 0:c.CAP])
            P.wait("tensor", dve)
        gat_rd[par] = mm.n
        he_done = dve.n
        P.wait("gpsimd", etk[0], E_state["scat_prev0"])
        P.wait("gpsimd", etk[1], E_state["scat_prev1"])
        for fm in range(c.NDB):
            p2 = fm % 2
            wstg2 = ar(O_EWD + p2 * c.NIC * 512, c.NIC * 512)
            P.wait("sync", mm, E_state["ewd_free"][p2])
            wl = dma("sync", ewl[p2], wstg2, ewdR[e, fm, :, :])
            P.wait("tensor", ewl[p2], wl)
            P.wait("tensor", dve, he_done)
            for r in range(c.NCAP):
                bank = PB[2 + (r % 2)]
                P.wait("tensor", dve, E_state["bankcp"][r % 2])
                for gc in range(c.NIC):
                    matmul(bank[:, :],
                           he[:, gc * c.CAP + r * 128:gc * c.CAP + r * 128 + 128],
                           wstg2[:, gc * 512:(gc + 1) * 512],
                           start=(gc == 0), stop=(gc == c.NIC - 1),
                           inc=mm if gc == c.NIC - 1 else None)
                P.wait("vector", mm)
                ep = (fm * c.NCAP + r) % 2
                etkb = ar(O_ETK + ep * 512, 512)
                P.wait("vector", etk[ep])
                sc_dve = vop("tensor_copy", etkb, bank[:, :])
                E_state["bankcp"][r % 2] = sc_dve
                P.wait("gpsimd", dve, sc_dve)
                P.wait("gpsimd", etk[0])
                P.wait("gpsimd", etk[1])
                P.op("gpsimd",
                     lambda E, etkb=etkb, r=r, fm=fm, par=par:
                     E.indirect_dma_start(
                         out=out[:, :],
                         out_offset=IndirectOffsetOnAxis(
                             eidx[:, par * c.NCAP + r:par * c.NCAP + r + 1], 0),
                         in_=etkb, in_offset=None,
                         element_offset=fm * 512,
                         compute_op=ALU.add), inc=etk[ep])
            E_state["ewd_free"][p2] = mm.n
        E_state["scat_prev0"] = etk[0].n
        E_state["scat_prev1"] = etk[1].n
        E_state["he_free"] = mm.n

    E_state = {"nldE": 0, "ewd_free": [0, 0], "scat_prev0": 0, "scat_prev1": 0,
               "he_free": 0, "bankcp": [0, 0], "eugsplit_a": [0, 0],
               "eugsplit_d": [0, 0]}
    gat_dve_done = [0] * (c.E + 1)
    for e in range(c.E + 1):
        if e < c.E:
            emit_prep(e)
            emit_gather(e)
            emit_transposes(e)
            gat_dve_done[e] = dve.n
        if e >= 1:
            emit_compute(e - 1, gat_dve_done[e - 1])

    P.wait("sync", swe)
    P.wait("sync", etk[0])
    P.wait("sync", etk[1])
    P.wait("sync", swa[0])
    P.wait("sync", swa[1])
    P.wait("sync", sst)
    P.finalize()
    return nc


def _rope_tables(S):
    try:
        import jax
        import jax.numpy as jnp
        cpu = jax.devices("cpu")[0]
        with jax.default_device(cpu):
            pos = jnp.arange(S, dtype=jnp.float32)[:, None]
            inv = jnp.exp(jnp.arange(0, 64, 2, dtype=jnp.float32)
                          * (-math.log(10000.0) / 64))
            ang = pos * inv[None, :]
            cos = np.asarray(jnp.cos(ang))
            sin = np.asarray(jnp.sin(ang))
    except Exception:
        pos = np.arange(S, dtype=np.float32)[:, None]
        inv = np.exp(np.arange(0, 64, 2, dtype=np.float32)
                     * np.float32(-math.log(10000.0) / 64)).astype(np.float32)
        ang = (pos * inv[None, :]).astype(np.float32)
        cos = np.cos(ang.astype(np.float64)).astype(np.float32)
        sin = np.sin(ang.astype(np.float64)).astype(np.float32)
    cosx = np.repeat(cos.T, 2, axis=0).astype(np.float32)
    sinx = np.empty((64, S), np.float32)
    sinx[0::2] = -sin.T
    sinx[1::2] = sin.T
    return np.ascontiguousarray(cosx), np.ascontiguousarray(sinx)


def make_inputs(c: CFG, core, x, wq, wk, wv, wo, router, w_ug, w_dn, s_ug, s_dn):
    import ml_dtypes
    b, qi = core // 4, core % 4
    own = ((np.arange(c.NM)[:, None] * 4 + qi) * 128
           + np.arange(128)[None, :]).reshape(-1)
    cosx, sinx = _rope_tables(c.S)
    qc = np.ascontiguousarray(cosx[:, own])
    qs = np.ascontiguousarray(sinx[:, own])
    m = np.zeros((16, 128, 512), np.float32)
    kk = np.arange(128)[:, None]
    jj = np.arange(128)[None, :]
    for r in range(16):
        for s in range(4):
            m[r][:, s * 128:(s + 1) * 128] = (
                (4 * s + qi - r) * 128 + jj - kk >= 0).astype(np.float32)
    ident = np.eye(128, dtype=np.float32)
    triu = np.triu(np.ones((128, 128), np.float32), 1)
    iota8 = np.broadcast_to(np.arange(c.E, dtype=np.float32)[None, :],
                            (128, c.E)).copy()
    itok = (np.arange(c.NM)[None, :] * 128
            + np.arange(128)[:, None]).astype(np.float32)
    cones = np.ones((128, 1), np.float32)
    onesrv = np.ones((128, 128), np.float32)
    ind64 = np.zeros((128, 2), np.float32)
    ind64[0:64, 0] = 1.0
    ind64[64:128, 1] = 1.0
    NC, NF, NJC, NIC, NDB, E, NTC = (c.NC, c.NF, c.NJC, c.NIC, c.NDB, c.E,
                                     c.NTC)
    NHG = NJC // c.HGRP
    xbarr = x[b].astype(np.float32)
    xbRv = np.ascontiguousarray(
        xbarr.reshape(NTC, 512, NC, 128).transpose(0, 3, 2, 1)
        .reshape(NTC, 128, NC * 512))
    xoarr = xbarr[own]
    xoRv = np.ascontiguousarray(
        xoarr.reshape(c.T // 512, 512, NC, 128).transpose(0, 3, 2, 1)
        .reshape(c.T // 512, 128, NC * 512))
    wqRv = np.ascontiguousarray(
        wq.reshape(NC, 128, NC, 128).transpose(0, 3, 2, 1)
        .reshape(NC, 128, NC * 128))
    woRv = np.ascontiguousarray(
        wo.reshape(NC, 128, NC, 128).transpose(0, 3, 2, 1)
        .reshape(NC, 128, NC * 128))
    wkRv = np.ascontiguousarray(
        wk.reshape(NF, 128, NC, 128).transpose(3, 2, 0, 1)
        .reshape(128, NC * NF * 128))
    wvRv = np.ascontiguousarray(
        wv.reshape(NF, 128, NC, 128).transpose(3, 2, 0, 1)
        .reshape(128, NC * NF * 128))
    routerRv = np.ascontiguousarray(
        router.reshape(NC, 128, E).transpose(1, 0, 2).reshape(128, NC * E))
    sugRv = np.ascontiguousarray(
        s_ug.reshape(NC, 128, 2 * NJC, 128).transpose(2, 1, 0, 3)
        .reshape(2 * NJC, 128, NC * 128))
    swdRv = np.ascontiguousarray(
        s_dn.reshape(NHG, c.HGRP, 128, NDB, 512).transpose(0, 3, 2, 1, 4)
        .reshape(NHG, NDB, 128, c.HGRP * 512))
    eugRv = np.ascontiguousarray(
        w_ug.reshape(E, NC, 128, 2 * NIC, 128).transpose(0, 3, 2, 1, 4)
        .reshape(E, 2 * NIC, 128, NC * 128))
    ewdRv = np.ascontiguousarray(
        w_dn.reshape(E, NIC, 128, NDB, 512).transpose(0, 3, 2, 1, 4)
        .reshape(E, NDB, 128, NIC * 512))
    return {
        "xbR": xbRv, "xoR": xoRv,
        "wqR": wqRv, "wkR": wkRv, "wvR": wvRv, "woR": woRv,
        "routerR": routerRv,
        "sugR": sugRv, "swdR": swdRv, "eugR": eugRv, "ewdR": ewdRv,
        "kcos": cosx, "ksin": sinx, "qcost": qc, "qsint": qs,
        "masks16": m.astype(ml_dtypes.bfloat16),
        "identin": ident, "triuin": triu, "iota8in": iota8, "itokin": itok,
        "conesin": cones, "onesrin": onesrv, "ind64in": ind64,
        "identbfin": ident.astype(ml_dtypes.bfloat16),
    }


_CFG = CFG()


def kernel(L0_attn_x, L0_attn_wq, L0_attn_wk, L0_attn_wv, L0_attn_wo,
           L0_ffn_router, L0_ffn_w_up_gate, L0_ffn_w_down,
           L0_ffn_shared_w_up_gate, L0_ffn_shared_w_down):
    c = _CFG
    x = np.asarray(L0_attn_x, np.float32)
    nc = build_program(c)
    in_maps = [make_inputs(c, core, x,
                           np.asarray(L0_attn_wq, np.float32),
                           np.asarray(L0_attn_wk, np.float32),
                           np.asarray(L0_attn_wv, np.float32),
                           np.asarray(L0_attn_wo, np.float32),
                           np.asarray(L0_ffn_router, np.float32),
                           np.asarray(L0_ffn_w_up_gate, np.float32),
                           np.asarray(L0_ffn_w_down, np.float32),
                           np.asarray(L0_ffn_shared_w_up_gate, np.float32),
                           np.asarray(L0_ffn_shared_w_down, np.float32))
               for core in range(8)]
    res = run_bass_kernel_spmd(nc, in_maps, core_ids=list(range(8)))
    outp = np.empty((2, c.S, c.D), np.float32)
    for core in range(8):
        b, qi = core // 4, core % 4
        o = np.asarray(res.results[core]["out"])[:c.T].reshape(c.NM, 128, c.D)
        for mt in range(c.NM):
            g = 4 * mt + qi
            outp[b, g * 128:(g + 1) * 128, :] = o[mt]
    return outp


# version 22
# speedup vs baseline: 1.0594x; 1.0594x over previous
import math
import numpy as np

import concourse.bass as bass
import concourse.mybir as mybir
from concourse.bass_utils import run_bass_kernel_spmd
from concourse.bass_core_rust import IndirectOffsetOnAxis

F32 = mybir.dt.float32
I32 = mybir.dt.int32
BF16 = mybir.dt.bfloat16
AF = mybir.ActivationFunctionType
ALU = mybir.AluOpType
AX = mybir.AxisListType
SWAPMASK = [i ^ 1 for i in range(32)]


class CFG:
    def __init__(self, D=2048, S=4096, H=32, HV=8, JH=5632, IH=1408, E=8, CAP=384):
        self.D, self.S, self.H, self.HV = D, S, H, HV
        self.JH, self.IH, self.E, self.CAP = JH, IH, E, CAP
        self.T = S // 4
        self.NM = self.T // 128
        self.NC = D // 128
        self.NKT = S // 128
        self.NTC = S // 512
        self.NG = self.NM // 4
        self.NF = HV * 64 // 128
        self.NCAP = CAP // 128
        self.NJC = JH // 128
        self.NIC = IH // 128
        self.HGRP = 4
        self.NDB = D // 512
        assert H == 4 * HV and H * 64 == D
        assert D % 512 == 0 and S % 512 == 0 and self.T % 512 == 0
        assert self.NJC % self.HGRP == 0 and IH % 128 == 0
        assert CAP % 128 == 0 and self.NM % 4 == 0 and HV * 64 % 128 == 0


class Chan:
    __slots__ = ("sem", "step", "n")

    def __init__(self, sem, step):
        self.sem, self.step, self.n = sem, step, 0


class Prog:
    ENGS = ("sync", "tensor", "vector", "scalar", "gpsimd")

    def __init__(self, nc):
        self.nc = nc
        self.q = {e: [] for e in self.ENGS}
        self.chans = {}
        self.seen = {e: {} for e in self.ENGS}
        self.nojoin = set()

    def chan(self, name, step=1):
        if name not in self.chans:
            self.chans[name] = Chan(self.nc.alloc_semaphore(name), step)
        return self.chans[name]

    def wait(self, eng, chan, val=None):
        v = chan.n if val is None else val
        if v <= 0:
            return
        seen = self.seen[eng]
        if seen.get(chan.sem.num, 0) >= v:
            return
        seen[chan.sem.num] = v
        sem = chan.sem
        self.q[eng].append(lambda E: E.wait_ge(sem, v))

    def op(self, eng, fn, inc=None, step=None):
        ss = getattr(self, "selfserial", None)
        if ss and eng in ss and inc is ss[eng]:
            self.wait(eng, inc, inc.n)
        if inc is not None:
            add = step if step is not None else inc.step
            inc.n += add
            sem = inc.sem
            self.q[eng].append(lambda E: fn(E).then_inc(sem, add))
            return inc.n
        self.q[eng].append(fn)
        return None

    def join(self):
        snap = [(ch.sem, ch.n) for ch in self.chans.values()
                if ch.n > 0 and ch not in self.nojoin]
        for e in self.ENGS:
            seen = self.seen[e]
            todo = [(s, v) for s, v in snap if seen.get(s.num, 0) < v]
            for s, v in todo:
                seen[s.num] = v
            if todo:
                def f(E, todo=todo):
                    for s, v in todo:
                        E.wait_ge(s, v)
                self.q[e].append(f)

    def finalize(self):
        with self.nc.Block() as block:
            for ename, dec in (("sync", block.sync), ("tensor", block.tensor),
                               ("vector", block.vector), ("scalar", block.scalar),
                               ("gpsimd", block.gpsimd)):
                fns = self.q[ename]
                def body(E, fns=fns):
                    for f in fns:
                        f(E)
                dec(body)


def build_program(c: CFG, upto="E", dbg_pairs=False):
    nc = bass.Bass("TRN2", target_bir_lowering=False, debug=False)
    din = lambda n, sh, d=F32: nc.dram_tensor(n, sh, d, kind="ExternalInput").ap()
    dint = lambda n, sh, d=F32: nc.dram_tensor(n, sh, d, kind="Internal").ap()

    NHG = c.NJC // c.HGRP
    xbR = din("xbR", [c.NTC, 128, c.NC * 512])
    xoR = din("xoR", [c.T // 512, 128, c.NC * 512])
    wqR = din("wqR", [c.NC, 128, c.NC * 128])
    wkR = din("wkR", [128, c.NC * c.NF * 128])
    wvR = din("wvR", [128, c.NC * c.NF * 128])
    woR = din("woR", [c.NC, 128, c.NC * 128])
    routerR = din("routerR", [128, c.NC * c.E])
    sugR = din("sugR", [2 * c.NJC, 128, c.NC * 128])
    swdR = din("swdR", [NHG, c.NDB, 128, c.HGRP * 512])
    eugR = din("eugR", [c.E, 2 * c.NIC, 128, c.NC * 128])
    ewdR = din("ewdR", [c.E, c.NDB, 128, c.NIC * 512])
    kcos = din("kcos", [64, c.S])
    ksin = din("ksin", [64, c.S])
    qcost = din("qcost", [64, c.T])
    qsint = din("qsint", [64, c.T])
    masks16 = din("masks16", [16, 128, 512], BF16)
    identin = din("identin", [128, 128])
    triuin = din("triuin", [128, 128])
    iota8in = din("iota8in", [128, c.E])
    itokin = din("itokin", [128, c.NM])
    conesin = din("conesin", [128, 1])
    onesrin = din("onesrin", [128, 128])
    ind64in = din("ind64in", [128, 2])
    identbfin = din("identbfin", [128, 128], BF16)

    out = nc.dram_tensor("out", [c.T + 1, c.D], F32, kind="ExternalOutput").ap()

    kspill = dint("kspill", [c.HV * 64, c.S])
    vspill = dint("vspill", [c.HV * 64, c.S])
    h2tok = dint("h2tok", [c.T + 1, c.D])
    NRANK = c.E * c.CAP + 128
    ranktok = dint("ranktok", [NRANK, 1])
    rankcw = dint("rankcw", [NRANK, 1])
    dpr = ((lambda n, sh, d: nc.dram_tensor(n, sh, d,
                                            kind="ExternalOutput").ap())
           if dbg_pairs else dint)
    sug_h = dpr("sug_h", [2 * c.NJC, 128, c.NC * 128], BF16)
    sug_l = dpr("sug_l", [2 * c.NJC, 128, c.NC * 128], BF16)

    P = Prog(nc)
    mm = P.chan("mm")
    act = P.chan("act")
    dve = P.chan("dve")
    gpc = P.chan("gpc")
    ldc = P.chan("ldc", 16)
    ldw = P.chan("ldw", 16)
    ldx = [P.chan("ldx0", 16), P.chan("ldx1", 16)]
    rp = P.chan("rp", 16)
    wst = [P.chan("wst0", 16), P.chan("wst1", 16)]
    wsd = P.chan("wsd", 16)
    lrt = P.chan("lrt", 16)
    kvl = P.chan("kvl", 16)
    kvv = P.chan("kvv", 16)
    lxr = [P.chan("lxr0", 16), P.chan("lxr1", 16)]
    spl = P.chan("spl", 16)
    sst = P.chan("sst", 16)
    ler = P.chan("ler", 16)
    lcw = P.chan("lcw", 16)
    swr = P.chan("swr", 16)
    swa = [P.chan("swa0", 16), P.chan("swa1", 16)]
    swe = P.chan("swe", 16)
    ewl = [P.chan("ewl0", 16), P.chan("ewl1", 16)]
    etk = [P.chan("etk0", 16), P.chan("etk1", 16)]
    tstc = [P.chan("tst0", 16), P.chan("tst1", 16)]
    plf = P.chan("plf", 16)
    pst = P.chan("pst", 16)
    psp = P.chan("psp")
    P.nojoin = {plf, pst, psp}
    P.selfserial = {"vector": dve, "scalar": act, "gpsimd": gpc}

    AR_COLS = 43648
    AR = nc.alloc_sbuf_tensor("AR", [128, AR_COLS], F32)
    R_A, R_B, R_W = 0, 16384, 32768
    RWE = AR_COLS - R_W

    def ar(off, cols, rows=128):
        assert off + cols <= AR_COLS, (off, cols)
        return AR[0:rows, off:off + cols]

    ARB = AR[:, :].bitcast(BF16)

    def arb(off_eq, cols_bf, rows=128):
        assert 2 * off_eq + cols_bf <= 2 * AR_COLS, (off_eq, cols_bf)
        return ARB[0:rows, 2 * off_eq:2 * off_eq + cols_bf]

    Al = lambda n, sh, d=F32: nc.alloc_sbuf_tensor(n, sh, d)
    sident = Al("sident", [128, 128])
    sidbf = Al("sidbf", [128, 128], BF16)
    striu = Al("striu", [128, 128])
    siota8 = Al("siota8", [128, c.E])
    sitok = Al("sitok", [128, c.NM])
    scones = Al("scones", [128, 1])
    onesr = Al("onesr", [128, 128])
    sind64 = Al("sind64", [128, 2])

    smask = Al("smask", [128, 16 * 512], BF16)
    kmrun = Al("kmrun", [1, c.HV])
    qmrun = Al("qmrun", [1, c.H])
    negrow = Al("negrow", [1, c.H])
    negBb = Al("negBb", [128, c.H])
    srowA = Al("srowA", [1, 512])
    srowB = Al("srowB", [1, 512])
    srowC = Al("srowC", [1, 512])
    s11 = Al("s11", [1, 1])
    sBC1 = Al("sBC1", [128, 512])
    sBC2 = Al("sBC2", [128, 512])
    sq = Al("sq", [128, 512])
    t1s = Al("t1s", [128, 512])
    slog = Al("slog", [128, c.E])
    sr_is1 = Al("sr_is1", [128, c.E])
    sr_is2 = Al("sr_is2", [128, c.E])
    sr_msk = Al("sr_msk", [128, c.E])
    sr_pos = Al("sr_pos", [128, c.E])
    sr_obc = Al("sr_obc", [128, c.E])
    sr_m1 = Al("sr_m1", [128, 1])
    sr_m2 = Al("sr_m2", [128, 1])
    sr_d = Al("sr_d", [128, 1])
    sr_w1 = Al("sr_w1", [128, 1])
    sr_w2 = Al("sr_w2", [128, 1])
    sr_i1 = Al("sr_i1", [128, 1])
    sr_i2 = Al("sr_i2", [128, 1])
    sr_p1 = Al("sr_p1", [128, 1])
    sr_p2 = Al("sr_p2", [128, 1])
    sr_t = Al("sr_t", [128, c.E])
    sr_s = Al("sr_s", [128, 1])
    srow8 = Al("srow8", [1, (c.NM + 1) * c.E])
    sd1i = Al("sd1i", [128, 2], I32)
    eidx = Al("eidx", [128, 2 * c.NCAP], I32)

    ZT = (NRANK + 127) // 128
    zrf = Al("zrf", [128, ZT])
    ztrf = Al("ztrf", [128, ZT])
    plsc = Al("plsc", [128, 512])
    plscb = plsc[:, :].bitcast(BF16)
    pl_f32 = plsc[:, 0:256]
    pl_hi = plscb[:, 512:768]
    pl_lo = plscb[:, 768:1024]
    erow = srowA[0:1, 0:128]
    cwrow = srowB[0:1, 0:c.CAP] if c.CAP <= 512 else None

    PB = [nc.alloc_psum_tensor("pb%d" % i, [128, 512], F32) for i in range(8)]

    def dma(eng, chan, o, i, **kw):
        return P.op(eng, lambda E: E.dma_start(out=o, in_=i, **kw), inc=chan)

    def ld_cm(eng, chan, dst2d, src, fsel=None):
        s3 = src.rearrange("(n p) f -> p n f", p=128)
        if fsel is not None:
            s3 = s3[:, :, fsel[0]:fsel[1]]
        f = s3.shape[2]
        d3 = dst2d.rearrange("p (n f) -> p n f", f=f)
        return dma(eng, chan, d3, s3)

    def matmul(o, l, r, start, stop, inc=None):
        return P.op("tensor",
                    lambda E: E.matmul(o, l, r, start=start, stop=stop), inc=inc)

    def vop(name, *aps, inc=dve):
        return P.op("vector", lambda E: getattr(E, name)(*aps), inc=inc)

    for o, i in ((sident[:, :], identin), (striu[:, :], triuin),
                 (siota8[:, :], iota8in), (sitok[:, :], itokin),
                 (scones[:, :], conesin), (sind64[:, :], ind64in),
                 (onesr[:, :], onesrin), (sidbf[:, :], identbfin)):
        dma("sync", ldc, o, i[:, :])
    dma("sync", ldc, smask[:, :].rearrange("p (r q) -> p r q", q=512),
        masks16.rearrange("r p q -> p r q"))
    ld_consts = ldc.n

    P.op("gpsimd", lambda E: E.memset(zrf[:, :], 0.0), inc=gpc)
    P.op("gpsimd", lambda E: E.memset(ztrf[:, :], float(c.T)), inc=gpc)
    P.op("gpsimd", lambda E: E.memset(kmrun[:, :], 0.0), inc=gpc)
    P.op("gpsimd", lambda E: E.memset(qmrun[:, :], 0.0), inc=gpc)
    init_zeros = gpc.n
    P.wait("gpsimd", gpc, init_zeros)
    dma("gpsimd", swr, ranktok.rearrange("(p z) one -> p (z one)", p=128),
        ztrf[:, :])
    dma("gpsimd", swr, rankcw.rearrange("(p z) one -> p (z one)", p=128),
        zrf[:, :])
    ranks_zeroed = swr.n
    P.op("gpsimd",
         lambda E: E.memset(bass.AP(AR, R_W + 64,
                                    [[AR_COLS, 128], [65, c.NKT]]), 1.0),
         inc=gpc)
    v65_ones = gpc.n

    PSLAB = 256
    assert (c.NC * 128) % PSLAB == 0
    nsub = (c.NC * 128) // PSLAB
    pr_state = [0, 0]

    def emit_pr_slab(src2, dsth2, dstl2):
        for sb in range(nsub):
            cs = slice(sb * PSLAB, (sb + 1) * PSLAB)
            pld = dma("gpsimd", plf, pl_f32, src2[:, cs])
            P.wait("gpsimd", plf, pld)
            P.wait("gpsimd", pst, pr_state[1])
            P.op("gpsimd", lambda E: E.tensor_copy(pl_hi, pl_f32), inc=psp)
            P.wait("gpsimd", psp)
            P.op("gpsimd", lambda E: E.tensor_sub(pl_lo, pl_f32, pl_hi),
                 inc=psp)
            pr_state[0] = psp.n
            P.wait("gpsimd", psp, pr_state[0])
            dma("gpsimd", pst, dsth2[:, cs], pl_hi)
            pr_state[1] = dma("gpsimd", pst, dstl2[:, cs], pl_lo)

    for blk in range(2 * c.NJC):
        emit_pr_slab(sugR[blk, :, :], sug_h[blk, :, :], sug_l[blk, :, :])
    pr_sug = pst.n

    O_V65 = R_W
    O_S1 = R_W + c.NKT * 65
    NKV = c.NF * 512

    w_k = ar(R_B, c.NC * c.HV * 64)
    w_v = ar(R_B + c.NC * c.HV * 64, c.NC * c.HV * 64)
    dma("sync", ldw, w_k, wkR[:, :])
    dma("sync", ldw, w_v, wvR[:, :])
    ld_wkv = ldw.n

    O_KST = O_S1
    O_VST = O_KST + NKV
    O_RC = O_VST + NKV
    O_RS = O_RC + 512
    O_RSH = O_RS + 512
    assert O_RSH + 512 <= AR_COLS
    kst = ar(O_KST, NKV)
    vst = ar(O_VST, NKV)
    rcs = ar(O_RC, 512)
    rsn = ar(O_RS, 512)

    def emit_ln(tile_of_n, pair=None):
        P.wait("tensor", act)
        P.wait("tensor", dve)
        for n in range(c.NC):
            matmul(PB[0][0:1, 0:512], scones[:, :], tile_of_n(n),
                   start=(n == 0), stop=(n == c.NC - 1),
                   inc=mm if n == c.NC - 1 else None)
        sq_mm = [0, 0, 0, 0]
        for n in range(c.NC):
            buf = (sq, t1s, sBC1, sBC2)[n % 4]
            P.wait("vector", mm, sq_mm[n % 4])
            vop("tensor_mul", buf[:, :], tile_of_n(n), tile_of_n(n))
            P.wait("tensor", dve)
            sq_mm[n % 4] = matmul(PB[1][0:1, 0:512], scones[:, :], buf[:, :],
                                  start=(n == 0), stop=(n == c.NC - 1), inc=mm)
        P.wait("vector", mm)
        vop("tensor_single_scalar", srowA[:, :], PB[0][0:1, :], 1.0 / c.D,
            ALU.mult)
        vop("tensor_single_scalar", srowB[:, :], PB[1][0:1, :], 1.0 / c.D,
            ALU.mult)
        vop("tensor_mul", srowC[:, :], srowA[:, :], srowA[:, :])
        vop("tensor_sub", srowB[:, :], srowB[:, :], srowC[:, :])
        vop("tensor_scalar_add", srowB[:, :], srowB[:, :], 1e-5)
        P.wait("scalar", dve)
        P.op("scalar", lambda E: E.activation(srowB[:, :], srowB[:, :],
                                              AF.Sqrt, bias=0.0, scale=1.0),
             inc=act)
        P.wait("vector", act)
        vop("reciprocal", srowB[:, :], srowB[:, :])
        vop("tensor_mul", srowC[:, :], srowA[:, :], srowB[:, :])
        vop("tensor_scalar_mul", srowC[:, :], srowC[:, :], -1.0)
        P.wait("tensor", dve)
        matmul(PB[2][:, :], onesr[0:1, :], srowB[:, :], True, True, inc=mm)
        matmul(PB[3][:, :], onesr[0:1, :], srowC[:, :], True, True, inc=mm)
        P.wait("vector", mm)
        for n in range(c.NC):
            vop("tensor_mul", t1s[:, :], tile_of_n(n), PB[2][:, :])
            if pair is None:
                vop("tensor_add", tile_of_n(n), t1s[:, :], PB[3][:, :])
            else:
                hi_n, lo_n = pair(n)
                vop("tensor_add", sq[:, :], t1s[:, :], PB[3][:, :])
                vop("tensor_copy", hi_n, sq[:, :])
                vop("tensor_sub", lo_n, sq[:, :], hi_n)

    def emit_rope(buf, col0, tcs=None, tsn=None, tcol=0):
        rsh = ar(O_RSH, 512)
        tcs = rcs if tcs is None else tcs
        tsn = rsn if tsn is None else tsn
        for half in range(2):
            hh = slice(half * 64, half * 64 + 64)
            src = buf[hh, col0:col0 + 512]
            vop("stream_shuffle", rsh[hh, :], src, SWAPMASK)
            vop("tensor_mul", rsh[hh, :], rsh[hh, :], tsn[hh, tcol:tcol + 512])
            vop("tensor_mul", src, src, tcs[hh, tcol:tcol + 512])
            vop("tensor_add", src, src, rsh[hh, :])

    norm_st = {"dve": 0, "mm": 0}

    def emit_norm_accum(buf, col0, run, col2):
        P.wait("vector", mm, norm_st["mm"])
        vop("tensor_mul", sq[:, :], buf[:, col0:col0 + 512],
            buf[:, col0:col0 + 512])
        P.wait("tensor", dve)
        P.wait("tensor", dve, norm_st["dve"])
        for half in range(2):
            matmul(PB[4 + half][0:1, :], sind64[:, half:half + 1], sq[:, :],
                   True, True, inc=mm)
        norm_st["mm"] = mm.n
        P.wait("vector", mm, norm_st["mm"])
        for half in range(2):
            vop("reduce_max", s11[:, :], PB[4 + half][0:1, :], AX.X)
            cc = col2 + half
            vop("tensor_max", run[0:1, cc:cc + 1], run[0:1, cc:cc + 1],
                s11[:, :])
        norm_st["dve"] = dve.n

    CNC = c.NC * 512
    xfree = [(0, 0), (0, 0)]
    prev_spill = 0
    for tc in range(c.NTC):
        par = tc % 2
        xbuf = ar(R_A + par * CNC, CNC)
        P.wait("sync", dve, xfree[par][0])
        P.wait("sync", mm, xfree[par][1])
        xld = dma("sync", ldx[par], xbuf, xbR[tc, :, :])
        P.wait("sync", dve)
        rld = dma("sync", rp, rcs[0:64, :], kcos[:, tc * 512:(tc + 1) * 512])
        rld = dma("sync", rp, rcs[64:128, :], kcos[:, tc * 512:(tc + 1) * 512])
        rld = dma("sync", rp, rsn[0:64, :], ksin[:, tc * 512:(tc + 1) * 512])
        rld = dma("sync", rp, rsn[64:128, :], ksin[:, tc * 512:(tc + 1) * 512])
        if tc == 0:
            P.wait("tensor", ldc, ld_consts)
            P.wait("vector", gpc, init_zeros)
        P.wait("tensor", ldx[par], xld)
        P.wait("vector", ldx[par], xld)
        emit_ln(lambda n, xbuf=xbuf: xbuf[:, n * 512:(n + 1) * 512])
        h_done = dve.n
        P.wait("tensor", ldw, ld_wkv)
        P.wait("tensor", dve, h_done)
        kvgroups = ([(w_k, kst, fo) for fo in range(c.NF)]
                    + [(w_v, vst, fo) for fo in range(c.NF)])
        for gi, (wres, stg, fo) in enumerate(kvgroups):
            bank = PB[2 + (gi % 2)]
            for n in range(c.NC):
                lt = wres[:, (n * c.NF + fo) * 128:(n * c.NF + fo) * 128 + 128]
                matmul(bank[:, :], lt, xbuf[:, n * 512:(n + 1) * 512],
                       start=(n == 0), stop=(n == c.NC - 1),
                       inc=mm if n == c.NC - 1 else None)
            P.wait("vector", mm)
            P.wait("vector", spl, prev_spill)
            vop("tensor_copy", stg[:, fo * 512:(fo + 1) * 512], bank[:, :])
            P.wait("tensor", dve)
            if 1 <= gi <= c.NF:
                kfo = gi - 1
                if gi == 1:
                    P.wait("vector", rp, rld)
                emit_rope(kst, kfo * 512)
                emit_norm_accum(kst, kfo * 512, kmrun, 2 * kfo)
        kv_mm = mm.n
        P.wait("scalar", dve)
        dma("scalar", spl,
            kspill.rearrange("(f p) s -> p f s", p=128)
            [:, :, tc * 512:(tc + 1) * 512],
            kst.rearrange("p (f s) -> p f s", s=512))
        prev_spill = dma(
            "scalar", spl,
            vspill.rearrange("(f p) s -> p f s", p=128)
            [:, :, tc * 512:(tc + 1) * 512],
            vst.rearrange("p (f s) -> p f s", s=512))
        xfree[par] = (h_done, kv_mm)
    a_spills = spl.n
    P.join()

    Qr = ar(R_A, c.NC * c.T)
    for t2 in range(c.T // 512):
        xbuf = ar(R_B + t2 * CNC, CNC)
        xld = dma("sync", ldx[t2 % 2], xbuf, xoR[t2, :, :])
        P.wait("tensor", ldx[t2 % 2], xld)
        P.wait("vector", ldx[t2 % 2], xld)
        emit_ln(lambda n, t2=t2: ar(R_B + t2 * CNC, CNC)
                [:, n * 512:(n + 1) * 512])
    h_own = dve.n

    O_WQ = O_S1
    WQC = c.NC * 128
    O_QRC = O_RSH + 512
    qrc = ar(O_QRC, c.T)
    qrs = ar(O_QRC + c.T, c.T)
    rq1 = dma("sync", rp, qrc[0:64, :], qcost[:, :])
    rq1 = dma("sync", rp, qrc[64:128, :], qcost[:, :])
    rq1 = dma("sync", rp, qrs[0:64, :], qsint[:, :])
    rq1 = dma("sync", rp, qrs[64:128, :], qsint[:, :])

    def emit_qtail(fo):
        for t2 in range(c.T // 512):
            col0 = fo * c.T + t2 * 512
            emit_rope(Qr, col0, qrc, qrs, t2 * 512)
            emit_norm_accum(Qr, col0, qmrun, 2 * fo)

    wqfree = [0, 0]
    for fo in range(c.NC):
        par = fo % 2
        wstg = ar(O_WQ + par * WQC, WQC)
        P.wait("sync", mm, wqfree[par])
        wld = dma("sync", wst[par], wstg, wqR[fo, :, :])
        P.wait("tensor", wst[par], wld)
        P.wait("tensor", dve, h_own)
        for t2 in range(c.T // 512):
            xbuf = ar(R_B + t2 * CNC, CNC)
            bank = PB[2 + (t2 % 2)]
            for n in range(c.NC):
                matmul(bank[:, :], wstg[:, n * 128:(n + 1) * 128],
                       xbuf[:, n * 512:(n + 1) * 512],
                       start=(n == 0), stop=(n == c.NC - 1),
                       inc=mm if n == c.NC - 1 else None)
            P.wait("vector", mm)
            if fo == 0 and t2 == 0:
                P.wait("vector", rp, rq1)
            vop("tensor_copy",
                Qr[:, fo * c.T + t2 * 512:fo * c.T + t2 * 512 + 512],
                bank[:, :])
            P.wait("tensor", dve)
        wqfree[par] = mm.n
        if fo > 0:
            emit_qtail(fo - 1)
    emit_qtail(c.NC - 1)
    P.wait("scalar", dve)
    P.op("scalar", lambda E: E.activation(qmrun[:, :], qmrun[:, :], AF.Sqrt,
                                          bias=0.0, scale=1.0), inc=act)
    P.op("scalar", lambda E: E.activation(kmrun[:, :], kmrun[:, :], AF.Sqrt,
                                          bias=0.0, scale=1.0), inc=act)
    P.wait("vector", act)
    for k in range(4):
        vop("tensor_copy", bass.AP(negrow, k, [[c.H, 1], [4, c.HV]]),
            kmrun[0:1, 0:c.HV])
    vop("tensor_mul", negrow[:, :], negrow[:, :], qmrun[:, :])
    vop("tensor_scalar_mul", negrow[:, :], negrow[:, :], -0.125)
    P.wait("tensor", dve)
    matmul(PB[5][0:128, 0:c.H], onesr[0:1, :], negrow[:, :], True, True,
           inc=mm)
    P.wait("vector", mm)
    vop("tensor_copy", negBb[:, :], PB[5][0:128, 0:c.H])
    P.join()

    if upto == "A2":
        dbg = nc.dram_tensor("dbg", [128, c.NC * c.T], F32,
                             kind="ExternalOutput").ap()
        dma("sync", sst, dbg[:, :], Qr)
        P.wait("sync", sst)
        P.finalize()
        return nc

    oT = ar(R_B, c.NC * c.T)
    v65c = ar(O_V65, c.NKT * 65)
    O_KB = O_S1
    O_VB = O_KB + c.S
    O_PT = O_VB + c.S
    assert O_PT + 512 <= AR_COLS
    kb = ar(O_KB, c.S)
    vb = ar(O_VB, c.S, rows=64)
    pt = [ar(O_PT, 512), sBC2[:, :]]
    lb = sBC1[0:64, 0:512]
    ostp = [P.chan("ostp0", 16), P.chan("ostp1", 16)]
    ost_n = [0]
    kvfree = (0, 0)
    for hv in range(c.HV):
        P.wait("sync", mm, kvfree[0])
        P.wait("sync", dve, kvfree[1])
        if hv == 0:
            P.wait("sync", spl, a_spills)
            P.wait("scalar", spl, a_spills)
        l0 = dma("sync", kvl, kb[0:64, :], kspill[hv * 64:hv * 64 + 64, :])
        l0 = dma("sync", kvl, kb[64:128, :], kspill[hv * 64:hv * 64 + 64, :])
        P.wait("scalar", mm, kvfree[0])
        l1 = dma("scalar", kvv, vb[0:64, :], vspill[hv * 64:hv * 64 + 64, :])
        if hv == 0:
            P.wait("tensor", gpc, v65_ones)
        P.wait("tensor", kvv, l1)
        for kt in range(c.NKT):
            bank = PB[3 + (kt % 2)]
            P.op("tensor", lambda E, kt=kt, bank=bank: E.transpose(
                bank[:, 0:64], vb[0:64, kt * 128:(kt + 1) * 128],
                sident[0:64, 0:64]), inc=mm)
            P.wait("vector", mm)
            vop("tensor_copy", v65c[:, kt * 65:kt * 65 + 64], bank[:, 0:64])
            P.wait("tensor", dve)
        P.wait("tensor", kvl, l0)
        SBa = (PB[0], PB[1])
        SBb = (PB[5], PB[6])
        PTa = (pt[0], pt[1])
        PTb = (ar(O_VB, 512), ar(O_VB + 512, 512))
        DPTH = 2
        for hp in range(2):
            a = hv * 4 + 2 * hp
            fo = a // 2
            qc0 = fo * c.T
            for g in range(c.NG):
                nkt = 16 * g + 16
                qs = qc0 + g * 512
                exp_t = [[0, 0], [0, 0]]
                msk_t = [[0, 0], [0, 0]]
                use_t = [[0, 0], [0, 0]]

                def emit_S(kt):
                    sp = kt % 2
                    P.wait("tensor", act, exp_t[0][sp])
                    P.wait("tensor", act, exp_t[1][sp])
                    ta = matmul(SBa[sp][:, :],
                                kb[0:64, kt * 128:(kt + 1) * 128],
                                Qr[0:64, qs:qs + 512], True, True, inc=mm)
                    tb = matmul(SBb[sp][:, :],
                                kb[64:128, kt * 128:(kt + 1) * 128],
                                Qr[64:128, qs:qs + 512], True, True, inc=mm)
                    return tb

                def emit_exp(kt, s_t):
                    sp = kt % 2
                    r = kt - 16 * g
                    P.wait("scalar", mm, s_t)
                    for hx, (SB, PT) in enumerate(((SBa, PTa), (SBb, PTb))):
                        P.wait("scalar", mm, use_t[hx][sp])
                        P.wait("scalar", dve, msk_t[hx][sp])
                        exp_t[hx][sp] = P.op(
                            "scalar",
                            lambda E, hx=hx, sp=sp, a=a: E.activation(
                                (PTa, PTb)[hx][sp], (SBa, SBb)[hx][sp][:, :],
                                AF.Exp, bias=negBb[:, a + hx:a + hx + 1],
                                scale=0.125), inc=act)
                        if r >= 0:
                            P.wait("vector", act, exp_t[hx][sp])
                            msk_t[hx][sp] = vop(
                                "tensor_mul", (PTa, PTb)[hx][sp],
                                (PTa, PTb)[hx][sp],
                                smask[:, r * 512:(r + 1) * 512])

                def emit_AV(kt):
                    sp = kt % 2
                    r = kt - 16 * g
                    for hx in range(2):
                        if r >= 0:
                            P.wait("tensor", dve, msk_t[hx][sp])
                        else:
                            P.wait("tensor", act, exp_t[hx][sp])
                        use_t[hx][sp] = matmul(
                            (PB[2], PB[7])[hx][0:65, :],
                            v65c[:, kt * 65:kt * 65 + 65],
                            (PTa, PTb)[hx][sp],
                            start=(kt == 0), stop=(kt == nkt - 1), inc=mm)

                for kt in range(nkt + DPTH):
                    if kt >= DPTH:
                        emit_AV(kt - DPTH)
                    if kt < nkt:
                        s_t = emit_S(kt)
                        emit_exp(kt, s_t)
                P.wait("vector", mm, use_t[1][(nkt - 1) % 2])
                for hx, AVB in enumerate((PB[2], PB[7])):
                    vop("reciprocal", sq[64:65, :], AVB[64:65, :])
                    P.wait("tensor", dve)
                    matmul(PB[3][0:64, :], onesr[64:65, 0:64], sq[64:65, :],
                           True, True, inc=mm)
                    P.wait("vector", mm)
                    vop("tensor_copy", lb, PB[3][0:64, :])
                    ocol = qs
                    if hx == 0:
                        vop("tensor_mul", oT[0:64, ocol:ocol + 512],
                            AVB[0:64, :], lb)
                    else:
                        par_o = ost_n[0] % 2
                        stg_o = (t1s[0:64, :], sq[0:64, :])[par_o]
                        P.wait("vector", ostp[par_o])
                        vop("tensor_mul", stg_o, AVB[0:64, :], lb)
                        P.wait("scalar", dve)
                        dma("scalar", ostp[par_o],
                            oT[64:128, ocol:ocol + 512], stg_o)
                        ost_n[0] += 1
                    P.wait("tensor", dve)
        kvfree = (mm.n, dve.n)
    P.join()

    if upto == "B":
        dbg = nc.dram_tensor("dbg", [128, c.NC * c.T], F32,
                             kind="ExternalOutput").ap()
        dma("sync", sst, dbg[:, :], oT)
        P.wait("sync", sst)
        P.finalize()
        return nc

    x2 = ar(R_A, c.NC * c.T)
    O_WO = O_S1
    O_XR = O_WO + 2 * WQC
    O_TS = O_XR + 1024
    assert O_TS + WQC <= AR_COLS
    wofree = [0, 0]
    xrfree = [0, 0]
    for fo in range(c.NC):
        par = fo % 2
        wstg = ar(O_WO + par * WQC, WQC)
        P.wait("sync", mm, wofree[par])
        wld = dma("sync", wst[par], wstg, woR[fo, :, :])
        for t2 in range(c.T // 512):
            xpar = t2 % 2
            xr = ar(O_XR + xpar * 512, 512)
            P.wait("scalar", dve, xrfree[xpar])
            xrld = dma("scalar", lxr[xpar], xr,
                       xoR[t2, :, fo * 512:(fo + 1) * 512])
            bank = PB[0 + (t2 % 2)]
            P.wait("tensor", wst[par], wld)
            for n in range(c.NC):
                matmul(bank[:, :], wstg[:, n * 128:(n + 1) * 128],
                       oT[:, n * c.T + t2 * 512:n * c.T + t2 * 512 + 512],
                       start=(n == 0), stop=(n == c.NC - 1),
                       inc=mm if n == c.NC - 1 else None)
            P.wait("vector", mm)
            P.wait("vector", lxr[xpar], xrld)
            xrfree[xpar] = vop(
                "tensor_add",
                x2[:, fo * c.T + t2 * 512:fo * c.T + t2 * 512 + 512],
                bank[:, :], xr)
            P.wait("tensor", dve)
        wofree[par] = mm.n
    P.join()

    if upto == "C1":
        dbg = nc.dram_tensor("dbg", [128, c.NC * c.T], F32,
                             kind="ExternalOutput").ap()
        dma("sync", sst, dbg[:, :], x2)
        P.wait("sync", sst)
        P.finalize()
        return nc

    h2hi = arb(R_B, c.NC * c.T)
    h2lo = arb(R_B + 8192, c.NC * c.T)
    for t2 in range(c.T // 512):
        emit_ln(lambda n, t2=t2:
                x2[:, n * c.T + t2 * 512:n * c.T + t2 * 512 + 512],
                pair=lambda n, t2=t2:
                (h2hi[:, n * c.T + t2 * 512:n * c.T + t2 * 512 + 512],
                 h2lo[:, n * c.T + t2 * 512:n * c.T + t2 * 512 + 512]))

    if upto == "C2":
        dbg = nc.dram_tensor("dbg", [128, 2 * c.NC * c.T], BF16,
                             kind="ExternalOutput").ap()
        P.join()
        dma("sync", sst, dbg[:, 0:c.NC * c.T], h2hi)
        dma("sync", sst, dbg[:, c.NC * c.T:], h2lo)
        P.wait("sync", sst)
        P.finalize()
        return nc

    tsb = [ar(O_XR, WQC), ar(O_XR + WQC, WQC)]
    tw = [0, 0]
    ntt = 0
    for which in range(2):
        dstap = (out, h2tok)[which]
        for tt in range(c.NM):
            tp_ = ntt % 2
            ntt += 1
            tstg = tsb[tp_]
            P.wait("vector", tstc[tp_], tw[tp_])
            for n in range(c.NC):
                bank = PB[4 + (n % 2)]
                cs0 = n * c.T + tt * 128
                if which == 0:
                    P.op("tensor", lambda E, cs0=cs0, bank=bank: E.transpose(
                         bank[:, 0:128], x2[:, cs0:cs0 + 128],
                         sident[:, :]), inc=mm)
                else:
                    matmul(bank[:, 0:128], h2hi[:, cs0:cs0 + 128],
                           sidbf[:, :], True, False)
                    matmul(bank[:, 0:128], h2lo[:, cs0:cs0 + 128],
                           sidbf[:, :], False, True, inc=mm)
                P.wait("vector", mm)
                vop("tensor_copy", tstg[:, n * 128:(n + 1) * 128],
                    bank[:, 0:128])
                P.wait("tensor", dve)
            P.wait("scalar", dve)
            tw[tp_] = dma("scalar", tstc[tp_], dstap[tt * 128:(tt + 1) * 128, :],
                          tstg)
    P.wait("vector", tstc[0])
    P.wait("vector", tstc[1])
    vop("memset", tsb[0][0:1, 0:c.D], 0.0)
    P.wait("scalar", dve)
    dma("scalar", sst, h2tok[c.T:c.T + 1, :], tsb[0][0:1, 0:c.D])
    out_base = sst.n
    tst_all = (tstc[0].n, tstc[1].n)

    srout = ar(O_WO, c.NC * c.E)
    srhi = arb(O_WO + 256, c.NC * c.E)
    srlo = arb(O_WO + 256 + c.NC * c.E // 2, c.NC * c.E)
    rout_ld = dma("sync", lrt, srout, routerR[:, :])
    P.wait("vector", lrt, rout_ld)
    vop("tensor_copy", srhi, srout)
    vop("tensor_sub", srlo, srout, srhi)
    P.wait("tensor", dve)
    for tt in range(c.NM):
        for ps in range(3):
            hh = (h2hi, h2lo, h2hi)[ps]
            rr = (srhi, srhi, srlo)[ps]
            for n in range(c.NC):
                matmul(PB[6][0:128, 0:c.E],
                       hh[:, n * c.T + tt * 128:n * c.T + tt * 128 + 128],
                       rr[:, n * c.E:(n + 1) * c.E],
                       start=(ps == 0 and n == 0),
                       stop=(ps == 2 and n == c.NC - 1),
                       inc=mm if (ps == 2 and n == c.NC - 1) else None)
        P.wait("vector", mm)
        P.wait("vector", swr)
        vop("tensor_copy", slog[:, :], PB[6][0:128, 0:c.E])
        P.wait("tensor", dve)
        vop("reduce_max", sr_m1[:, :], slog[:, :], AX.X)
        vop("tensor_single_scalar", sr_is1[:, :], slog[:, :], sr_m1[:, :],
            ALU.is_ge)
        vop("tensor_scalar", sr_msk[:, :], sr_is1[:, :], -1e30, None, ALU.mult)
        vop("tensor_add", sr_msk[:, :], sr_msk[:, :], slog[:, :])
        vop("reduce_max", sr_m2[:, :], sr_msk[:, :], AX.X)
        vop("tensor_single_scalar", sr_is2[:, :], sr_msk[:, :], sr_m2[:, :],
            ALU.is_ge)
        vop("tensor_sub", sr_d[:, :], sr_m2[:, :], sr_m1[:, :])
        P.wait("scalar", dve)
        P.op("scalar", lambda E: E.activation(sr_d[:, :], sr_d[:, :], AF.Exp,
                                              bias=0.0, scale=1.0), inc=act)
        P.wait("vector", act)
        vop("tensor_scalar_add", sr_s[:, :], sr_d[:, :], 1.0)
        vop("reciprocal", sr_w1[:, :], sr_s[:, :])
        vop("tensor_mul", sr_w2[:, :], sr_d[:, :], sr_w1[:, :])
        vop("tensor_mul", sr_t[:, :], sr_is1[:, :], siota8[:, :])
        vop("reduce_sum", sr_i1[:, :], sr_t[:, :], AX.X)
        vop("tensor_mul", sr_t[:, :], sr_is2[:, :], siota8[:, :])
        vop("reduce_sum", sr_i2[:, :], sr_t[:, :], AX.X)
        vop("tensor_add", sr_t[:, :], sr_is1[:, :], sr_is2[:, :])
        P.wait("tensor", dve)
        P.wait("tensor", act)
        matmul(PB[7][0:128, 0:c.E], striu[:, :], sr_t[:, :], True, True,
               inc=mm)
        matmul(PB[6][0:1, 0:c.E], scones[:, :], sr_t[:, :], True, True,
               inc=mm)
        P.wait("vector", mm)
        vop("tensor_copy", srow8[:, tt * c.E:(tt + 1) * c.E], PB[6][0:1, 0:c.E])
        if tt == 0:
            vop("memset", srow8[:, c.NM * c.E:(c.NM + 1) * c.E], 0.0)
        else:
            vop("tensor_add", srow8[:, c.NM * c.E:(c.NM + 1) * c.E],
                srow8[:, c.NM * c.E:(c.NM + 1) * c.E],
                srow8[:, (tt - 1) * c.E:tt * c.E])
        P.wait("tensor", dve)
        matmul(PB[5][0:128, 0:c.E], onesr[0:1, :],
               srow8[0:1, c.NM * c.E:(c.NM + 1) * c.E], True, True, inc=mm)
        P.wait("vector", mm)
        vop("tensor_copy", sr_obc[:, :], PB[5][0:128, 0:c.E])
        vop("tensor_add", sr_pos[:, :], PB[7][0:128, 0:c.E], sr_obc[:, :])
        P.wait("tensor", dve)
        for isx, ix, px, wx, k in ((sr_is1, sr_i1, sr_p1, sr_w1, 0),
                                   (sr_is2, sr_i2, sr_p2, sr_w2, 1)):
            vop("tensor_mul", sr_t[:, :], isx[:, :], sr_pos[:, :])
            vop("reduce_sum", px[:, :], sr_t[:, :], AX.X)
            vop("tensor_scalar", sr_s[:, :], ix[:, :], float(c.CAP), None,
                ALU.mult)
            vop("tensor_add", sr_s[:, :], sr_s[:, :], px[:, :])
            vop("tensor_scalar_min", sr_s[:, :], sr_s[:, :],
                float(c.E * c.CAP))
            vop("tensor_copy", sd1i[:, k:k + 1], sr_s[:, :])
            P.wait("gpsimd", dve)
            if tt == 0 and k == 0:
                P.wait("gpsimd", swr, ranks_zeroed)
            P.op("gpsimd", lambda E, k=k, tt=tt: E.indirect_dma_start(
                out=ranktok[:, :],
                out_offset=IndirectOffsetOnAxis(sd1i[:, k:k + 1], 0),
                in_=sitok[:, tt:tt + 1], in_offset=None), inc=swr)
            P.op("gpsimd", lambda E, k=k, wx=wx: E.indirect_dma_start(
                out=rankcw[:, :],
                out_offset=IndirectOffsetOnAxis(sd1i[:, k:k + 1], 0),
                in_=wx[:, :], in_offset=None), inc=swr)
    routing_done = swr.n
    P.join()

    if upto == "C":
        dbg = nc.dram_tensor("dbg", [128, 2 * c.NC * c.T], BF16,
                             kind="ExternalOutput").ap()
        dma("sync", sst, dbg[:, 0:c.NC * c.T], h2hi)
        dma("sync", sst, dbg[:, c.NC * c.T:], h2lo)
        P.wait("sync", sst)
        P.finalize()
        return nc

    O_HS = R_W
    O_SUG = R_W + c.HGRP * c.T
    SWDC = c.HGRP * 512
    O_SWDF = R_A
    O_SWDP = R_A + 2 * SWDC
    O_ACCS = R_A + 4 * SWDC
    assert O_ACCS + 2 * c.NM * 512 <= R_A + 16384
    HST = c.HGRP * c.T
    hshi = arb(O_HS, HST)
    hslo = arb(O_HS + HST // 2, HST)
    ldslot = [0, 0]
    nld = 0
    down_done = 0
    swd_split_a = [0, 0]
    swd_split_d = [0, 0]
    swd_rd = [0, 0]
    accS_rd = [0, 0]
    scr_act = [0, 0]
    d_h2 = dve.n
    d_act = [0, 0]
    d_dve = [0, 0]
    for hg in range(NHG):
        for gl in range(c.HGRP):
            gc = hg * c.HGRP + gl
            stg = [None, None]
            lds_t = [0, 0]
            for half, colblk in ((0, gc), (1, c.NJC + gc)):
                par = nld % 2
                nld += 1
                sgh = arb(O_SUG + par * WQC, WQC)
                sgl = arb(O_SUG + par * WQC + WQC // 2, WQC)
                stg[half] = (sgh, sgl)
                P.wait("sync", mm, ldslot[par])
                if nld <= 2:
                    P.wait("sync", pst, pr_sug)
                dma("sync", wst[par], sgh, sug_h[colblk, :, :])
                lds_t[half] = dma("sync", wst[par], sgl, sug_l[colblk, :, :])
            last_gate = 0
            for t2 in range(c.T // 512):
                bG = PB[0 + 2 * (t2 % 2)]
                bU = PB[1 + 2 * (t2 % 2)]
                P.wait("tensor", wst[(nld - 2) % 2], lds_t[0])
                P.wait("tensor", dve, d_h2)
                P.wait("tensor", act, d_act[t2 % 2])
                P.wait("tensor", dve, d_dve[t2 % 2])
                for bank, half in ((bG, 0), (bU, 1)):
                    if half == 1:
                        P.wait("tensor", wst[(nld - 1) % 2], lds_t[1])
                    for ps in range(3):
                        wv_ = stg[half][(0, 1, 0)[ps]]
                        xv_ = (h2hi, h2hi, h2lo)[ps]
                        for n in range(c.NC):
                            matmul(bank[:, :], wv_[:, n * 128:(n + 1) * 128],
                                   xv_[:, n * c.T + t2 * 512:
                                       n * c.T + t2 * 512 + 512],
                                   start=(ps == 0 and n == 0),
                                   stop=(ps == 2 and n == c.NC - 1),
                                   inc=mm if (ps == 2 and n == c.NC - 1)
                                   else None)
                    if half == 0:
                        last_gate = mm.n
                gum = mm.n
                P.wait("scalar", mm, gum)
                P.wait("scalar", dve)
                P.op("scalar", lambda E, bG=bG: E.activation(
                    t1s[:, :], bG[:, :], AF.Silu, bias=0.0, scale=1.0),
                    inc=act)
                d_act[t2 % 2] = act.n
                scr = (sBC1, sBC2)[t2 % 2]
                hcol = gl * c.T + t2 * 512
                P.wait("vector", act, scr_act[t2 % 2])
                P.wait("vector", act, d_act[t2 % 2])
                vop("tensor_mul", scr[:, :], t1s[:, :], bU[:, :])
                d_dve[t2 % 2] = dve.n
                P.wait("scalar", dve, d_dve[t2 % 2])
                P.wait("scalar", mm, down_done)
                scr_act[t2 % 2] = P.op(
                    "scalar", lambda E, scr=scr, hcol=hcol: E.activation(
                        hshi[:, hcol:hcol + 512], scr[:, :], AF.Copy,
                        bias=0.0, scale=1.0), inc=act)
                P.wait("vector", act, scr_act[t2 % 2])
                P.wait("vector", mm, down_done)
                vop("tensor_sub", hslo[:, hcol:hcol + 512], scr[:, :],
                    hshi[:, hcol:hcol + 512])
            ldslot[(nld - 2) % 2] = last_gate
            ldslot[(nld - 1) % 2] = mm.n
        hs_done = dve.n
        hs_act = act.n
        if upto == "Dhs" and hg == 0:
            dbg = nc.dram_tensor("dbg", [128, 2 * HST], BF16,
                                 kind="ExternalOutput").ap()
            P.join()
            dma("sync", sst, dbg[:, 0:HST], hshi)
            dma("sync", sst, dbg[:, HST:], hslo)
            P.wait("sync", sst)
            P.finalize()
            return nc
        for fm in range(c.NDB):
            kf = hg * c.NDB + fm
            p2s = kf % 2
            wf = ar(O_SWDF + p2s * SWDC, SWDC)
            wph = arb(O_SWDP + p2s * SWDC, SWDC)
            wpl = arb(O_SWDP + p2s * SWDC + SWDC // 2, SWDC)
            P.wait("sync", act, swd_split_a[p2s])
            P.wait("sync", dve, swd_split_d[p2s])
            swl = dma("sync", wsd, wf, swdR[hg, fm, :, :])
            P.wait("scalar", wsd, swl)
            P.wait("scalar", mm, swd_rd[p2s])
            swd_split_a[p2s] = P.op(
                "scalar", lambda E, wph=wph, wf=wf: E.activation(
                    wph, wf, AF.Copy, bias=0.0, scale=1.0), inc=act)
            P.wait("vector", wsd, swl)
            P.wait("vector", mm, swd_rd[p2s])
            P.wait("vector", act, swd_split_a[p2s])
            swd_split_d[p2s] = vop("tensor_sub", wpl, wf, wph)
            accS = ar(O_ACCS + p2s * (c.NM * 512), c.NM * 512)
            P.wait("tensor", act, swd_split_a[p2s])
            P.wait("tensor", dve, swd_split_d[p2s])
            P.wait("tensor", dve, hs_done)
            P.wait("tensor", act, hs_act)
            for tt in range(c.NM):
                bank = PB[4 + (tt % 2)]
                for ps in range(3):
                    hv_ = (hshi, hslo, hshi)[ps]
                    wv2 = (wph, wph, wpl)[ps]
                    for gl in range(c.HGRP):
                        matmul(bank[:, :],
                               hv_[:, gl * c.T + tt * 128:
                                   gl * c.T + tt * 128 + 128],
                               wv2[:, gl * 512:(gl + 1) * 512],
                               start=(ps == 0 and gl == 0),
                               stop=(ps == 2 and gl == c.HGRP - 1),
                               inc=mm if (ps == 2 and gl == c.HGRP - 1)
                               else None)
                P.wait("vector", mm)
                P.wait("vector", swa[p2s], accS_rd[p2s])
                vop("tensor_copy", accS[:, tt * 512:(tt + 1) * 512],
                    bank[:, :])
                P.wait("tensor", dve)
            swd_rd[p2s] = mm.n
            P.wait("gpsimd", dve)
            P.wait("gpsimd", sst, out_base)
            P.wait("gpsimd", tstc[0], tst_all[0])
            P.wait("gpsimd", tstc[1], tst_all[1])
            accS_rd[p2s] = dma(
                "gpsimd", swa[p2s],
                out[0:c.T, :].rearrange("(t p) d -> p t d", p=128)
                [:, :, fm * 512:(fm + 1) * 512],
                accS.rearrange("p (t d) -> p t d", d=512),
                accum_op=ALU.add)
        down_done = mm.n
    P.join()
    shared_adds = (swa[0].n, swa[1].n)
    if upto == "Dpair":
        dbg = nc.dram_tensor("dbg", [128, 2 * c.NC * c.T], BF16,
                             kind="ExternalOutput").ap()
        dma("sync", sst, dbg[:, 0:c.NC * c.T], h2hi)
        dma("sync", sst, dbg[:, c.NC * c.T:], h2lo)
        P.wait("sync", sst)
        P.finalize()
        return nc

    if upto == "D":
        P.wait("sync", swa[0], shared_adds[0])
        P.wait("sync", swa[1], shared_adds[1])
        P.finalize()
        return nc

    O_GAT = R_B
    O_EF = R_B + 2 * c.NC * c.CAP
    O_HE = R_W
    O_GT = R_W + c.NIC * c.CAP
    O_EWD = R_A
    O_EUG = R_A + 2 * c.NIC * 512
    O_ETK = O_EUG + 2 * WQC
    assert O_ETK + 1024 <= R_A + 16384
    assert O_EF + 2 * WQC <= R_B + 16384
    assert 2 * c.NC * c.CAP <= 16384
    assert O_GT + c.NCAP * c.D <= AR_COLS
    nldE = 0
    eugfree = [0, 0]
    ewd_free = 0
    he_free = 0
    gat_rd = [0, 0]
    tr_done = [0, 0]
    gather_done = [0] * (c.E + 1)
    eidx_done = [0] * (c.E + 1)
    scat_prev = 0
    for eng in ("sync", "vector", "tensor", "scalar", "gpsimd"):
        P.wait(eng, swa[0], shared_adds[0])
        P.wait(eng, swa[1], shared_adds[1])
    P.wait("scalar", swr, routing_done)

    def emit_prep(e):
        par = e % 2
        P.wait("vector", etk[0])
        P.wait("vector", etk[1])
        for r in range(c.NCAP):
            P.wait("scalar", dve)
            lr = dma("scalar", ler, erow,
                     ranktok.rearrange("(a b) one -> a (b one)", a=1)
                     [:, e * c.CAP + r * 128:e * c.CAP + r * 128 + 128])
            P.wait("tensor", ler, lr)
            P.wait("tensor", dve)
            P.op("tensor", lambda E: E.transpose(
                PB[6][0:128, 0:1], erow, sident[0:1, 0:1]), inc=mm)
            P.wait("vector", mm)
            vop("tensor_copy", eidx[:, par * c.NCAP + r:par * c.NCAP + r + 1],
                PB[6][0:128, 0:1])
        cwr = (srowB, srowC)[par][0:1, 0:c.CAP]
        P.wait("scalar", mm)
        lcww = dma("scalar", lcw, cwr,
                   rankcw.rearrange("(a b) one -> a (b one)", a=1)
                   [:, e * c.CAP:(e + 1) * c.CAP])
        P.wait("tensor", lcw, lcww)
        P.wait("tensor", act)
        P.wait("tensor", dve)
        matmul(PB[6][0:128, 0:c.CAP], onesr[0:1, :], cwr, True, True, inc=mm)
        P.wait("vector", mm)
        vop("tensor_copy", (sBC1, sBC2)[par][:, 0:c.CAP],
            PB[6][0:128, 0:c.CAP])
        eidx_done[e] = dve.n

    def emit_gather(e):
        par = e % 2
        P.wait("gpsimd", sst, out_base)
        P.wait("gpsimd", tstc[0], tst_all[0])
        P.wait("gpsimd", tstc[1], tst_all[1])
        P.wait("gpsimd", dve, eidx_done[e])
        P.wait("gpsimd", mm, tr_done[1 - par] if e > 0 else 0)
        for r in range(c.NCAP):
            P.op("gpsimd", lambda E, r=r, par=par: E.indirect_dma_start(
                out=ar(O_GT + r * c.D, c.D), out_offset=None,
                in_=h2tok[:, :],
                in_offset=IndirectOffsetOnAxis(
                    eidx[:, par * c.NCAP + r:par * c.NCAP + r + 1], 0)),
                inc=swe)
        gather_done[e] = swe.n

    def gview(par):
        base = O_GAT + par * c.NC * c.CAP
        return (arb(base, c.NC * c.CAP),
                arb(base + c.NC * c.CAP // 2, c.NC * c.CAP))

    def emit_transposes(e):
        par = e % 2
        gh, gl2 = gview(par)
        P.wait("tensor", swe, gather_done[e])
        P.wait("tensor", mm, gat_rd[par])
        bk3 = (PB[4], PB[5], PB[7])
        last_a = [0, 0, 0]
        last_d = [0, 0, 0]
        k3 = 0
        for r in range(c.NCAP):
            for n in range(c.NC):
                bi = k3 % 3
                k3 += 1
                bank = bk3[bi]
                P.wait("tensor", act, last_a[bi])
                P.wait("tensor", dve, last_d[bi])
                P.op("tensor", lambda E, r=r, n=n, bank=bank: E.transpose(
                    bank[:, 0:128],
                    ar(O_GT + r * c.D, c.D)[:, n * 128:(n + 1) * 128],
                    sident[:, :]), inc=mm)
                tmm = mm.n
                cs2 = slice(n * c.CAP + r * 128, n * c.CAP + r * 128 + 128)
                P.wait("scalar", mm, tmm)
                last_a[bi] = P.op(
                    "scalar", lambda E, gh=gh, cs2=cs2, bank=bank:
                    E.activation(gh[:, cs2], bank[:, 0:128], AF.Copy,
                                 bias=0.0, scale=1.0), inc=act)
                P.wait("vector", mm, tmm)
                P.wait("vector", act, last_a[bi])
                last_d[bi] = vop("tensor_sub", gl2[:, cs2], bank[:, 0:128],
                                 gh[:, cs2])
        tr_done[par] = mm.n

    def emit_compute(e, gat_dve):
        par = e % 2
        gath, gatl = gview(par)
        he = ar(O_HE, c.NIC * c.CAP)
        for gc in range(c.NIC):
            stg = [None, None]
            lt = [0, 0]
            pars = [0, 0]
            spl_t = [0, 0]
            for half, colblk in ((0, gc), (1, c.NIC + gc)):
                p2 = E_state["nldE"] % 2
                E_state["nldE"] += 1
                pars[half] = p2
                ef = ar(O_EF + p2 * WQC, WQC)
                egh = arb(O_EUG + p2 * WQC, WQC)
                egl = arb(O_EUG + p2 * WQC + WQC // 2, WQC)
                stg[half] = (egh, egl)
                P.wait("sync", act, E_state["eugsplit_a"][p2])
                P.wait("sync", dve, E_state["eugsplit_d"][p2])
                lt[half] = dma("sync", wst[p2], ef, eugR[e, colblk, :, :])
                P.wait("scalar", wst[p2], lt[half])
                P.wait("scalar", mm, eugfree[p2])
                sa = P.op("scalar", lambda E, egh=egh, ef=ef: E.activation(
                    egh, ef, AF.Copy, bias=0.0, scale=1.0), inc=act)
                P.wait("vector", wst[p2], lt[half])
                P.wait("vector", mm, eugfree[p2])
                P.wait("vector", act, sa)
                sd = vop("tensor_sub", egl, ef, egh)
                spl_t[half] = (sa, sd)
                E_state["eugsplit_a"][p2] = sa
                E_state["eugsplit_d"][p2] = sd
            bG, bU = PB[0], PB[1]
            P.wait("tensor", act, spl_t[0][0])
            P.wait("tensor", dve, spl_t[0][1])
            P.wait("tensor", dve, gat_dve)
            grp_end = [0, 0]
            for bank, half in ((bG, 0), (bU, 1)):
                if half == 1:
                    P.wait("tensor", act, spl_t[1][0])
                    P.wait("tensor", dve, spl_t[1][1])
                for ps in range(3):
                    wv_ = stg[half][(0, 1, 0)[ps]]
                    gv_ = (gath, gath, gatl)[ps]
                    for n in range(c.NC):
                        matmul(bank[0:128, 0:c.CAP],
                               wv_[:, n * 128:(n + 1) * 128],
                               gv_[:, n * c.CAP:(n + 1) * c.CAP],
                               start=(ps == 0 and n == 0),
                               stop=(ps == 2 and n == c.NC - 1),
                               inc=mm if (ps == 2 and n == c.NC - 1)
                               else None)
                grp_end[half] = mm.n
            eugfree[pars[0]] = grp_end[0]
            eugfree[pars[1]] = grp_end[1]
            P.wait("scalar", mm)
            P.wait("scalar", dve)
            P.op("scalar", lambda E, bG=bG: E.activation(
                t1s[:, 0:c.CAP], bG[0:128, 0:c.CAP],
                AF.Silu, bias=0.0, scale=1.0), inc=act)
            P.wait("vector", act)
            vop("tensor_mul", sq[:, 0:c.CAP], t1s[:, 0:c.CAP],
                bU[0:128, 0:c.CAP])
            P.wait("vector", mm, E_state["he_free"])
            vop("tensor_mul", he[:, gc * c.CAP:(gc + 1) * c.CAP],
                sq[:, 0:c.CAP], (sBC1, sBC2)[par][:, 0:c.CAP])
            P.wait("tensor", dve)
        gat_rd[par] = mm.n
        he_done = dve.n
        P.wait("gpsimd", etk[0], E_state["scat_prev0"])
        P.wait("gpsimd", etk[1], E_state["scat_prev1"])
        for fm in range(c.NDB):
            p2 = fm % 2
            wstg2 = ar(O_EWD + p2 * c.NIC * 512, c.NIC * 512)
            P.wait("sync", mm, E_state["ewd_free"][p2])
            wl = dma("sync", ewl[p2], wstg2, ewdR[e, fm, :, :])
            P.wait("tensor", ewl[p2], wl)
            P.wait("tensor", dve, he_done)
            for r in range(c.NCAP):
                bank = PB[2 + (r % 2)]
                P.wait("tensor", dve, E_state["bankcp"][r % 2])
                for gc in range(c.NIC):
                    matmul(bank[:, :],
                           he[:, gc * c.CAP + r * 128:gc * c.CAP + r * 128 + 128],
                           wstg2[:, gc * 512:(gc + 1) * 512],
                           start=(gc == 0), stop=(gc == c.NIC - 1),
                           inc=mm if gc == c.NIC - 1 else None)
                P.wait("vector", mm)
                ep = (fm * c.NCAP + r) % 2
                etkb = ar(O_ETK + ep * 512, 512)
                P.wait("vector", etk[ep])
                sc_dve = vop("tensor_copy", etkb, bank[:, :])
                E_state["bankcp"][r % 2] = sc_dve
                P.wait("gpsimd", dve, sc_dve)
                P.wait("gpsimd", etk[0])
                P.wait("gpsimd", etk[1])
                P.op("gpsimd",
                     lambda E, etkb=etkb, r=r, fm=fm, par=par:
                     E.indirect_dma_start(
                         out=out[:, :],
                         out_offset=IndirectOffsetOnAxis(
                             eidx[:, par * c.NCAP + r:par * c.NCAP + r + 1], 0),
                         in_=etkb, in_offset=None,
                         element_offset=fm * 512,
                         compute_op=ALU.add), inc=etk[ep])
            E_state["ewd_free"][p2] = mm.n
        E_state["scat_prev0"] = etk[0].n
        E_state["scat_prev1"] = etk[1].n
        E_state["he_free"] = mm.n

    E_state = {"nldE": 0, "ewd_free": [0, 0], "scat_prev0": 0, "scat_prev1": 0,
               "he_free": 0, "bankcp": [0, 0], "eugsplit_a": [0, 0],
               "eugsplit_d": [0, 0]}
    gat_dve_done = [0] * (c.E + 1)
    for e in range(c.E + 1):
        if e < c.E:
            emit_prep(e)
            emit_gather(e)
            emit_transposes(e)
            gat_dve_done[e] = dve.n
        if e >= 1:
            emit_compute(e - 1, gat_dve_done[e - 1])

    P.wait("sync", swe)
    P.wait("sync", etk[0])
    P.wait("sync", etk[1])
    P.wait("sync", swa[0])
    P.wait("sync", swa[1])
    P.wait("sync", sst)
    P.finalize()
    return nc


def _rope_tables(S):
    try:
        import jax
        import jax.numpy as jnp
        cpu = jax.devices("cpu")[0]
        with jax.default_device(cpu):
            pos = jnp.arange(S, dtype=jnp.float32)[:, None]
            inv = jnp.exp(jnp.arange(0, 64, 2, dtype=jnp.float32)
                          * (-math.log(10000.0) / 64))
            ang = pos * inv[None, :]
            cos = np.asarray(jnp.cos(ang))
            sin = np.asarray(jnp.sin(ang))
    except Exception:
        pos = np.arange(S, dtype=np.float32)[:, None]
        inv = np.exp(np.arange(0, 64, 2, dtype=np.float32)
                     * np.float32(-math.log(10000.0) / 64)).astype(np.float32)
        ang = (pos * inv[None, :]).astype(np.float32)
        cos = np.cos(ang.astype(np.float64)).astype(np.float32)
        sin = np.sin(ang.astype(np.float64)).astype(np.float32)
    cosx = np.repeat(cos.T, 2, axis=0).astype(np.float32)
    sinx = np.empty((64, S), np.float32)
    sinx[0::2] = -sin.T
    sinx[1::2] = sin.T
    return np.ascontiguousarray(cosx), np.ascontiguousarray(sinx)


def make_inputs(c: CFG, core, x, wq, wk, wv, wo, router, w_ug, w_dn, s_ug, s_dn):
    import ml_dtypes
    b, qi = core // 4, core % 4
    own = ((np.arange(c.NM)[:, None] * 4 + qi) * 128
           + np.arange(128)[None, :]).reshape(-1)
    cosx, sinx = _rope_tables(c.S)
    qc = np.ascontiguousarray(cosx[:, own])
    qs = np.ascontiguousarray(sinx[:, own])
    m = np.zeros((16, 128, 512), np.float32)
    kk = np.arange(128)[:, None]
    jj = np.arange(128)[None, :]
    for r in range(16):
        for s in range(4):
            m[r][:, s * 128:(s + 1) * 128] = (
                (4 * s + qi - r) * 128 + jj - kk >= 0).astype(np.float32)
    ident = np.eye(128, dtype=np.float32)
    triu = np.triu(np.ones((128, 128), np.float32), 1)
    iota8 = np.broadcast_to(np.arange(c.E, dtype=np.float32)[None, :],
                            (128, c.E)).copy()
    itok = (np.arange(c.NM)[None, :] * 128
            + np.arange(128)[:, None]).astype(np.float32)
    cones = np.ones((128, 1), np.float32)
    onesrv = np.ones((128, 128), np.float32)
    ind64 = np.zeros((128, 2), np.float32)
    ind64[0:64, 0] = 1.0
    ind64[64:128, 1] = 1.0
    NC, NF, NJC, NIC, NDB, E, NTC = (c.NC, c.NF, c.NJC, c.NIC, c.NDB, c.E,
                                     c.NTC)
    NHG = NJC // c.HGRP
    xbarr = x[b].astype(np.float32)
    xbRv = np.ascontiguousarray(
        xbarr.reshape(NTC, 512, NC, 128).transpose(0, 3, 2, 1)
        .reshape(NTC, 128, NC * 512))
    xoarr = xbarr[own]
    xoRv = np.ascontiguousarray(
        xoarr.reshape(c.T // 512, 512, NC, 128).transpose(0, 3, 2, 1)
        .reshape(c.T // 512, 128, NC * 512))
    wqRv = np.ascontiguousarray(
        wq.reshape(NC, 128, NC, 128).transpose(0, 3, 2, 1)
        .reshape(NC, 128, NC * 128))
    woRv = np.ascontiguousarray(
        wo.reshape(NC, 128, NC, 128).transpose(0, 3, 2, 1)
        .reshape(NC, 128, NC * 128))
    wkRv = np.ascontiguousarray(
        wk.reshape(NF, 128, NC, 128).transpose(3, 2, 0, 1)
        .reshape(128, NC * NF * 128))
    wvRv = np.ascontiguousarray(
        wv.reshape(NF, 128, NC, 128).transpose(3, 2, 0, 1)
        .reshape(128, NC * NF * 128))
    routerRv = np.ascontiguousarray(
        router.reshape(NC, 128, E).transpose(1, 0, 2).reshape(128, NC * E))
    sugRv = np.ascontiguousarray(
        s_ug.reshape(NC, 128, 2 * NJC, 128).transpose(2, 1, 0, 3)
        .reshape(2 * NJC, 128, NC * 128))
    swdRv = np.ascontiguousarray(
        s_dn.reshape(NHG, c.HGRP, 128, NDB, 512).transpose(0, 3, 2, 1, 4)
        .reshape(NHG, NDB, 128, c.HGRP * 512))
    eugRv = np.ascontiguousarray(
        w_ug.reshape(E, NC, 128, 2 * NIC, 128).transpose(0, 3, 2, 1, 4)
        .reshape(E, 2 * NIC, 128, NC * 128))
    ewdRv = np.ascontiguousarray(
        w_dn.reshape(E, NIC, 128, NDB, 512).transpose(0, 3, 2, 1, 4)
        .reshape(E, NDB, 128, NIC * 512))
    return {
        "xbR": xbRv, "xoR": xoRv,
        "wqR": wqRv, "wkR": wkRv, "wvR": wvRv, "woR": woRv,
        "routerR": routerRv,
        "sugR": sugRv, "swdR": swdRv, "eugR": eugRv, "ewdR": ewdRv,
        "kcos": cosx, "ksin": sinx, "qcost": qc, "qsint": qs,
        "masks16": m.astype(ml_dtypes.bfloat16),
        "identin": ident, "triuin": triu, "iota8in": iota8, "itokin": itok,
        "conesin": cones, "onesrin": onesrv, "ind64in": ind64,
        "identbfin": ident.astype(ml_dtypes.bfloat16),
    }


_CFG = CFG()


def kernel(L0_attn_x, L0_attn_wq, L0_attn_wk, L0_attn_wv, L0_attn_wo,
           L0_ffn_router, L0_ffn_w_up_gate, L0_ffn_w_down,
           L0_ffn_shared_w_up_gate, L0_ffn_shared_w_down):
    c = _CFG
    x = np.asarray(L0_attn_x, np.float32)
    nc = build_program(c)
    in_maps = [make_inputs(c, core, x,
                           np.asarray(L0_attn_wq, np.float32),
                           np.asarray(L0_attn_wk, np.float32),
                           np.asarray(L0_attn_wv, np.float32),
                           np.asarray(L0_attn_wo, np.float32),
                           np.asarray(L0_ffn_router, np.float32),
                           np.asarray(L0_ffn_w_up_gate, np.float32),
                           np.asarray(L0_ffn_w_down, np.float32),
                           np.asarray(L0_ffn_shared_w_up_gate, np.float32),
                           np.asarray(L0_ffn_shared_w_down, np.float32))
               for core in range(8)]
    res = run_bass_kernel_spmd(nc, in_maps, core_ids=list(range(8)))
    outp = np.empty((2, c.S, c.D), np.float32)
    for core in range(8):
        b, qi = core // 4, core % 4
        o = np.asarray(res.results[core]["out"])[:c.T].reshape(c.NM, 128, c.D)
        for mt in range(c.NM):
            g = 4 * mt + qi
            outp[b, g * 128:(g + 1) * 128, :] = o[mt]
    return outp


# version 23
# speedup vs baseline: 1.0785x; 1.0180x over previous
import math
import numpy as np

import concourse.bass as bass
import concourse.mybir as mybir
from concourse.bass_utils import run_bass_kernel_spmd
from concourse.bass_core_rust import IndirectOffsetOnAxis

F32 = mybir.dt.float32
I32 = mybir.dt.int32
BF16 = mybir.dt.bfloat16
AF = mybir.ActivationFunctionType
ALU = mybir.AluOpType
AX = mybir.AxisListType
SWAPMASK = [i ^ 1 for i in range(32)]


class CFG:
    def __init__(self, D=2048, S=4096, H=32, HV=8, JH=5632, IH=1408, E=8, CAP=384):
        self.D, self.S, self.H, self.HV = D, S, H, HV
        self.JH, self.IH, self.E, self.CAP = JH, IH, E, CAP
        self.T = S // 4
        self.NM = self.T // 128
        self.NC = D // 128
        self.NKT = S // 128
        self.NTC = S // 512
        self.NG = self.NM // 4
        self.NF = HV * 64 // 128
        self.NCAP = CAP // 128
        self.NJC = JH // 128
        self.NIC = IH // 128
        self.HGRP = 4
        self.NDB = D // 512
        assert H == 4 * HV and H * 64 == D
        assert D % 512 == 0 and S % 512 == 0 and self.T % 512 == 0
        assert self.NJC % self.HGRP == 0 and IH % 128 == 0
        assert CAP % 128 == 0 and self.NM % 4 == 0 and HV * 64 % 128 == 0


class Chan:
    __slots__ = ("sem", "step", "n")

    def __init__(self, sem, step):
        self.sem, self.step, self.n = sem, step, 0


class Prog:
    ENGS = ("sync", "tensor", "vector", "scalar", "gpsimd")

    def __init__(self, nc):
        self.nc = nc
        self.q = {e: [] for e in self.ENGS}
        self.chans = {}
        self.seen = {e: {} for e in self.ENGS}
        self.nojoin = set()

    def chan(self, name, step=1):
        if name not in self.chans:
            self.chans[name] = Chan(self.nc.alloc_semaphore(name), step)
        return self.chans[name]

    def wait(self, eng, chan, val=None):
        v = chan.n if val is None else val
        if v <= 0:
            return
        seen = self.seen[eng]
        if seen.get(chan.sem.num, 0) >= v:
            return
        seen[chan.sem.num] = v
        sem = chan.sem
        self.q[eng].append(lambda E: E.wait_ge(sem, v))

    def op(self, eng, fn, inc=None, step=None):
        ss = getattr(self, "selfserial", None)
        if ss and eng in ss and inc is ss[eng]:
            self.wait(eng, inc, inc.n)
        if inc is not None:
            add = step if step is not None else inc.step
            inc.n += add
            sem = inc.sem
            self.q[eng].append(lambda E: fn(E).then_inc(sem, add))
            return inc.n
        self.q[eng].append(fn)
        return None

    def join(self):
        snap = [(ch.sem, ch.n) for ch in self.chans.values()
                if ch.n > 0 and ch not in self.nojoin]
        for e in self.ENGS:
            seen = self.seen[e]
            todo = [(s, v) for s, v in snap if seen.get(s.num, 0) < v]
            for s, v in todo:
                seen[s.num] = v
            if todo:
                def f(E, todo=todo):
                    for s, v in todo:
                        E.wait_ge(s, v)
                self.q[e].append(f)

    def finalize(self):
        with self.nc.Block() as block:
            for ename, dec in (("sync", block.sync), ("tensor", block.tensor),
                               ("vector", block.vector), ("scalar", block.scalar),
                               ("gpsimd", block.gpsimd)):
                fns = self.q[ename]
                def body(E, fns=fns):
                    for f in fns:
                        f(E)
                dec(body)


def build_program(c: CFG, upto="E", dbg_pairs=False):
    nc = bass.Bass("TRN2", target_bir_lowering=False, debug=False)
    din = lambda n, sh, d=F32: nc.dram_tensor(n, sh, d, kind="ExternalInput").ap()
    dint = lambda n, sh, d=F32: nc.dram_tensor(n, sh, d, kind="Internal").ap()

    NHG = c.NJC // c.HGRP
    xbR = din("xbR", [c.NTC, 128, c.NC * 512])
    xoR = din("xoR", [c.T // 512, 128, c.NC * 512])
    wqR = din("wqR", [c.NC, 128, c.NC * 128])
    wkR = din("wkR", [128, c.NC * c.NF * 128])
    wvR = din("wvR", [128, c.NC * c.NF * 128])
    woR = din("woR", [c.NC, 128, c.NC * 128])
    routerR = din("routerR", [128, c.NC * c.E])
    sugR = din("sugR", [2 * c.NJC, 128, c.NC * 128])
    swdR = din("swdR", [NHG, c.NDB, 128, c.HGRP * 512])
    eugR = din("eugR", [c.E, 2 * c.NIC, 128, c.NC * 128])
    ewdR = din("ewdR", [c.E, c.NDB, 128, c.NIC * 512])
    kcos = din("kcos", [64, c.S])
    ksin = din("ksin", [64, c.S])
    qcost = din("qcost", [64, c.T])
    qsint = din("qsint", [64, c.T])
    masks16 = din("masks16", [16, 128, 512], BF16)
    identin = din("identin", [128, 128])
    triuin = din("triuin", [128, 128])
    iota8in = din("iota8in", [128, c.E])
    itokin = din("itokin", [128, c.NM])
    conesin = din("conesin", [128, 1])
    onesrin = din("onesrin", [128, 128])
    ind64in = din("ind64in", [128, 2])
    identbfin = din("identbfin", [128, 128], BF16)

    out = nc.dram_tensor("out", [c.T + 1, c.D], F32, kind="ExternalOutput").ap()

    kspill = dint("kspill", [c.HV * 64, c.S])
    vspill = dint("vspill", [c.HV * 64, c.S])
    h2tok = dint("h2tok", [c.T + 1, c.D])
    NRANK = c.E * c.CAP + 128
    ranktok = dint("ranktok", [NRANK, 1])
    rankcw = dint("rankcw", [NRANK, 1])
    dpr = ((lambda n, sh, d: nc.dram_tensor(n, sh, d,
                                            kind="ExternalOutput").ap())
           if dbg_pairs else dint)
    sug_h = dpr("sug_h", [2 * c.NJC, 128, c.NC * 128], BF16)
    sug_l = dpr("sug_l", [2 * c.NJC, 128, c.NC * 128], BF16)

    P = Prog(nc)
    mm = P.chan("mm")
    act = P.chan("act")
    dve = P.chan("dve")
    gpc = P.chan("gpc")
    ldc = P.chan("ldc", 16)
    ldw = P.chan("ldw", 16)
    ldx = [P.chan("ldx0", 16), P.chan("ldx1", 16)]
    rp = P.chan("rp", 16)
    wst = [P.chan("wst0", 16), P.chan("wst1", 16)]
    wsd = [P.chan("wsd0", 16), P.chan("wsd1", 16)]
    lrt = P.chan("lrt", 16)
    kvl = P.chan("kvl", 16)
    kvv = P.chan("kvv", 16)
    lxr = [P.chan("lxr0", 16), P.chan("lxr1", 16)]
    spl = P.chan("spl", 16)
    sst = P.chan("sst", 16)
    ler = P.chan("ler", 16)
    lcw = P.chan("lcw", 16)
    swr = P.chan("swr", 16)
    swa = [P.chan("swa0", 16), P.chan("swa1", 16)]
    swe = P.chan("swe", 16)
    ewl = [P.chan("ewl0", 16), P.chan("ewl1", 16)]
    etk = [P.chan("etk0", 16), P.chan("etk1", 16)]
    tstc = [P.chan("tst0", 16), P.chan("tst1", 16)]
    plf = P.chan("plf", 16)
    pst = P.chan("pst", 16)
    psp = P.chan("psp")
    P.nojoin = {plf, pst, psp}
    P.selfserial = {"vector": dve, "scalar": act, "gpsimd": gpc}

    AR_COLS = 43648
    AR = nc.alloc_sbuf_tensor("AR", [128, AR_COLS], F32)
    R_A, R_B, R_W = 0, 16384, 32768
    RWE = AR_COLS - R_W

    def ar(off, cols, rows=128):
        assert off + cols <= AR_COLS, (off, cols)
        return AR[0:rows, off:off + cols]

    ARB = AR[:, :].bitcast(BF16)

    def arb(off_eq, cols_bf, rows=128):
        assert 2 * off_eq + cols_bf <= 2 * AR_COLS, (off_eq, cols_bf)
        return ARB[0:rows, 2 * off_eq:2 * off_eq + cols_bf]

    Al = lambda n, sh, d=F32: nc.alloc_sbuf_tensor(n, sh, d)
    sident = Al("sident", [128, 128])
    sidbf = Al("sidbf", [128, 128], BF16)
    striu = Al("striu", [128, 128])
    siota8 = Al("siota8", [128, c.E])
    sitok = Al("sitok", [128, c.NM])
    scones = Al("scones", [128, 1])
    onesr = Al("onesr", [128, 128])
    sind64 = Al("sind64", [128, 2])

    smask = Al("smask", [128, 16 * 512], BF16)
    kmrun = Al("kmrun", [1, c.HV])
    qmrun = Al("qmrun", [1, c.H])
    negrow = Al("negrow", [1, c.H])
    negBb = Al("negBb", [128, c.H])
    srowA = Al("srowA", [1, 512])
    srowB = Al("srowB", [1, 512])
    srowC = Al("srowC", [1, 512])
    s11 = Al("s11", [1, 1])
    sBC1 = Al("sBC1", [128, 512])
    sBC2 = Al("sBC2", [128, 512])
    sq = Al("sq", [128, 512])
    t1s = Al("t1s", [128, 512])
    slog = Al("slog", [128, c.E])
    sr_is1 = Al("sr_is1", [128, c.E])
    sr_is2 = Al("sr_is2", [128, c.E])
    sr_msk = Al("sr_msk", [128, c.E])
    sr_pos = Al("sr_pos", [128, c.E])
    sr_obc = Al("sr_obc", [128, c.E])
    sr_m1 = Al("sr_m1", [128, 1])
    sr_m2 = Al("sr_m2", [128, 1])
    sr_d = Al("sr_d", [128, 1])
    sr_w1 = Al("sr_w1", [128, 1])
    sr_w2 = Al("sr_w2", [128, 1])
    sr_i1 = Al("sr_i1", [128, 1])
    sr_i2 = Al("sr_i2", [128, 1])
    sr_p1 = Al("sr_p1", [128, 1])
    sr_p2 = Al("sr_p2", [128, 1])
    sr_t = Al("sr_t", [128, c.E])
    sr_s = Al("sr_s", [128, 1])
    srow8 = Al("srow8", [1, (c.NM + 1) * c.E])
    sd1i = Al("sd1i", [128, 2], I32)
    eidx = Al("eidx", [128, 2 * c.NCAP], I32)

    ZT = (NRANK + 127) // 128
    zrf = Al("zrf", [128, ZT])
    ztrf = Al("ztrf", [128, ZT])
    plsc = Al("plsc", [128, 512])
    plscb = plsc[:, :].bitcast(BF16)
    pl_f32 = plsc[:, 0:256]
    pl_hi = plscb[:, 512:768]
    pl_lo = plscb[:, 768:1024]
    erow = srowA[0:1, 0:128]
    cwrow = srowB[0:1, 0:c.CAP] if c.CAP <= 512 else None

    PB = [nc.alloc_psum_tensor("pb%d" % i, [128, 512], F32) for i in range(8)]

    def dma(eng, chan, o, i, **kw):
        return P.op(eng, lambda E: E.dma_start(out=o, in_=i, **kw), inc=chan)

    def ld_cm(eng, chan, dst2d, src, fsel=None):
        s3 = src.rearrange("(n p) f -> p n f", p=128)
        if fsel is not None:
            s3 = s3[:, :, fsel[0]:fsel[1]]
        f = s3.shape[2]
        d3 = dst2d.rearrange("p (n f) -> p n f", f=f)
        return dma(eng, chan, d3, s3)

    def matmul(o, l, r, start, stop, inc=None):
        return P.op("tensor",
                    lambda E: E.matmul(o, l, r, start=start, stop=stop), inc=inc)

    def vop(name, *aps, inc=dve):
        return P.op("vector", lambda E: getattr(E, name)(*aps), inc=inc)

    for o, i in ((sident[:, :], identin), (striu[:, :], triuin),
                 (siota8[:, :], iota8in), (sitok[:, :], itokin),
                 (scones[:, :], conesin), (sind64[:, :], ind64in),
                 (onesr[:, :], onesrin), (sidbf[:, :], identbfin)):
        dma("sync", ldc, o, i[:, :])
    dma("sync", ldc, smask[:, :].rearrange("p (r q) -> p r q", q=512),
        masks16.rearrange("r p q -> p r q"))
    ld_consts = ldc.n

    P.op("gpsimd", lambda E: E.memset(zrf[:, :], 0.0), inc=gpc)
    P.op("gpsimd", lambda E: E.memset(ztrf[:, :], float(c.T)), inc=gpc)
    P.op("gpsimd", lambda E: E.memset(kmrun[:, :], 0.0), inc=gpc)
    P.op("gpsimd", lambda E: E.memset(qmrun[:, :], 0.0), inc=gpc)
    init_zeros = gpc.n
    P.wait("gpsimd", gpc, init_zeros)
    dma("gpsimd", swr, ranktok.rearrange("(p z) one -> p (z one)", p=128),
        ztrf[:, :])
    dma("gpsimd", swr, rankcw.rearrange("(p z) one -> p (z one)", p=128),
        zrf[:, :])
    ranks_zeroed = swr.n
    P.op("gpsimd",
         lambda E: E.memset(bass.AP(AR, R_W + 64,
                                    [[AR_COLS, 128], [65, c.NKT]]), 1.0),
         inc=gpc)
    v65_ones = gpc.n

    PSLAB = 256
    assert (c.NC * 128) % PSLAB == 0
    nsub = (c.NC * 128) // PSLAB
    pr_state = [0, 0]

    def emit_pr_slab(src2, dsth2, dstl2):
        for sb in range(nsub):
            cs = slice(sb * PSLAB, (sb + 1) * PSLAB)
            pld = dma("gpsimd", plf, pl_f32, src2[:, cs])
            P.wait("gpsimd", plf, pld)
            P.wait("gpsimd", pst, pr_state[1])
            P.op("gpsimd", lambda E: E.tensor_copy(pl_hi, pl_f32), inc=psp)
            P.wait("gpsimd", psp)
            P.op("gpsimd", lambda E: E.tensor_sub(pl_lo, pl_f32, pl_hi),
                 inc=psp)
            pr_state[0] = psp.n
            P.wait("gpsimd", psp, pr_state[0])
            dma("gpsimd", pst, dsth2[:, cs], pl_hi)
            pr_state[1] = dma("gpsimd", pst, dstl2[:, cs], pl_lo)

    for blk in range(2 * c.NJC):
        emit_pr_slab(sugR[blk, :, :], sug_h[blk, :, :], sug_l[blk, :, :])
    pr_sug = pst.n

    O_V65 = R_W
    O_S1 = R_W + c.NKT * 65
    NKV = c.NF * 512

    w_k = ar(R_B, c.NC * c.HV * 64)
    w_v = ar(R_B + c.NC * c.HV * 64, c.NC * c.HV * 64)
    dma("sync", ldw, w_k, wkR[:, :])
    dma("sync", ldw, w_v, wvR[:, :])
    ld_wkv = ldw.n

    O_KST = O_S1
    O_VST = O_KST + NKV
    O_RC = O_VST + NKV
    O_RS = O_RC + 512
    O_RSH = O_RS + 512
    assert O_RSH + 512 <= AR_COLS
    kst = ar(O_KST, NKV)
    vst = ar(O_VST, NKV)
    rcs = ar(O_RC, 512)
    rsn = ar(O_RS, 512)

    def emit_ln(tile_of_n, pair=None):
        P.wait("tensor", act)
        P.wait("tensor", dve)
        for n in range(c.NC):
            matmul(PB[0][0:1, 0:512], scones[:, :], tile_of_n(n),
                   start=(n == 0), stop=(n == c.NC - 1),
                   inc=mm if n == c.NC - 1 else None)
        sq_mm = [0, 0]
        for n in range(c.NC):
            buf = (sq, t1s)[n % 2]
            P.wait("vector", mm, sq_mm[n % 2])
            vop("tensor_mul", buf[:, :], tile_of_n(n), tile_of_n(n))
            P.wait("tensor", dve)
            sq_mm[n % 2] = matmul(PB[1][0:1, 0:512], scones[:, :], buf[:, :],
                                  start=(n == 0), stop=(n == c.NC - 1), inc=mm)
        P.wait("vector", mm)
        vop("tensor_single_scalar", srowA[:, :], PB[0][0:1, :], 1.0 / c.D,
            ALU.mult)
        vop("tensor_single_scalar", srowB[:, :], PB[1][0:1, :], 1.0 / c.D,
            ALU.mult)
        vop("tensor_mul", srowC[:, :], srowA[:, :], srowA[:, :])
        vop("tensor_sub", srowB[:, :], srowB[:, :], srowC[:, :])
        vop("tensor_scalar_add", srowB[:, :], srowB[:, :], 1e-5)
        P.wait("scalar", dve)
        P.op("scalar", lambda E: E.activation(srowB[:, :], srowB[:, :],
                                              AF.Sqrt, bias=0.0, scale=1.0),
             inc=act)
        P.wait("vector", act)
        vop("reciprocal", srowB[:, :], srowB[:, :])
        vop("tensor_mul", srowC[:, :], srowA[:, :], srowB[:, :])
        vop("tensor_scalar_mul", srowC[:, :], srowC[:, :], -1.0)
        P.wait("tensor", dve)
        matmul(PB[2][:, :], onesr[0:1, :], srowB[:, :], True, True, inc=mm)
        matmul(PB[3][:, :], onesr[0:1, :], srowC[:, :], True, True, inc=mm)
        P.wait("vector", mm)
        for n in range(c.NC):
            vop("tensor_mul", t1s[:, :], tile_of_n(n), PB[2][:, :])
            if pair is None:
                vop("tensor_add", tile_of_n(n), t1s[:, :], PB[3][:, :])
            else:
                hi_n, lo_n = pair(n)
                vop("tensor_add", sq[:, :], t1s[:, :], PB[3][:, :])
                vop("tensor_copy", hi_n, sq[:, :])
                vop("tensor_sub", lo_n, sq[:, :], hi_n)

    def emit_rope(buf, col0, tcs=None, tsn=None, tcol=0):
        rsh = ar(O_RSH, 512)
        tcs = rcs if tcs is None else tcs
        tsn = rsn if tsn is None else tsn
        for half in range(2):
            hh = slice(half * 64, half * 64 + 64)
            src = buf[hh, col0:col0 + 512]
            vop("stream_shuffle", rsh[hh, :], src, SWAPMASK)
            vop("tensor_mul", rsh[hh, :], rsh[hh, :], tsn[hh, tcol:tcol + 512])
            vop("tensor_mul", src, src, tcs[hh, tcol:tcol + 512])
            vop("tensor_add", src, src, rsh[hh, :])

    norm_st = {"dve": 0, "mm": 0}

    def emit_norm_accum(buf, col0, run, col2):
        P.wait("vector", mm, norm_st["mm"])
        vop("tensor_mul", sq[:, :], buf[:, col0:col0 + 512],
            buf[:, col0:col0 + 512])
        P.wait("tensor", dve)
        P.wait("tensor", dve, norm_st["dve"])
        for half in range(2):
            matmul(PB[4 + half][0:1, :], sind64[:, half:half + 1], sq[:, :],
                   True, True, inc=mm)
        norm_st["mm"] = mm.n
        P.wait("vector", mm, norm_st["mm"])
        for half in range(2):
            vop("reduce_max", s11[:, :], PB[4 + half][0:1, :], AX.X)
            cc = col2 + half
            vop("tensor_max", run[0:1, cc:cc + 1], run[0:1, cc:cc + 1],
                s11[:, :])
        norm_st["dve"] = dve.n

    CNC = c.NC * 512
    xfree = [(0, 0), (0, 0)]
    prev_spill = 0
    for tc in range(c.NTC):
        par = tc % 2
        xbuf = ar(R_A + par * CNC, CNC)
        P.wait("sync", dve, xfree[par][0])
        P.wait("sync", mm, xfree[par][1])
        xld = dma("sync", ldx[par], xbuf, xbR[tc, :, :])
        P.wait("sync", dve)
        rld = dma("sync", rp, rcs[0:64, :], kcos[:, tc * 512:(tc + 1) * 512])
        rld = dma("sync", rp, rcs[64:128, :], kcos[:, tc * 512:(tc + 1) * 512])
        rld = dma("sync", rp, rsn[0:64, :], ksin[:, tc * 512:(tc + 1) * 512])
        rld = dma("sync", rp, rsn[64:128, :], ksin[:, tc * 512:(tc + 1) * 512])
        if tc == 0:
            P.wait("tensor", ldc, ld_consts)
            P.wait("vector", gpc, init_zeros)
        P.wait("tensor", ldx[par], xld)
        P.wait("vector", ldx[par], xld)
        emit_ln(lambda n, xbuf=xbuf: xbuf[:, n * 512:(n + 1) * 512])
        h_done = dve.n
        P.wait("tensor", ldw, ld_wkv)
        P.wait("tensor", dve, h_done)
        kvgroups = ([(w_k, kst, fo) for fo in range(c.NF)]
                    + [(w_v, vst, fo) for fo in range(c.NF)])
        for gi, (wres, stg, fo) in enumerate(kvgroups):
            bank = PB[2 + (gi % 2)]
            for n in range(c.NC):
                lt = wres[:, (n * c.NF + fo) * 128:(n * c.NF + fo) * 128 + 128]
                matmul(bank[:, :], lt, xbuf[:, n * 512:(n + 1) * 512],
                       start=(n == 0), stop=(n == c.NC - 1),
                       inc=mm if n == c.NC - 1 else None)
            P.wait("vector", mm)
            P.wait("vector", spl, prev_spill)
            vop("tensor_copy", stg[:, fo * 512:(fo + 1) * 512], bank[:, :])
            P.wait("tensor", dve)
            if 1 <= gi <= c.NF:
                kfo = gi - 1
                if gi == 1:
                    P.wait("vector", rp, rld)
                emit_rope(kst, kfo * 512)
                emit_norm_accum(kst, kfo * 512, kmrun, 2 * kfo)
        kv_mm = mm.n
        P.wait("scalar", dve)
        dma("scalar", spl,
            kspill.rearrange("(f p) s -> p f s", p=128)
            [:, :, tc * 512:(tc + 1) * 512],
            kst.rearrange("p (f s) -> p f s", s=512))
        prev_spill = dma(
            "scalar", spl,
            vspill.rearrange("(f p) s -> p f s", p=128)
            [:, :, tc * 512:(tc + 1) * 512],
            vst.rearrange("p (f s) -> p f s", s=512))
        xfree[par] = (h_done, kv_mm)
    a_spills = spl.n
    P.join()

    Qr = ar(R_A, c.NC * c.T)
    for t2 in range(c.T // 512):
        xbuf = ar(R_B + t2 * CNC, CNC)
        xld = dma("sync", ldx[t2 % 2], xbuf, xoR[t2, :, :])
        P.wait("tensor", ldx[t2 % 2], xld)
        P.wait("vector", ldx[t2 % 2], xld)
        emit_ln(lambda n, t2=t2: ar(R_B + t2 * CNC, CNC)
                [:, n * 512:(n + 1) * 512])
    h_own = dve.n

    O_WQ = O_S1
    WQC = c.NC * 128
    O_QRC = O_RSH + 512
    qrc = ar(O_QRC, c.T)
    qrs = ar(O_QRC + c.T, c.T)
    rq1 = dma("sync", rp, qrc[0:64, :], qcost[:, :])
    rq1 = dma("sync", rp, qrc[64:128, :], qcost[:, :])
    rq1 = dma("sync", rp, qrs[0:64, :], qsint[:, :])
    rq1 = dma("sync", rp, qrs[64:128, :], qsint[:, :])

    def emit_qtail(fo):
        for t2 in range(c.T // 512):
            col0 = fo * c.T + t2 * 512
            emit_rope(Qr, col0, qrc, qrs, t2 * 512)
            emit_norm_accum(Qr, col0, qmrun, 2 * fo)

    wqfree = [0, 0]
    for fo in range(c.NC):
        par = fo % 2
        wstg = ar(O_WQ + par * WQC, WQC)
        P.wait("sync", mm, wqfree[par])
        wld = dma("sync", wst[par], wstg, wqR[fo, :, :])
        P.wait("tensor", wst[par], wld)
        P.wait("tensor", dve, h_own)
        for t2 in range(c.T // 512):
            xbuf = ar(R_B + t2 * CNC, CNC)
            bank = PB[2 + (t2 % 2)]
            for n in range(c.NC):
                matmul(bank[:, :], wstg[:, n * 128:(n + 1) * 128],
                       xbuf[:, n * 512:(n + 1) * 512],
                       start=(n == 0), stop=(n == c.NC - 1),
                       inc=mm if n == c.NC - 1 else None)
            P.wait("vector", mm)
            if fo == 0 and t2 == 0:
                P.wait("vector", rp, rq1)
            vop("tensor_copy",
                Qr[:, fo * c.T + t2 * 512:fo * c.T + t2 * 512 + 512],
                bank[:, :])
            P.wait("tensor", dve)
        wqfree[par] = mm.n
        if fo > 0:
            emit_qtail(fo - 1)
    emit_qtail(c.NC - 1)
    P.wait("scalar", dve)
    P.op("scalar", lambda E: E.activation(qmrun[:, :], qmrun[:, :], AF.Sqrt,
                                          bias=0.0, scale=1.0), inc=act)
    P.op("scalar", lambda E: E.activation(kmrun[:, :], kmrun[:, :], AF.Sqrt,
                                          bias=0.0, scale=1.0), inc=act)
    P.wait("vector", act)
    for k in range(4):
        vop("tensor_copy", bass.AP(negrow, k, [[c.H, 1], [4, c.HV]]),
            kmrun[0:1, 0:c.HV])
    vop("tensor_mul", negrow[:, :], negrow[:, :], qmrun[:, :])
    vop("tensor_scalar_mul", negrow[:, :], negrow[:, :], -0.125)
    P.wait("tensor", dve)
    matmul(PB[5][0:128, 0:c.H], onesr[0:1, :], negrow[:, :], True, True,
           inc=mm)
    P.wait("vector", mm)
    vop("tensor_copy", negBb[:, :], PB[5][0:128, 0:c.H])
    P.join()

    if upto == "A2":
        dbg = nc.dram_tensor("dbg", [128, c.NC * c.T], F32,
                             kind="ExternalOutput").ap()
        dma("sync", sst, dbg[:, :], Qr)
        P.wait("sync", sst)
        P.finalize()
        return nc

    oT = ar(R_B, c.NC * c.T)
    v65c = ar(O_V65, c.NKT * 65)
    O_KB = O_S1
    O_VB = O_KB + c.S
    O_PT = O_VB + c.S
    assert O_PT + 512 <= AR_COLS
    kb = ar(O_KB, c.S)
    vb = ar(O_VB, c.S, rows=64)
    pt = [ar(O_PT, 512), sBC2[:, :]]
    lb = sBC1[0:64, 0:512]
    ostp = [P.chan("ostp0", 16), P.chan("ostp1", 16)]
    ost_n = [0]
    kvfree = (0, 0)
    for hv in range(c.HV):
        P.wait("sync", mm, kvfree[0])
        P.wait("sync", dve, kvfree[1])
        if hv == 0:
            P.wait("sync", spl, a_spills)
            P.wait("scalar", spl, a_spills)
        l0 = dma("sync", kvl, kb[0:64, :], kspill[hv * 64:hv * 64 + 64, :])
        l0 = dma("sync", kvl, kb[64:128, :], kspill[hv * 64:hv * 64 + 64, :])
        P.wait("scalar", mm, kvfree[0])
        l1 = dma("scalar", kvv, vb[0:64, :], vspill[hv * 64:hv * 64 + 64, :])
        if hv == 0:
            P.wait("tensor", gpc, v65_ones)
        P.wait("tensor", kvv, l1)
        for kt in range(c.NKT):
            bank = PB[3 + (kt % 2)]
            P.op("tensor", lambda E, kt=kt, bank=bank: E.transpose(
                bank[:, 0:64], vb[0:64, kt * 128:(kt + 1) * 128],
                sident[0:64, 0:64]), inc=mm)
            P.wait("vector", mm)
            vop("tensor_copy", v65c[:, kt * 65:kt * 65 + 64], bank[:, 0:64])
            P.wait("tensor", dve)
        P.wait("tensor", kvl, l0)
        SBa = (PB[0], PB[1])
        SBb = (PB[5], PB[6])
        PTa = (pt[0], pt[1])
        PTb = (ar(O_VB, 512), ar(O_VB + 512, 512))
        DPTH = 2
        for hp in range(2):
            a = hv * 4 + 2 * hp
            fo = a // 2
            qc0 = fo * c.T
            for g in range(c.NG):
                nkt = 16 * g + 16
                qs = qc0 + g * 512
                exp_t = [[0, 0], [0, 0]]
                msk_t = [[0, 0], [0, 0]]
                use_t = [[0, 0], [0, 0]]

                def emit_S(kt):
                    sp = kt % 2
                    P.wait("tensor", act, exp_t[0][sp])
                    P.wait("tensor", act, exp_t[1][sp])
                    ta = matmul(SBa[sp][:, :],
                                kb[0:64, kt * 128:(kt + 1) * 128],
                                Qr[0:64, qs:qs + 512], True, True, inc=mm)
                    tb = matmul(SBb[sp][:, :],
                                kb[64:128, kt * 128:(kt + 1) * 128],
                                Qr[64:128, qs:qs + 512], True, True, inc=mm)
                    return tb

                def emit_exp(kt, s_t):
                    sp = kt % 2
                    r = kt - 16 * g
                    P.wait("scalar", mm, s_t)
                    for hx, (SB, PT) in enumerate(((SBa, PTa), (SBb, PTb))):
                        P.wait("scalar", mm, use_t[hx][sp])
                        P.wait("scalar", dve, msk_t[hx][sp])
                        exp_t[hx][sp] = P.op(
                            "scalar",
                            lambda E, hx=hx, sp=sp, a=a: E.activation(
                                (PTa, PTb)[hx][sp], (SBa, SBb)[hx][sp][:, :],
                                AF.Exp, bias=negBb[:, a + hx:a + hx + 1],
                                scale=0.125), inc=act)
                        if r >= 0:
                            P.wait("vector", act, exp_t[hx][sp])
                            msk_t[hx][sp] = vop(
                                "tensor_mul", (PTa, PTb)[hx][sp],
                                (PTa, PTb)[hx][sp],
                                smask[:, r * 512:(r + 1) * 512])

                def emit_AV(kt):
                    sp = kt % 2
                    r = kt - 16 * g
                    for hx in range(2):
                        if r >= 0:
                            P.wait("tensor", dve, msk_t[hx][sp])
                        else:
                            P.wait("tensor", act, exp_t[hx][sp])
                        use_t[hx][sp] = matmul(
                            (PB[2], PB[7])[hx][0:65, :],
                            v65c[:, kt * 65:kt * 65 + 65],
                            (PTa, PTb)[hx][sp],
                            start=(kt == 0), stop=(kt == nkt - 1), inc=mm)

                for kt in range(nkt + DPTH):
                    if kt >= DPTH:
                        emit_AV(kt - DPTH)
                    if kt < nkt:
                        s_t = emit_S(kt)
                        emit_exp(kt, s_t)
                P.wait("vector", mm, use_t[1][(nkt - 1) % 2])
                for hx, AVB in enumerate((PB[2], PB[7])):
                    vop("reciprocal", sq[64:65, :], AVB[64:65, :])
                    P.wait("tensor", dve)
                    matmul(PB[3][0:64, :], onesr[64:65, 0:64], sq[64:65, :],
                           True, True, inc=mm)
                    P.wait("vector", mm)
                    vop("tensor_copy", lb, PB[3][0:64, :])
                    ocol = qs
                    if hx == 0:
                        vop("tensor_mul", oT[0:64, ocol:ocol + 512],
                            AVB[0:64, :], lb)
                    else:
                        par_o = ost_n[0] % 2
                        stg_o = (t1s[0:64, :], sq[0:64, :])[par_o]
                        P.wait("vector", ostp[par_o])
                        vop("tensor_mul", stg_o, AVB[0:64, :], lb)
                        P.wait("scalar", dve)
                        dma("scalar", ostp[par_o],
                            oT[64:128, ocol:ocol + 512], stg_o)
                        ost_n[0] += 1
                    P.wait("tensor", dve)
        kvfree = (mm.n, dve.n)
    P.join()

    if upto == "B":
        dbg = nc.dram_tensor("dbg", [128, c.NC * c.T], F32,
                             kind="ExternalOutput").ap()
        dma("sync", sst, dbg[:, :], oT)
        P.wait("sync", sst)
        P.finalize()
        return nc

    x2 = ar(R_A, c.NC * c.T)
    O_WO = O_S1
    O_XR = O_WO + 2 * WQC
    O_TS = O_XR + 1024
    assert O_TS + WQC <= AR_COLS
    wofree = [0, 0]
    xrfree = [0, 0]
    for fo in range(c.NC):
        par = fo % 2
        wstg = ar(O_WO + par * WQC, WQC)
        P.wait("sync", mm, wofree[par])
        wld = dma("sync", wst[par], wstg, woR[fo, :, :])
        for t2 in range(c.T // 512):
            xpar = t2 % 2
            xr = ar(O_XR + xpar * 512, 512)
            P.wait("scalar", dve, xrfree[xpar])
            xrld = dma("scalar", lxr[xpar], xr,
                       xoR[t2, :, fo * 512:(fo + 1) * 512])
            bank = PB[0 + (t2 % 2)]
            P.wait("tensor", wst[par], wld)
            for n in range(c.NC):
                matmul(bank[:, :], wstg[:, n * 128:(n + 1) * 128],
                       oT[:, n * c.T + t2 * 512:n * c.T + t2 * 512 + 512],
                       start=(n == 0), stop=(n == c.NC - 1),
                       inc=mm if n == c.NC - 1 else None)
            P.wait("vector", mm)
            P.wait("vector", lxr[xpar], xrld)
            xrfree[xpar] = vop(
                "tensor_add",
                x2[:, fo * c.T + t2 * 512:fo * c.T + t2 * 512 + 512],
                bank[:, :], xr)
            P.wait("tensor", dve)
        wofree[par] = mm.n
    P.join()

    if upto == "C1":
        dbg = nc.dram_tensor("dbg", [128, c.NC * c.T], F32,
                             kind="ExternalOutput").ap()
        dma("sync", sst, dbg[:, :], x2)
        P.wait("sync", sst)
        P.finalize()
        return nc

    h2hi = arb(R_B, c.NC * c.T)
    h2lo = arb(R_B + 8192, c.NC * c.T)
    for t2 in range(c.T // 512):
        emit_ln(lambda n, t2=t2:
                x2[:, n * c.T + t2 * 512:n * c.T + t2 * 512 + 512],
                pair=lambda n, t2=t2:
                (h2hi[:, n * c.T + t2 * 512:n * c.T + t2 * 512 + 512],
                 h2lo[:, n * c.T + t2 * 512:n * c.T + t2 * 512 + 512]))

    if upto == "C2":
        dbg = nc.dram_tensor("dbg", [128, 2 * c.NC * c.T], BF16,
                             kind="ExternalOutput").ap()
        P.join()
        dma("sync", sst, dbg[:, 0:c.NC * c.T], h2hi)
        dma("sync", sst, dbg[:, c.NC * c.T:], h2lo)
        P.wait("sync", sst)
        P.finalize()
        return nc

    tsb = [ar(O_XR, WQC), ar(O_XR + WQC, WQC)]
    tw = [0, 0]
    ntt = 0
    for which in range(2):
        dstap = (out, h2tok)[which]
        for tt in range(c.NM):
            tp_ = ntt % 2
            ntt += 1
            tstg = tsb[tp_]
            P.wait("vector", tstc[tp_], tw[tp_])
            for n in range(c.NC):
                bank = PB[4 + (n % 2)]
                cs0 = n * c.T + tt * 128
                if which == 0:
                    P.op("tensor", lambda E, cs0=cs0, bank=bank: E.transpose(
                         bank[:, 0:128], x2[:, cs0:cs0 + 128],
                         sident[:, :]), inc=mm)
                else:
                    matmul(bank[:, 0:128], h2hi[:, cs0:cs0 + 128],
                           sidbf[:, :], True, False)
                    matmul(bank[:, 0:128], h2lo[:, cs0:cs0 + 128],
                           sidbf[:, :], False, True, inc=mm)
                P.wait("vector", mm)
                vop("tensor_copy", tstg[:, n * 128:(n + 1) * 128],
                    bank[:, 0:128])
                P.wait("tensor", dve)
            P.wait("scalar", dve)
            tw[tp_] = dma("scalar", tstc[tp_], dstap[tt * 128:(tt + 1) * 128, :],
                          tstg)
    P.wait("vector", tstc[0])
    P.wait("vector", tstc[1])
    vop("memset", tsb[0][0:1, 0:c.D], 0.0)
    P.wait("scalar", dve)
    dma("scalar", sst, h2tok[c.T:c.T + 1, :], tsb[0][0:1, 0:c.D])
    out_base = sst.n
    tst_all = (tstc[0].n, tstc[1].n)

    srout = ar(O_WO, c.NC * c.E)
    srhi = arb(O_WO + 256, c.NC * c.E)
    srlo = arb(O_WO + 256 + c.NC * c.E // 2, c.NC * c.E)
    rout_ld = dma("sync", lrt, srout, routerR[:, :])
    P.wait("vector", lrt, rout_ld)
    vop("tensor_copy", srhi, srout)
    vop("tensor_sub", srlo, srout, srhi)
    P.wait("tensor", dve)
    for tt in range(c.NM):
        for ps in range(3):
            hh = (h2hi, h2lo, h2hi)[ps]
            rr = (srhi, srhi, srlo)[ps]
            for n in range(c.NC):
                matmul(PB[6][0:128, 0:c.E],
                       hh[:, n * c.T + tt * 128:n * c.T + tt * 128 + 128],
                       rr[:, n * c.E:(n + 1) * c.E],
                       start=(ps == 0 and n == 0),
                       stop=(ps == 2 and n == c.NC - 1),
                       inc=mm if (ps == 2 and n == c.NC - 1) else None)
        P.wait("vector", mm)
        P.wait("vector", swr)
        vop("tensor_copy", slog[:, :], PB[6][0:128, 0:c.E])
        P.wait("tensor", dve)
        vop("reduce_max", sr_m1[:, :], slog[:, :], AX.X)
        vop("tensor_single_scalar", sr_is1[:, :], slog[:, :], sr_m1[:, :],
            ALU.is_ge)
        vop("tensor_scalar", sr_msk[:, :], sr_is1[:, :], -1e30, None, ALU.mult)
        vop("tensor_add", sr_msk[:, :], sr_msk[:, :], slog[:, :])
        vop("reduce_max", sr_m2[:, :], sr_msk[:, :], AX.X)
        vop("tensor_single_scalar", sr_is2[:, :], sr_msk[:, :], sr_m2[:, :],
            ALU.is_ge)
        vop("tensor_sub", sr_d[:, :], sr_m2[:, :], sr_m1[:, :])
        P.wait("scalar", dve)
        P.op("scalar", lambda E: E.activation(sr_d[:, :], sr_d[:, :], AF.Exp,
                                              bias=0.0, scale=1.0), inc=act)
        P.wait("vector", act)
        vop("tensor_scalar_add", sr_s[:, :], sr_d[:, :], 1.0)
        vop("reciprocal", sr_w1[:, :], sr_s[:, :])
        vop("tensor_mul", sr_w2[:, :], sr_d[:, :], sr_w1[:, :])
        vop("tensor_mul", sr_t[:, :], sr_is1[:, :], siota8[:, :])
        vop("reduce_sum", sr_i1[:, :], sr_t[:, :], AX.X)
        vop("tensor_mul", sr_t[:, :], sr_is2[:, :], siota8[:, :])
        vop("reduce_sum", sr_i2[:, :], sr_t[:, :], AX.X)
        vop("tensor_add", sr_t[:, :], sr_is1[:, :], sr_is2[:, :])
        P.wait("tensor", dve)
        P.wait("tensor", act)
        matmul(PB[7][0:128, 0:c.E], striu[:, :], sr_t[:, :], True, True,
               inc=mm)
        matmul(PB[6][0:1, 0:c.E], scones[:, :], sr_t[:, :], True, True,
               inc=mm)
        P.wait("vector", mm)
        vop("tensor_copy", srow8[:, tt * c.E:(tt + 1) * c.E], PB[6][0:1, 0:c.E])
        if tt == 0:
            vop("memset", srow8[:, c.NM * c.E:(c.NM + 1) * c.E], 0.0)
        else:
            vop("tensor_add", srow8[:, c.NM * c.E:(c.NM + 1) * c.E],
                srow8[:, c.NM * c.E:(c.NM + 1) * c.E],
                srow8[:, (tt - 1) * c.E:tt * c.E])
        P.wait("tensor", dve)
        matmul(PB[5][0:128, 0:c.E], onesr[0:1, :],
               srow8[0:1, c.NM * c.E:(c.NM + 1) * c.E], True, True, inc=mm)
        P.wait("vector", mm)
        vop("tensor_copy", sr_obc[:, :], PB[5][0:128, 0:c.E])
        vop("tensor_add", sr_pos[:, :], PB[7][0:128, 0:c.E], sr_obc[:, :])
        P.wait("tensor", dve)
        for isx, ix, px, wx, k in ((sr_is1, sr_i1, sr_p1, sr_w1, 0),
                                   (sr_is2, sr_i2, sr_p2, sr_w2, 1)):
            vop("tensor_mul", sr_t[:, :], isx[:, :], sr_pos[:, :])
            vop("reduce_sum", px[:, :], sr_t[:, :], AX.X)
            vop("tensor_scalar", sr_s[:, :], ix[:, :], float(c.CAP), None,
                ALU.mult)
            vop("tensor_add", sr_s[:, :], sr_s[:, :], px[:, :])
            vop("tensor_scalar_min", sr_s[:, :], sr_s[:, :],
                float(c.E * c.CAP))
            vop("tensor_copy", sd1i[:, k:k + 1], sr_s[:, :])
            P.wait("gpsimd", dve)
            if tt == 0 and k == 0:
                P.wait("gpsimd", swr, ranks_zeroed)
            P.op("gpsimd", lambda E, k=k, tt=tt: E.indirect_dma_start(
                out=ranktok[:, :],
                out_offset=IndirectOffsetOnAxis(sd1i[:, k:k + 1], 0),
                in_=sitok[:, tt:tt + 1], in_offset=None), inc=swr)
            P.op("gpsimd", lambda E, k=k, wx=wx: E.indirect_dma_start(
                out=rankcw[:, :],
                out_offset=IndirectOffsetOnAxis(sd1i[:, k:k + 1], 0),
                in_=wx[:, :], in_offset=None), inc=swr)
    routing_done = swr.n
    P.join()

    if upto == "C":
        dbg = nc.dram_tensor("dbg", [128, 2 * c.NC * c.T], BF16,
                             kind="ExternalOutput").ap()
        dma("sync", sst, dbg[:, 0:c.NC * c.T], h2hi)
        dma("sync", sst, dbg[:, c.NC * c.T:], h2lo)
        P.wait("sync", sst)
        P.finalize()
        return nc

    O_HS = R_W
    O_SUG = R_W + c.HGRP * c.T
    SWDC = c.HGRP * 512
    O_SWDF = R_A
    O_SWDP = R_A + 2 * SWDC
    O_ACCS = R_A + 4 * SWDC
    assert O_ACCS + 2 * c.NM * 512 <= R_A + 16384
    HST = c.HGRP * c.T
    hshi = arb(O_HS, HST)
    hslo = arb(O_HS + HST // 2, HST)
    ldslot = [0, 0]
    nld = 0
    down_done = 0
    swd_split_a = [0, 0]
    swd_split_d = [0, 0]
    swd_rd = [0, 0]
    accS_rd = [0, 0]
    scr_act = [0, 0]
    d_h2 = dve.n
    d_act = [0, 0]
    d_dve = [0, 0]
    for hg in range(NHG):
        for gl in range(c.HGRP):
            gc = hg * c.HGRP + gl
            stg = [None, None]
            lds_t = [0, 0]
            for half, colblk in ((0, gc), (1, c.NJC + gc)):
                par = nld % 2
                nld += 1
                sgh = arb(O_SUG + par * WQC, WQC)
                sgl = arb(O_SUG + par * WQC + WQC // 2, WQC)
                stg[half] = (sgh, sgl)
                P.wait("sync", mm, ldslot[par])
                if nld <= 2:
                    P.wait("sync", pst, pr_sug)
                dma("sync", wst[par], sgh, sug_h[colblk, :, :])
                lds_t[half] = dma("sync", wst[par], sgl, sug_l[colblk, :, :])
            last_gate = 0
            for t2 in range(c.T // 512):
                bG = PB[0 + 2 * (t2 % 2)]
                bU = PB[1 + 2 * (t2 % 2)]
                P.wait("tensor", wst[(nld - 2) % 2], lds_t[0])
                P.wait("tensor", dve, d_h2)
                P.wait("tensor", act, d_act[t2 % 2])
                P.wait("tensor", dve, d_dve[t2 % 2])
                for bank, half in ((bG, 0), (bU, 1)):
                    if half == 1:
                        P.wait("tensor", wst[(nld - 1) % 2], lds_t[1])
                    for ps in range(3):
                        wv_ = stg[half][(0, 1, 0)[ps]]
                        xv_ = (h2hi, h2hi, h2lo)[ps]
                        for n in range(c.NC):
                            matmul(bank[:, :], wv_[:, n * 128:(n + 1) * 128],
                                   xv_[:, n * c.T + t2 * 512:
                                       n * c.T + t2 * 512 + 512],
                                   start=(ps == 0 and n == 0),
                                   stop=(ps == 2 and n == c.NC - 1),
                                   inc=mm if (ps == 2 and n == c.NC - 1)
                                   else None)
                    if half == 0:
                        last_gate = mm.n
                gum = mm.n
                P.wait("scalar", mm, gum)
                P.wait("scalar", dve)
                P.op("scalar", lambda E, bG=bG: E.activation(
                    t1s[:, :], bG[:, :], AF.Silu, bias=0.0, scale=1.0),
                    inc=act)
                d_act[t2 % 2] = act.n
                scr = (sBC1, sBC2)[t2 % 2]
                hcol = gl * c.T + t2 * 512
                P.wait("vector", act, scr_act[t2 % 2])
                P.wait("vector", act, d_act[t2 % 2])
                vop("tensor_mul", scr[:, :], t1s[:, :], bU[:, :])
                d_dve[t2 % 2] = dve.n
                P.wait("scalar", dve, d_dve[t2 % 2])
                P.wait("scalar", mm, down_done)
                scr_act[t2 % 2] = P.op(
                    "scalar", lambda E, scr=scr, hcol=hcol: E.activation(
                        hshi[:, hcol:hcol + 512], scr[:, :], AF.Copy,
                        bias=0.0, scale=1.0), inc=act)
                P.wait("vector", act, scr_act[t2 % 2])
                P.wait("vector", mm, down_done)
                vop("tensor_sub", hslo[:, hcol:hcol + 512], scr[:, :],
                    hshi[:, hcol:hcol + 512])
            ldslot[(nld - 2) % 2] = last_gate
            ldslot[(nld - 1) % 2] = mm.n
        hs_done = dve.n
        hs_act = act.n
        if upto == "Dhs" and hg == 0:
            dbg = nc.dram_tensor("dbg", [128, 2 * HST], BF16,
                                 kind="ExternalOutput").ap()
            P.join()
            dma("sync", sst, dbg[:, 0:HST], hshi)
            dma("sync", sst, dbg[:, HST:], hslo)
            P.wait("sync", sst)
            P.finalize()
            return nc
        for fm in range(c.NDB):
            kf = hg * c.NDB + fm
            p2s = kf % 2
            wf = ar(O_SWDF + p2s * SWDC, SWDC)
            wph = arb(O_SWDP + p2s * SWDC, SWDC)
            wpl = arb(O_SWDP + p2s * SWDC + SWDC // 2, SWDC)
            P.wait("sync", act, swd_split_a[p2s])
            P.wait("sync", dve, swd_split_d[p2s])
            swl = dma("sync", wsd[p2s], wf, swdR[hg, fm, :, :])
            P.wait("scalar", wsd[p2s], swl)
            P.wait("scalar", mm, swd_rd[p2s])
            swd_split_a[p2s] = P.op(
                "scalar", lambda E, wph=wph, wf=wf: E.activation(
                    wph, wf, AF.Copy, bias=0.0, scale=1.0), inc=act)
            P.wait("vector", wsd[p2s], swl)
            P.wait("vector", mm, swd_rd[p2s])
            P.wait("vector", act, swd_split_a[p2s])
            swd_split_d[p2s] = vop("tensor_sub", wpl, wf, wph)
            accS = ar(O_ACCS + p2s * (c.NM * 512), c.NM * 512)
            P.wait("tensor", act, swd_split_a[p2s])
            P.wait("tensor", dve, swd_split_d[p2s])
            P.wait("tensor", dve, hs_done)
            P.wait("tensor", act, hs_act)
            for tt in range(c.NM):
                bank = PB[4 + (tt % 2)]
                for ps in range(3):
                    hv_ = (hshi, hslo, hshi)[ps]
                    wv2 = (wph, wph, wpl)[ps]
                    for gl in range(c.HGRP):
                        matmul(bank[:, :],
                               hv_[:, gl * c.T + tt * 128:
                                   gl * c.T + tt * 128 + 128],
                               wv2[:, gl * 512:(gl + 1) * 512],
                               start=(ps == 0 and gl == 0),
                               stop=(ps == 2 and gl == c.HGRP - 1),
                               inc=mm if (ps == 2 and gl == c.HGRP - 1)
                               else None)
                P.wait("vector", mm)
                P.wait("vector", swa[p2s], accS_rd[p2s])
                vop("tensor_copy", accS[:, tt * 512:(tt + 1) * 512],
                    bank[:, :])
                P.wait("tensor", dve)
            swd_rd[p2s] = mm.n
            P.wait("gpsimd", dve)
            P.wait("gpsimd", sst, out_base)
            P.wait("gpsimd", tstc[0], tst_all[0])
            P.wait("gpsimd", tstc[1], tst_all[1])
            accS_rd[p2s] = dma(
                "gpsimd", swa[p2s],
                out[0:c.T, :].rearrange("(t p) d -> p t d", p=128)
                [:, :, fm * 512:(fm + 1) * 512],
                accS.rearrange("p (t d) -> p t d", d=512),
                accum_op=ALU.add)
        down_done = mm.n
    P.join()
    shared_adds = (swa[0].n, swa[1].n)
    if upto == "Dpair":
        dbg = nc.dram_tensor("dbg", [128, 2 * c.NC * c.T], BF16,
                             kind="ExternalOutput").ap()
        dma("sync", sst, dbg[:, 0:c.NC * c.T], h2hi)
        dma("sync", sst, dbg[:, c.NC * c.T:], h2lo)
        P.wait("sync", sst)
        P.finalize()
        return nc

    if upto == "D":
        P.wait("sync", swa[0], shared_adds[0])
        P.wait("sync", swa[1], shared_adds[1])
        P.finalize()
        return nc

    O_GAT = R_B
    O_EF = R_B + 2 * c.NC * c.CAP
    O_HE = R_W
    O_GT = R_W + c.NIC * c.CAP
    O_EWD = R_A
    O_EUG = R_A + 2 * c.NIC * 512
    O_ETK = O_EUG + 2 * WQC
    assert O_ETK + 1024 <= R_A + 16384
    assert O_EF + 2 * WQC <= R_B + 16384
    assert 2 * c.NC * c.CAP <= 16384
    assert O_GT + c.NCAP * c.D <= AR_COLS
    nldE = 0
    eugfree = [0, 0]
    ewd_free = 0
    he_free = 0
    gat_rd = [0, 0]
    tr_done = [0, 0]
    gather_done = [0] * (c.E + 1)
    eidx_done = [0] * (c.E + 1)
    scat_prev = 0
    for eng in ("sync", "vector", "tensor", "scalar", "gpsimd"):
        P.wait(eng, swa[0], shared_adds[0])
        P.wait(eng, swa[1], shared_adds[1])
    P.wait("scalar", swr, routing_done)

    def emit_prep(e):
        par = e % 2
        P.wait("vector", etk[0])
        P.wait("vector", etk[1])
        for r in range(c.NCAP):
            P.wait("scalar", dve)
            lr = dma("scalar", ler, erow,
                     ranktok.rearrange("(a b) one -> a (b one)", a=1)
                     [:, e * c.CAP + r * 128:e * c.CAP + r * 128 + 128])
            P.wait("tensor", ler, lr)
            P.wait("tensor", dve)
            P.op("tensor", lambda E: E.transpose(
                PB[6][0:128, 0:1], erow, sident[0:1, 0:1]), inc=mm)
            P.wait("vector", mm)
            vop("tensor_copy", eidx[:, par * c.NCAP + r:par * c.NCAP + r + 1],
                PB[6][0:128, 0:1])
        cwr = (srowB, srowC)[par][0:1, 0:c.CAP]
        P.wait("scalar", mm)
        lcww = dma("scalar", lcw, cwr,
                   rankcw.rearrange("(a b) one -> a (b one)", a=1)
                   [:, e * c.CAP:(e + 1) * c.CAP])
        P.wait("tensor", lcw, lcww)
        P.wait("tensor", act)
        P.wait("tensor", dve)
        matmul(PB[6][0:128, 0:c.CAP], onesr[0:1, :], cwr, True, True, inc=mm)
        P.wait("vector", mm)
        vop("tensor_copy", (sBC1, sBC2)[par][:, 0:c.CAP],
            PB[6][0:128, 0:c.CAP])
        eidx_done[e] = dve.n

    def emit_gather(e):
        par = e % 2
        P.wait("gpsimd", sst, out_base)
        P.wait("gpsimd", tstc[0], tst_all[0])
        P.wait("gpsimd", tstc[1], tst_all[1])
        P.wait("gpsimd", dve, eidx_done[e])
        P.wait("gpsimd", mm, tr_done[1 - par] if e > 0 else 0)
        for r in range(c.NCAP):
            P.op("gpsimd", lambda E, r=r, par=par: E.indirect_dma_start(
                out=ar(O_GT + r * c.D, c.D), out_offset=None,
                in_=h2tok[:, :],
                in_offset=IndirectOffsetOnAxis(
                    eidx[:, par * c.NCAP + r:par * c.NCAP + r + 1], 0)),
                inc=swe)
        gather_done[e] = swe.n

    def gview(par):
        base = O_GAT + par * c.NC * c.CAP
        return (arb(base, c.NC * c.CAP),
                arb(base + c.NC * c.CAP // 2, c.NC * c.CAP))

    def emit_transposes(e):
        par = e % 2
        gh, gl2 = gview(par)
        P.wait("tensor", swe, gather_done[e])
        P.wait("tensor", mm, gat_rd[par])
        bk3 = (PB[4], PB[5], PB[7])
        last_a = [0, 0, 0]
        last_d = [0, 0, 0]
        k3 = 0
        for r in range(c.NCAP):
            for n in range(c.NC):
                bi = k3 % 3
                k3 += 1
                bank = bk3[bi]
                P.wait("tensor", act, last_a[bi])
                P.wait("tensor", dve, last_d[bi])
                P.op("tensor", lambda E, r=r, n=n, bank=bank: E.transpose(
                    bank[:, 0:128],
                    ar(O_GT + r * c.D, c.D)[:, n * 128:(n + 1) * 128],
                    sident[:, :]), inc=mm)
                tmm = mm.n
                cs2 = slice(n * c.CAP + r * 128, n * c.CAP + r * 128 + 128)
                P.wait("scalar", mm, tmm)
                last_a[bi] = P.op(
                    "scalar", lambda E, gh=gh, cs2=cs2, bank=bank:
                    E.activation(gh[:, cs2], bank[:, 0:128], AF.Copy,
                                 bias=0.0, scale=1.0), inc=act)
                P.wait("vector", mm, tmm)
                P.wait("vector", act, last_a[bi])
                last_d[bi] = vop("tensor_sub", gl2[:, cs2], bank[:, 0:128],
                                 gh[:, cs2])
        tr_done[par] = mm.n

    def emit_compute(e, gat_dve):
        par = e % 2
        gath, gatl = gview(par)
        he = ar(O_HE, c.NIC * c.CAP)
        for gc in range(c.NIC):
            stg = [None, None]
            lt = [0, 0]
            pars = [0, 0]
            spl_t = [0, 0]
            for half, colblk in ((0, gc), (1, c.NIC + gc)):
                p2 = E_state["nldE"] % 2
                E_state["nldE"] += 1
                pars[half] = p2
                ef = ar(O_EF + p2 * WQC, WQC)
                egh = arb(O_EUG + p2 * WQC, WQC)
                egl = arb(O_EUG + p2 * WQC + WQC // 2, WQC)
                stg[half] = (egh, egl)
                P.wait("sync", act, E_state["eugsplit_a"][p2])
                P.wait("sync", dve, E_state["eugsplit_d"][p2])
                lt[half] = dma("sync", wst[p2], ef, eugR[e, colblk, :, :])
                P.wait("scalar", wst[p2], lt[half])
                P.wait("scalar", mm, eugfree[p2])
                sa = P.op("scalar", lambda E, egh=egh, ef=ef: E.activation(
                    egh, ef, AF.Copy, bias=0.0, scale=1.0), inc=act)
                P.wait("vector", wst[p2], lt[half])
                P.wait("vector", mm, eugfree[p2])
                P.wait("vector", act, sa)
                sd = vop("tensor_sub", egl, ef, egh)
                spl_t[half] = (sa, sd)
                E_state["eugsplit_a"][p2] = sa
                E_state["eugsplit_d"][p2] = sd
            bG, bU = PB[0], PB[1]
            P.wait("tensor", act, spl_t[0][0])
            P.wait("tensor", dve, spl_t[0][1])
            P.wait("tensor", dve, gat_dve)
            grp_end = [0, 0]
            for bank, half in ((bG, 0), (bU, 1)):
                if half == 1:
                    P.wait("tensor", act, spl_t[1][0])
                    P.wait("tensor", dve, spl_t[1][1])
                for ps in range(3):
                    wv_ = stg[half][(0, 1, 0)[ps]]
                    gv_ = (gath, gath, gatl)[ps]
                    for n in range(c.NC):
                        matmul(bank[0:128, 0:c.CAP],
                               wv_[:, n * 128:(n + 1) * 128],
                               gv_[:, n * c.CAP:(n + 1) * c.CAP],
                               start=(ps == 0 and n == 0),
                               stop=(ps == 2 and n == c.NC - 1),
                               inc=mm if (ps == 2 and n == c.NC - 1)
                               else None)
                grp_end[half] = mm.n
            eugfree[pars[0]] = grp_end[0]
            eugfree[pars[1]] = grp_end[1]
            P.wait("scalar", mm)
            P.wait("scalar", dve)
            P.op("scalar", lambda E, bG=bG: E.activation(
                t1s[:, 0:c.CAP], bG[0:128, 0:c.CAP],
                AF.Silu, bias=0.0, scale=1.0), inc=act)
            P.wait("vector", act)
            vop("tensor_mul", sq[:, 0:c.CAP], t1s[:, 0:c.CAP],
                bU[0:128, 0:c.CAP])
            P.wait("vector", mm, E_state["he_free"])
            vop("tensor_mul", he[:, gc * c.CAP:(gc + 1) * c.CAP],
                sq[:, 0:c.CAP], (sBC1, sBC2)[par][:, 0:c.CAP])
            P.wait("tensor", dve)
        gat_rd[par] = mm.n
        he_done = dve.n
        P.wait("gpsimd", etk[0], E_state["scat_prev0"])
        P.wait("gpsimd", etk[1], E_state["scat_prev1"])
        for fm in range(c.NDB):
            p2 = fm % 2
            wstg2 = ar(O_EWD + p2 * c.NIC * 512, c.NIC * 512)
            P.wait("sync", mm, E_state["ewd_free"][p2])
            wl = dma("sync", ewl[p2], wstg2, ewdR[e, fm, :, :])
            P.wait("tensor", ewl[p2], wl)
            P.wait("tensor", dve, he_done)
            for r in range(c.NCAP):
                bank = PB[2 + (r % 2)]
                P.wait("tensor", dve, E_state["bankcp"][r % 2])
                for gc in range(c.NIC):
                    matmul(bank[:, :],
                           he[:, gc * c.CAP + r * 128:gc * c.CAP + r * 128 + 128],
                           wstg2[:, gc * 512:(gc + 1) * 512],
                           start=(gc == 0), stop=(gc == c.NIC - 1),
                           inc=mm if gc == c.NIC - 1 else None)
                P.wait("vector", mm)
                ep = (fm * c.NCAP + r) % 2
                etkb = ar(O_ETK + ep * 512, 512)
                P.wait("vector", etk[ep])
                sc_dve = vop("tensor_copy", etkb, bank[:, :])
                E_state["bankcp"][r % 2] = sc_dve
                P.wait("gpsimd", dve, sc_dve)
                P.wait("gpsimd", etk[0])
                P.wait("gpsimd", etk[1])
                P.op("gpsimd",
                     lambda E, etkb=etkb, r=r, fm=fm, par=par:
                     E.indirect_dma_start(
                         out=out[:, :],
                         out_offset=IndirectOffsetOnAxis(
                             eidx[:, par * c.NCAP + r:par * c.NCAP + r + 1], 0),
                         in_=etkb, in_offset=None,
                         element_offset=fm * 512,
                         compute_op=ALU.add), inc=etk[ep])
            E_state["ewd_free"][p2] = mm.n
        E_state["scat_prev0"] = etk[0].n
        E_state["scat_prev1"] = etk[1].n
        E_state["he_free"] = mm.n

    E_state = {"nldE": 0, "ewd_free": [0, 0], "scat_prev0": 0, "scat_prev1": 0,
               "he_free": 0, "bankcp": [0, 0], "eugsplit_a": [0, 0],
               "eugsplit_d": [0, 0]}
    gat_dve_done = [0] * (c.E + 1)
    for e in range(c.E + 1):
        if e < c.E:
            emit_prep(e)
            emit_gather(e)
            emit_transposes(e)
            gat_dve_done[e] = dve.n
        if e >= 1:
            emit_compute(e - 1, gat_dve_done[e - 1])

    P.wait("sync", swe)
    P.wait("sync", etk[0])
    P.wait("sync", etk[1])
    P.wait("sync", swa[0])
    P.wait("sync", swa[1])
    P.wait("sync", sst)
    P.finalize()
    return nc


def _rope_tables(S):
    try:
        import jax
        import jax.numpy as jnp
        cpu = jax.devices("cpu")[0]
        with jax.default_device(cpu):
            pos = jnp.arange(S, dtype=jnp.float32)[:, None]
            inv = jnp.exp(jnp.arange(0, 64, 2, dtype=jnp.float32)
                          * (-math.log(10000.0) / 64))
            ang = pos * inv[None, :]
            cos = np.asarray(jnp.cos(ang))
            sin = np.asarray(jnp.sin(ang))
    except Exception:
        pos = np.arange(S, dtype=np.float32)[:, None]
        inv = np.exp(np.arange(0, 64, 2, dtype=np.float32)
                     * np.float32(-math.log(10000.0) / 64)).astype(np.float32)
        ang = (pos * inv[None, :]).astype(np.float32)
        cos = np.cos(ang.astype(np.float64)).astype(np.float32)
        sin = np.sin(ang.astype(np.float64)).astype(np.float32)
    cosx = np.repeat(cos.T, 2, axis=0).astype(np.float32)
    sinx = np.empty((64, S), np.float32)
    sinx[0::2] = -sin.T
    sinx[1::2] = sin.T
    return np.ascontiguousarray(cosx), np.ascontiguousarray(sinx)


def make_inputs(c: CFG, core, x, wq, wk, wv, wo, router, w_ug, w_dn, s_ug, s_dn):
    import ml_dtypes
    b, qi = core // 4, core % 4
    own = ((np.arange(c.NM)[:, None] * 4 + qi) * 128
           + np.arange(128)[None, :]).reshape(-1)
    cosx, sinx = _rope_tables(c.S)
    qc = np.ascontiguousarray(cosx[:, own])
    qs = np.ascontiguousarray(sinx[:, own])
    m = np.zeros((16, 128, 512), np.float32)
    kk = np.arange(128)[:, None]
    jj = np.arange(128)[None, :]
    for r in range(16):
        for s in range(4):
            m[r][:, s * 128:(s + 1) * 128] = (
                (4 * s + qi - r) * 128 + jj - kk >= 0).astype(np.float32)
    ident = np.eye(128, dtype=np.float32)
    triu = np.triu(np.ones((128, 128), np.float32), 1)
    iota8 = np.broadcast_to(np.arange(c.E, dtype=np.float32)[None, :],
                            (128, c.E)).copy()
    itok = (np.arange(c.NM)[None, :] * 128
            + np.arange(128)[:, None]).astype(np.float32)
    cones = np.ones((128, 1), np.float32)
    onesrv = np.ones((128, 128), np.float32)
    ind64 = np.zeros((128, 2), np.float32)
    ind64[0:64, 0] = 1.0
    ind64[64:128, 1] = 1.0
    NC, NF, NJC, NIC, NDB, E, NTC = (c.NC, c.NF, c.NJC, c.NIC, c.NDB, c.E,
                                     c.NTC)
    NHG = NJC // c.HGRP
    xbarr = x[b].astype(np.float32)
    xbRv = np.ascontiguousarray(
        xbarr.reshape(NTC, 512, NC, 128).transpose(0, 3, 2, 1)
        .reshape(NTC, 128, NC * 512))
    xoarr = xbarr[own]
    xoRv = np.ascontiguousarray(
        xoarr.reshape(c.T // 512, 512, NC, 128).transpose(0, 3, 2, 1)
        .reshape(c.T // 512, 128, NC * 512))
    wqRv = np.ascontiguousarray(
        wq.reshape(NC, 128, NC, 128).transpose(0, 3, 2, 1)
        .reshape(NC, 128, NC * 128))
    woRv = np.ascontiguousarray(
        wo.reshape(NC, 128, NC, 128).transpose(0, 3, 2, 1)
        .reshape(NC, 128, NC * 128))
    wkRv = np.ascontiguousarray(
        wk.reshape(NF, 128, NC, 128).transpose(3, 2, 0, 1)
        .reshape(128, NC * NF * 128))
    wvRv = np.ascontiguousarray(
        wv.reshape(NF, 128, NC, 128).transpose(3, 2, 0, 1)
        .reshape(128, NC * NF * 128))
    routerRv = np.ascontiguousarray(
        router.reshape(NC, 128, E).transpose(1, 0, 2).reshape(128, NC * E))
    sugRv = np.ascontiguousarray(
        s_ug.reshape(NC, 128, 2 * NJC, 128).transpose(2, 1, 0, 3)
        .reshape(2 * NJC, 128, NC * 128))
    swdRv = np.ascontiguousarray(
        s_dn.reshape(NHG, c.HGRP, 128, NDB, 512).transpose(0, 3, 2, 1, 4)
        .reshape(NHG, NDB, 128, c.HGRP * 512))
    eugRv = np.ascontiguousarray(
        w_ug.reshape(E, NC, 128, 2 * NIC, 128).transpose(0, 3, 2, 1, 4)
        .reshape(E, 2 * NIC, 128, NC * 128))
    ewdRv = np.ascontiguousarray(
        w_dn.reshape(E, NIC, 128, NDB, 512).transpose(0, 3, 2, 1, 4)
        .reshape(E, NDB, 128, NIC * 512))
    return {
        "xbR": xbRv, "xoR": xoRv,
        "wqR": wqRv, "wkR": wkRv, "wvR": wvRv, "woR": woRv,
        "routerR": routerRv,
        "sugR": sugRv, "swdR": swdRv, "eugR": eugRv, "ewdR": ewdRv,
        "kcos": cosx, "ksin": sinx, "qcost": qc, "qsint": qs,
        "masks16": m.astype(ml_dtypes.bfloat16),
        "identin": ident, "triuin": triu, "iota8in": iota8, "itokin": itok,
        "conesin": cones, "onesrin": onesrv, "ind64in": ind64,
        "identbfin": ident.astype(ml_dtypes.bfloat16),
    }


_CFG = CFG()


def kernel(L0_attn_x, L0_attn_wq, L0_attn_wk, L0_attn_wv, L0_attn_wo,
           L0_ffn_router, L0_ffn_w_up_gate, L0_ffn_w_down,
           L0_ffn_shared_w_up_gate, L0_ffn_shared_w_down):
    c = _CFG
    x = np.asarray(L0_attn_x, np.float32)
    nc = build_program(c)
    in_maps = [make_inputs(c, core, x,
                           np.asarray(L0_attn_wq, np.float32),
                           np.asarray(L0_attn_wk, np.float32),
                           np.asarray(L0_attn_wv, np.float32),
                           np.asarray(L0_attn_wo, np.float32),
                           np.asarray(L0_ffn_router, np.float32),
                           np.asarray(L0_ffn_w_up_gate, np.float32),
                           np.asarray(L0_ffn_w_down, np.float32),
                           np.asarray(L0_ffn_shared_w_up_gate, np.float32),
                           np.asarray(L0_ffn_shared_w_down, np.float32))
               for core in range(8)]
    res = run_bass_kernel_spmd(nc, in_maps, core_ids=list(range(8)))
    outp = np.empty((2, c.S, c.D), np.float32)
    for core in range(8):
        b, qi = core // 4, core % 4
        o = np.asarray(res.results[core]["out"])[:c.T].reshape(c.NM, 128, c.D)
        for mt in range(c.NM):
            g = 4 * mt + qi
            outp[b, g * 128:(g + 1) * 128, :] = o[mt]
    return outp
